# version 07
# baseline (speedup 1.0000x reference)
import numpy as np
import ml_dtypes
import concourse.bass as bass
import concourse.mybir as mybir
from concourse.bass_utils import run_bass_kernel_spmd

F32 = mybir.dt.float32
BF16 = mybir.dt.bfloat16
I32 = mybir.dt.int32
I16 = mybir.dt.int16
ALU = mybir.AluOpType
AF = mybir.ActivationFunctionType
AX = mybir.AxisListType

HB = 2048
S = 4096
TL = 2048
NH = 16
KVH = 4
D = 128
MLP = 16384
EPS_RMS = 1e-6
EPS_LN = 1e-5
QK_SCALE = float(D) ** -0.5
G_C1 = 0.044715 * 0.7978845608028654
G_C2 = 0.7978845608028654
KT512 = [8, 6, 4, 2]
KT128 = [32, 24, 16, 8]
SLOT_CHUNKS = {0: [7, 4, 3, 0], 1: [6, 5, 2, 1]}
NEGV = -60000.0
DEBUG = False

DCH = ([("g", 0, j) for j in range(8)] + [("u", 0, j) for j in range(8)]
       + [("d", t, 0) for t in range(8)])
NDC = len(DCH)
A3L = [(sl, h) for sl in range(4) for h in range(16)]


def _a3_sched():
    n1 = n2 = 0
    out = []
    for i, (sl, h) in enumerate(A3L):
        u = {"i": i, "sl": sl, "h": h, "n1_0": n1, "n2_0": n2}
        n1 += 4 * KT512[sl]
        n2 += KT128[sl]
        u["n1_1"], u["n2_1"] = n1, n2
        out.append(u)
    return out


A3S = _a3_sched()

A4_BASE = 40


def _a4_blk_cnt(sl, pr, ks):
    return sl * A4_BASE + 4 + pr * 18 + ks + 1


def _a4_y1t_cnt(sl, qt):
    return sl * A4_BASE + 4 + (qt // 2) * 18 + 16 + (qt % 2) + 1


def build_nc():
    nc = bass.Bass("TRN2", target_bir_lowering=False,
                   detect_race_conditions=False)
    xb = nc.dram_tensor("xb", [S, HB], F32, kind="ExternalInput").ap()
    xq = nc.dram_tensor("xq", [TL, HB], F32, kind="ExternalInput").ap()
    wq = nc.dram_tensor("wq", [HB, NH * D], F32, kind="ExternalInput").ap()
    wkv = nc.dram_tensor("wkv", [2 * HB, KVH * D], F32, kind="ExternalInput").ap()
    wo = nc.dram_tensor("wo", [NH * D, HB], F32, kind="ExternalInput").ap()
    gqv = nc.dram_tensor("gqv", [D, 1], F32, kind="ExternalInput").ap()
    gkv = nc.dram_tensor("gkv", [D, 1], F32, kind="ExternalInput").ap()
    gsh = nc.dram_tensor("gsh", [128, MLP], I32, kind="ExternalInput").ap()
    ush = nc.dram_tensor("ush", [128, MLP], I32, kind="ExternalInput").ap()
    dsh = nc.dram_tensor("dsh", [1024, HB], I32, kind="ExternalInput").ap()
    gsc = nc.dram_tensor("gsc", [8, MLP], F32, kind="ExternalInput").ap()
    usc = nc.dram_tensor("usc", [8, MLP], F32, kind="ExternalInput").ap()
    dsc = nc.dram_tensor("dsc", [64, HB], F32, kind="ExternalInput").ap()
    e8 = nc.dram_tensor("e8", [8, 128], F32, kind="ExternalInput").ap()
    identin = nc.dram_tensor("identin", [128, 128], BF16, kind="ExternalInput").ap()
    m1in = nc.dram_tensor("m1in", [4 * 128, 8 * 512], BF16, kind="ExternalInput").ap()
    m2in = nc.dram_tensor("m2in", [4 * 128, 8 * 512], BF16, kind="ExternalInput").ap()
    y2out = nc.dram_tensor("y2out", [TL, HB], F32, kind="ExternalOutput").ap()
    if DEBUG:
        y1d_out = nc.dram_tensor("y1d_out", [TL, HB], F32, kind="ExternalOutput").ap()
        qt_out = nc.dram_tensor("qt_out", [NH * D, TL], BF16, kind="ExternalOutput").ap()
        gg_out = nc.dram_tensor("gg_out", [HB, MLP], BF16, kind="ExternalOutput").ap()
        gd_out = nc.dram_tensor("gd_out", [MLP, HB], BF16, kind="ExternalOutput").ap()
        pt_out = nc.dram_tensor("pt_out", [TL, HB], F32, kind="ExternalOutput").ap()
        qf_dbg = nc.dram_tensor("qf_dbg", [128, HB], F32, kind="ExternalOutput").ap()
        qn2_dbg = nc.dram_tensor("qn2_dbg", [128, HB], BF16, kind="ExternalOutput").ap()
        ofn_dbg = nc.dram_tensor("ofn_dbg", [128, 640], F32, kind="ExternalOutput").ap()
        os_dbg = nc.dram_tensor("os_dbg", [128, 512], BF16, kind="ExternalOutput").ap()
        mcb_dbg = nc.dram_tensor("mcb_dbg", [128, 8], F32, kind="ExternalOutput").ap()
        negm_dbg = nc.dram_tensor("negm_dbg", [1, 512], BF16, kind="ExternalOutput").ap()
        a4c_dbg = nc.dram_tensor("a4c_dbg", [128, 8], F32, kind="ExternalOutput").ap()
        fb2_dbg = nc.dram_tensor("fb2_dbg", [128, HB], F32, kind="ExternalOutput").ap()
        it_dbg = nc.dram_tensor("it_dbg", [128, 16384], BF16, kind="ExternalOutput").ap()
        y1t_dbg = nc.dram_tensor("y1t_dbg", [128, 32768], BF16, kind="ExternalOutput").ap()
        y1td_dbg = nc.dram_tensor("y1td_dbg", [2048, 2048], BF16, kind="ExternalOutput").ap()
        qe_dbg = nc.dram_tensor("qe_dbg", [128, HB], BF16, kind="ExternalOutput").ap()
        col_dbg = nc.dram_tensor("col_dbg", [128, 48], F32, kind="ExternalOutput").ap()
        qn_dbg = nc.dram_tensor("qn_dbg", [128, HB], BF16, kind="ExternalOutput").ap()

    wo_bf = nc.dram_tensor("wo_bf", [NH * D, HB], BF16, kind="Internal").ap()
    qt_dram = nc.dram_tensor("qt_dram", [NH * D, TL], BF16, kind="Internal").ap()
    y1_dram = nc.dram_tensor("y1_dram", [TL, HB], F32, kind="Internal").ap()
    y1t_dram = nc.dram_tensor("y1t_dram", [HB, TL], BF16, kind="Internal").ap()
    part = nc.dram_tensor("part", [TL, HB], F32, kind="Internal").ap()
    pg = nc.dram_tensor("pg", [256, MLP], BF16, kind="Internal").ap()
    pu = nc.dram_tensor("pu", [256, MLP], BF16, kind="Internal").ap()
    pd = nc.dram_tensor("pd", [2048, HB], BF16, kind="Internal").ap()
    gg = nc.dram_tensor("gg", [HB, MLP], BF16, kind="Internal").ap()
    gu = nc.dram_tensor("gu", [HB, MLP], BF16, kind="Internal").ap()
    gd = nc.dram_tensor("gd", [MLP, HB], BF16, kind="Internal").ap()

    import contextlib
    stack = contextlib.ExitStack()

    def sbt(name, shape, dt):
        return stack.enter_context(nc.sbuf_tensor(name, shape, dt))

    P64A = sbt("P64A", [128, 32768], BF16)
    P32A = sbt("P32A", [128, 16384], BF16)
    P32B = sbt("P32B", [128, 16384], BF16)
    MISC = sbt("MISC", [128, 8192], BF16)
    FB1 = sbt("FB1", [128, 2048], F32)
    FB2 = sbt("FB2", [128, 2048], F32)
    FB3 = sbt("FB3", [128, 2048], F32)
    FB4 = sbt("FB4", [128, 2048], F32)
    BB1 = sbt("BB1", [128, 4096], BF16)
    BB2 = sbt("BB2", [128, 4096], BF16)
    BB3 = sbt("BB3", [128, 4096], BF16)
    E8S = sbt("E8S", [8, 128], F32)
    IDB = sbt("IDB", [128, 128], BF16)
    ONES1 = sbt("ONES1", [1, 128], BF16)
    ONESC = sbt("ONESC", [128, 1], BF16)
    GQC = sbt("GQC", [128, 1], F32)
    GKC = sbt("GKC", [128, 1], F32)
    GQS = sbt("GQS", [128, 1], F32)
    COLA = sbt("COLA", [128, 16], F32)
    COLB = sbt("COLB", [128, 16], F32)
    COLC = sbt("COLC", [128, 16], F32)
    MCOL = sbt("MCOL", [128, 4], F32)
    MCB = sbt("MCB", [128, 4], BF16)
    DUM = sbt("DUM", [128, 64], F32)
    DUMW = sbt("DUMW", [128, 256], F32)
    NEGM = sbt("NEGM", [1, 512], BF16)
    PTB = sbt("PTB", [128, 1024], BF16)
    QTS = sbt("QTS", [128, 1024], BF16)
    RC = sbt("RC", [128, 4], F32)
    EPSC = sbt("EPSC", [128, 2], F32)

    CSD = P64A[:, 0:8192].bitcast(I32)
    DI1 = P64A[:, 8192:12288].bitcast(I32)
    DI2 = P64A[:, 12288:16384].bitcast(I32)
    DI3 = P64A[:, 16384:20480].bitcast(I32)
    DI4 = P64A[:, 20480:24576].bitcast(I32)
    DBITS = P64A[:, 24576:28672].bitcast(I16)
    DT1 = P64A[:, 28672:30720].bitcast(I16)
    DNSX = P32A[:, 0:4096]
    DLO = P32A[:, 4096:6144]
    DHI = P32A[:, 6144:8192]
    DC1 = P32A[:, 8192:10240]
    DC2 = P32A[:, 10240:12288]
    DT2 = P32A[:, 12288:14336]
    DT3 = P32A[:, 14336:16384]
    WQS = P64A
    WKS = P64A[:, 0:8192]
    WVS = P64A[:, 8192:16384]
    KTS = P32A
    VS = P32B
    OS = P64A[:, 16384:24576]
    OTS = P64A[:, 24576:32768]
    M1S = MISC[:, 0:4096]
    M2S = MISC[:, 4096:8192]
    SCL0 = FB1[0:8, 0:2048]
    SCL1 = FB2[0:8, 0:2048]
    PS1F = P64A[:, 0:2048].bitcast(F32)
    NEGF = P64A[0:1, 2048:3072].bitcast(F32)
    OFN = P64A[:, 3072:4352].bitcast(F32)
    PS2F = P64A[:, 4352:6400].bitcast(F32)
    AOF = P64A[:, 8192:16384].bitcast(F32)
    Y1TS = P64A
    INTERT = P32A
    GUBF = MISC
    WD8 = P32B

    PSA = stack.enter_context(nc.psum_tensor("PSA", [128, 2048], F32))
    PSB = stack.enter_context(nc.psum_tensor("PSB", [128, 2048], F32))
    PSAb = PSA[:, :].bitcast(BF16)
    PSBb = PSB[:, :].bitcast(BF16)

    sems = {}
    for n in ("c_ld", "d_l0", "d_l1", "d_sc", "d_pe", "d_ac", "d_dv", "d_cc",
              "w_g0", "w_g1", "w_s0", "w_s1",
              "q_gw", "q_g0", "q_g1", "q_pe", "q_pe2", "q_ac", "q_dv", "q_st",
              "k_gw", "k_g0", "k_g1", "k_pe", "k_pe2", "k_ac", "k_dv",
              "a_lm", "a_l0", "a_l1", "a_p1", "a_a1", "a_d1", "a_dm",
              "a_pr", "a_an", "a_dn", "a_ao",
              "a_p2", "a_e2", "a_d2", "a_ae", "a_pe", "a_df",
              "o_l0", "o_l1", "o_lx", "o_pe", "o_oa", "o_ac", "o_dv",
              "o_sy", "o_sz",
              "f_ly", "f_lw", "f_lg0", "f_lg1", "f_pe", "f_pe2",
              "f_ac", "f_ac2", "f_tg", "f_th", "f_dv", "f_g0", "f_g1",
              "l_ld", "l_ac", "l_dv", "l_st"):
        sems[n] = stack.enter_context(nc.semaphore(n))
    sm = type("SM", (), sems)
    DL = [sems["d_l0"], sems["d_l1"]]
    WG = [sems["w_g0"], sems["w_g1"]]
    WS = [sems["w_s0"], sems["w_s1"]]
    QG = [sems["q_g0"], sems["q_g1"]]
    KG = [sems["k_g0"], sems["k_g1"]]
    AL = [sems["a_l0"], sems["a_l1"]]
    OL = [sems["o_l0"], sems["o_l1"]]
    FLG = [sems["f_lg0"], sems["f_lg1"]]
    FG = [sems["f_g0"], sems["f_g1"]]

    with nc.Block() as block:

        def emit_a4_tensor(tensor, sl):
            tensor.wait_ge(sm.a_df, 16 * (sl + 1))
            tensor.wait_ge(sm.a_ae, A3S[16 * (sl + 1) - 1]["n2_1"])
            tensor.wait_ge(sm.a_a1, A3S[16 * (sl + 1) - 1]["n1_1"])
            for r in range(4):
                if r >= 1:
                    tensor.wait_ge(sm.o_oa, sl * 4 + r)
                for jj in range(16):
                    f = r * 16 + jj
                    ks, qt = f // 4, f % 4
                    op = tensor.transpose(PSAb[:, jj * 128:(jj + 1) * 128],
                                          OS[:, qt * 2048 + ks * 128:
                                             qt * 2048 + (ks + 1) * 128],
                                          IDB[:, :])
                op.then_inc(sm.o_pe, 1)
            tensor.wait_ge(sm.o_oa, sl * 4 + 4)
            for pr in range(2):
                if pr == 1:
                    tensor.wait_ge(sm.o_dv, 3 * (sl * 4 + 2))
                    tensor.wait_ge(sm.o_ac, 3 * (sl * 4 + 1) + 3)
                for ks in range(16):
                    it = sl * 32 + pr * 16 + ks
                    tensor.wait_ge(OL[it % 2], 16 * (it // 2 + 1))
                    wob = BB1[:, (it % 2) * 2048:(it % 2) * 2048 + 2048]
                    for qt2 in range(2):
                        qt = pr * 2 + qt2
                        aop = PSA if qt2 == 0 else PSB
                        for hc in range(4):
                            op = tensor.matmul(aop[:, hc * 512:(hc + 1) * 512],
                                               OTS[:, (ks * 4 + qt) * 128:
                                                   (ks * 4 + qt + 1) * 128],
                                               wob[:, hc * 512:(hc + 1) * 512],
                                               start=(ks == 0), stop=(ks == 15))
                    op.then_inc(sm.o_pe, 1)
                for qt2 in range(2):
                    qt = pr * 2 + qt2
                    it2 = sl * 4 + qt
                    tensor.wait_ge(sm.o_dv, 3 * it2 + 3)
                    if it2 >= 2:
                        tensor.wait_ge(sm.o_ac, 3 * (it2 - 2) + 3)
                    y1b = BB3[:, (qt % 2) * 2048:(qt % 2) * 2048 + 2048]
                    dstb = PSAb if qt2 == 0 else PSBb
                    for hh in range(16):
                        op = tensor.transpose(dstb[:, hh * 128:(hh + 1) * 128],
                                              y1b[:, hh * 128:(hh + 1) * 128],
                                              IDB[:, :])
                    op.then_inc(sm.o_pe, 1)

        def emit_a4_scalar(scalar, sl):
            for r in range(4):
                scalar.wait_ge(sm.o_pe, sl * A4_BASE + r + 1)
                op = scalar.activation(OTS[:, r * 2048:(r + 1) * 2048],
                                       PSAb[:, 0:2048], AF.Identity)
                if r == 3:
                    op = scalar.activation(DUMW[:, 128:256], IDB[:, 0:128], AF.Identity)
                op.then_inc(sm.o_oa, 1)
            def y1t_store(j):
                scalar.dma_start(
                    out=bass.AP(y1t_dram.tensor, j * 128,
                                [[TL, 128], [128 * TL, 16], [1, 128]]),
                    in_=BB2[:, (j % 2) * 2048:(j % 2) * 2048 + 2048
                            ].rearrange("p (h j) -> p h j", h=16)
                ).then_inc(sm.o_sz, 16)

            for qt in range(4):
                it2 = sl * 4 + qt
                pr, qt2 = qt // 2, qt % 2
                scalar.wait_ge(sm.o_pe, _a4_blk_cnt(sl, pr, 15))
                if it2 >= 2:
                    scalar.wait_ge(sm.o_dv, 3 * (it2 - 2) + 1)
                aop = PSA if qt2 == 0 else PSB
                scalar.activation(AOF[:, (qt % 2) * 2048:(qt % 2) * 2048 + 2048],
                                  aop[:, :], AF.Identity)
                scalar.activation(DUMW[:, 128:256], IDB[:, 0:128], AF.Identity).then_inc(sm.o_ac, 1)
                if it2 >= 2:
                    scalar.wait_ge(sm.o_sz, 16 * (it2 - 1))
                scalar.wait_ge(sm.o_dv, 3 * it2 + 2)
                if DEBUG and it2 == 0:
                    scalar.dma_start(out=a4c_dbg[:, :], in_=COLA[:, 0:8]
                                     ).then_inc(sm.c_ld, 16)
                    scalar.dma_start(out=fb2_dbg[:, :], in_=FB2[:, :]
                                     ).then_inc(sm.c_ld, 16)
                    scalar.wait_ge(sm.c_ld, 64 + 16 * 16)
                scalar.activation(COLC[:, 0:1], COLA[:, 5:6], AF.Sqrt,
                                  bias=EPSC[:, 1:2], scale=1.0)
                scalar.activation(DUM[:, 0:1], COLC[:, 0:1], AF.Sqrt
                                  ).then_inc(sm.o_ac, 1)
                scalar.wait_ge(sm.o_dv, 3 * it2 + 3)
                scalar.dma_start(out=y1_dram[it2 * 128:(it2 + 1) * 128, :],
                                 in_=FB4[:, :]).then_inc(sm.o_sy, 16)
                scalar.wait_ge(sm.o_pe, _a4_y1t_cnt(sl, qt))
                pb = PSAb if qt % 2 == 0 else PSBb
                scalar.activation(BB2[:, (qt % 2) * 2048:(qt % 2) * 2048 + 2048],
                                  pb[:, 0:2048], AF.Identity).then_inc(sm.o_ac, 1)
                if it2 >= 1:
                    y1t_store(it2 - 1)
            if sl == 3:
                scalar.wait_ge(sm.o_sy, 16 * 16)
                scalar.wait_ge(sm.o_sz, 16 * 15)
                y1t_store(15)

        def emit_a4_vector(vector, sl):
            for pr in range(2):
                for qt2 in range(2):
                    qt = pr * 2 + qt2
                    it2 = sl * 4 + qt
                    vector.wait_ge(sm.o_ac, 3 * it2 + 1)
                    vector.wait_ge(sm.o_lx, 16 * (it2 + 1))
                    vector.scalar_tensor_tensor(FB2[:, :],
                                                AOF[:, (qt % 2) * 2048:
                                                    (qt % 2) * 2048 + 2048],
                                                0.0, FB1[:, :],
                                                ALU.add, ALU.add,
                                                accum_out=COLA[:, 0:1]
                                                ).then_inc(sm.o_dv, 1)
                    vector.scalar_tensor_tensor(FB3[:, :], FB2[:, :], 0.0,
                                                FB2[:, :], ALU.add, ALU.mult,
                                                accum_out=COLA[:, 1:2])
                    vector.tensor_scalar(COLA[:, 2:3], COLA[:, 0:1], 1.0 / HB,
                                         None, ALU.mult)
                    vector.tensor_copy(out=DUM[:, 32:48], in_=IDB[:, 0:16])
                    vector.tensor_copy(out=DUM[:, 48:64], in_=IDB[:, 16:32])
                    vector.tensor_tensor(COLA[:, 3:4], COLA[:, 2:3],
                                         COLA[:, 2:3], ALU.mult)
                    vector.tensor_scalar(COLA[:, 4:5], COLA[:, 1:2], 1.0 / HB,
                                         None, ALU.mult)
                    vector.tensor_copy(out=DUM[:, 32:48], in_=IDB[:, 0:16])
                    vector.tensor_copy(out=DUM[:, 48:64], in_=IDB[:, 16:32])
                    vector.tensor_tensor(COLA[:, 5:6], COLA[:, 4:5],
                                         COLA[:, 3:4], ALU.subtract)
                    vector.tensor_copy(out=DUM[:, 16:17], in_=COLA[:, 5:6]
                                       ).then_inc(sm.o_dv, 1)
                    vector.wait_ge(sm.o_ac, 3 * it2 + 2)
                    vector.reciprocal(COLB[:, 0:1], COLC[:, 0:1])
                    vector.tensor_copy(out=DUM[:, 32:33], in_=COLB[:, 0:1])
                    vector.tensor_copy(out=DUM[:, 48:49], in_=DUM[:, 32:33])
                    if it2 >= 1:
                        vector.wait_ge(sm.o_sy, 16 * it2)
                    vector.tensor_scalar(FB4[:, :], FB2[:, :], COLA[:, 2:3],
                                         COLB[:, 0:1], ALU.subtract, ALU.mult)
                    if it2 >= 2:
                        vector.wait_ge(sm.o_pe, _a4_y1t_cnt(sl, qt - 2)
                                       if qt >= 2 else
                                       _a4_y1t_cnt(sl - 1, qt + 2))
                    vector.tensor_copy(out=BB3[:, (qt % 2) * 2048:
                                               (qt % 2) * 2048 + 2048],
                                       in_=FB4[:, :])
                    vector.tensor_copy(out=DUMW[:, 0:128], in_=IDB[:, 0:128]).then_inc(sm.o_dv, 1)

        @block.sync
        def _(sync):
            sync.dma_start(out=E8S[:, :], in_=e8[:, :]).then_inc(sm.c_ld, 16)
            sync.dma_start(out=GQC[:, :], in_=gqv[:, :]).then_inc(sm.c_ld, 16)
            sync.dma_start(out=GKC[:, :], in_=gkv[:, :]).then_inc(sm.c_ld, 16)
            sync.dma_start(out=IDB[:, :], in_=identin[:, :]).then_inc(sm.c_ld, 16)
            for j, (m, wt, ch) in enumerate(DCH):
                if j >= 2:
                    sync.wait_ge(sm.d_dv, j - 1)
                    sync.wait_ge(sm.d_pe, j - 1)
                buf = CSD[:, (j % 2) * 2048:(j % 2) * 2048 + 2048]
                sc = (SCL0 if j % 2 == 0 else SCL1)[:, :]
                dl = DL[j % 2]
                if m == "g":
                    sync.dma_start(out=buf, in_=gsh[:, ch * 2048:(ch + 1) * 2048]
                                   ).then_inc(dl, 16)
                    sync.dma_start(out=sc, in_=gsc[:, ch * 2048:(ch + 1) * 2048]
                                   ).then_inc(dl, 16)
                elif m == "u":
                    sync.dma_start(out=buf, in_=ush[:, ch * 2048:(ch + 1) * 2048]
                                   ).then_inc(dl, 16)
                    sync.dma_start(out=sc, in_=usc[:, ch * 2048:(ch + 1) * 2048]
                                   ).then_inc(dl, 16)
                else:
                    sync.dma_start(out=buf, in_=dsh[wt * 128:(wt + 1) * 128, :]
                                   ).then_inc(dl, 16)
                    sync.dma_start(out=sc, in_=dsc[wt * 8:(wt + 1) * 8, :]
                                   ).then_inc(dl, 16)
            def emit_xq(sl, qt):
                it2 = sl * 4 + qt
                if it2 >= 1:
                    sync.wait_ge(sm.o_dv, 3 * it2 - 2)
                sync.dma_start(
                    out=FB1[:, :],
                    in_=xq[(sl * 4 + qt) * 128:(sl * 4 + qt + 1) * 128, :]
                ).then_inc(sm.o_lx, 16)

            def emit_a4_sync(sl):
                emit_xq(sl, 0)
                for pr in range(2):
                    for ks in range(16):
                        it = sl * 32 + pr * 16 + ks
                        if it == 0:
                            sync.wait_ge(WS[0], 16 * 8)
                            sync.wait_ge(WS[1], 16 * 8)
                        if it >= 2:
                            it_prev = it - 2
                            slp, rem = divmod(it_prev, 32)
                            prp, ksp = divmod(rem, 16)
                            sync.wait_ge(sm.o_pe, _a4_blk_cnt(slp, prp, ksp))
                        sync.wait_ge(sm.a_df, 16 * (sl + 1))
                        sync.dma_start(
                            out=BB1[:, (it % 2) * 2048:(it % 2) * 2048 + 2048],
                            in_=wo_bf[ks * 128:(ks + 1) * 128, :]
                        ).then_inc(OL[it % 2], 16)
                    emit_xq(sl, 1 + pr)
                emit_xq(sl, 3)

            for u in A3S:
                i, sl, h = u["i"], u["sl"], u["h"]
                if h == 0:
                    if sl > 0:
                        emit_a4_sync(sl - 1)
                        sync.wait_ge(sm.a_pe, i)
                    sync.dma_start(out=M1S[:, :],
                                   in_=m1in[sl * 128:(sl + 1) * 128, :]
                                   ).then_inc(sm.a_lm, 16)
                    sync.dma_start(out=M2S[:, :],
                                   in_=m2in[sl * 128:(sl + 1) * 128, :]
                                   ).then_inc(sm.a_lm, 16)
                if i == 0:
                    sync.wait_ge(sm.q_st, 16 * 16)
                if i >= 2:
                    sync.wait_ge(sm.a_pe, i - 1)
                sync.dma_start(
                    out=QTS[:, (i % 2) * 512:(i % 2) * 512 + 512],
                    in_=qt_dram[h * 128:(h + 1) * 128, sl * 512:(sl + 1) * 512]
                ).then_inc(AL[i % 2], 16)
            emit_a4_sync(3)
            sync.wait_ge(sm.o_sy, 16 * 16)
            sync.wait_ge(sm.o_sz, 16 * 16)
            for hc in range(16):
                sync.dma_start(out=Y1TS[:, hc * 2048:(hc + 1) * 2048],
                               in_=y1t_dram[hc * 128:(hc + 1) * 128, :]
                               ).then_inc(sm.f_ly, 16)
            for s in range(16):
                for sub in range(8):
                    if sub == 0:
                        if s == 0:
                            sync.wait_ge(sm.d_cc, 3)
                        else:
                            sync.wait_ge(sm.f_pe2, 16 * s)
                    sync.dma_start(
                        out=WD8[:, sub * 2048:(sub + 1) * 2048],
                        in_=gd[(s * 8 + sub) * 128:(s * 8 + sub + 1) * 128, :]
                    ).then_inc(sm.f_lw, 16)
                for sub in range(8):
                    for mi, gap in enumerate((gg, gu)):
                        gidx = s * 8 + sub
                        if gidx >= 2:
                            sync.wait_ge(sm.f_pe, 4 * (gidx - 1))
                        slot = (sub % 2) * 2 + mi
                        sync.dma_start(
                            out=GUBF[:, slot * 2048:(slot + 1) * 2048],
                            in_=bass.AP(gap.tensor, gidx * 128,
                                        [[MLP, 128], [128 * MLP, 16], [1, 128]])
                        ).then_inc(FLG[sub % 2], 16)
            sync.wait_ge(FG[0], 16 * 128)
            sync.wait_ge(FG[1], 16 * 128)
            for tg in range(16):
                if tg >= 1:
                    sync.wait_ge(sm.l_dv, 2 * tg)
                sync.dma_start(out=FB1[:, :],
                               in_=part[tg * 128:(tg + 1) * 128, :]
                               ).then_inc(sm.l_ld, 16)
                sync.dma_start(out=FB2[:, :],
                               in_=y1_dram[tg * 128:(tg + 1) * 128, :]
                               ).then_inc(sm.l_ld, 16)
            if DEBUG:
                sync.dma_start(out=y1d_out[:, :], in_=y1_dram[:, :]
                               ).then_inc(sm.c_ld, 16)
                sync.dma_start(out=qt_out[:, :], in_=qt_dram[:, :]
                               ).then_inc(sm.c_ld, 16)
                sync.dma_start(out=gg_out[:, :], in_=gg[:, :]
                               ).then_inc(sm.c_ld, 16)
                sync.dma_start(out=gd_out[:, :], in_=gd[:, :]
                               ).then_inc(sm.c_ld, 16)
                sync.dma_start(out=pt_out[:, :], in_=part[:, :]
                               ).then_inc(sm.c_ld, 16)
                sync.wait_ge(sm.c_ld, 64 + 10 * 16)

        @block.gpsimd
        def _(gpsimd):
            gpsimd.wait_ge(sm.d_dv, NDC)
            for i in range(16):
                gpsimd.dma_start(out=WQS[:, i * 2048:(i + 1) * 2048],
                                 in_=wq[i * 128:(i + 1) * 128, :]
                                 ).then_inc(sm.q_gw, 16)
            for qt in range(16):
                if qt >= 2:
                    gpsimd.wait_ge(sm.q_pe2, qt - 1)
                gpsimd.dma_start(out=BB1[:, (qt % 2) * 2048:(qt % 2) * 2048 + 2048],
                                 in_=xq[qt * 128:(qt + 1) * 128, :]
                                 ).then_inc(QG[qt % 2], 16)
            for i in range(16):
                if i >= 2:
                    gpsimd.wait_ge(WS[i % 2], 16 * ((i - 2) // 2 + 1))
                gpsimd.dma_start(out=P32B[:, (i % 2) * 2048:(i % 2) * 2048 + 2048],
                                 in_=wo[i * 128:(i + 1) * 128, :]
                                 ).then_inc(WG[i % 2], 16)
            gpsimd.wait_ge(sm.q_pe, 32)
            gpsimd.dma_start(
                out=WKS[:, :].rearrange("p (t d) -> p t d", t=16),
                in_=bass.AP(wkv.tensor, 0, [[512, 128], [128 * 512, 16], [1, 512]])
            ).then_inc(sm.k_gw, 16)
            gpsimd.dma_start(
                out=WVS[:, :].rearrange("p (t d) -> p t d", t=16),
                in_=bass.AP(wkv.tensor, HB * 512,
                            [[512, 128], [128 * 512, 16], [1, 512]])
            ).then_inc(sm.k_gw, 16)
            for i in range(32):
                if i >= 2:
                    gpsimd.wait_ge(sm.k_pe2, i - 1)
                gpsimd.dma_start(out=BB1[:, (i % 2) * 2048:(i % 2) * 2048 + 2048],
                                 in_=xb[i * 128:(i + 1) * 128, :]
                                 ).then_inc(KG[i % 2], 16)
            for gi, (piece, gath) in enumerate(((pg, gg), (pu, gu), (pd, gd))):
                gpsimd.wait_ge(sm.d_sc, 16 * 16 * (gi + 1))
                gpsimd.collective_compute(
                    "AllGather", ALU.bypass,
                    replica_groups=[[0, 1, 2, 3, 4, 5, 6, 7]],
                    ins=[piece.opt()], outs=[gath.opt()],
                ).then_inc(sm.d_cc)
            for s in range(16):
                for tg in range(16):
                    it = s * 16 + tg
                    gpsimd.wait_ge(sm.f_ac2, it + 1)
                    if s >= 1:
                        prev = (s - 1) * 16 + tg
                        gpsimd.wait_ge(FG[tg % 2], 16 * (prev // 2 + 1))
                    stg = FB4 if tg % 2 == 0 else FB1
                    gpsimd.dma_start(
                        out=part[tg * 128:(tg + 1) * 128, :],
                        in_=stg[:, :],
                        accum_op=(ALU.bypass if s == 0 else ALU.add),
                    ).then_inc(FG[tg % 2], 16)
            gpsimd.wait_ge(FG[0], 16 * 128)
            gpsimd.wait_ge(FG[1], 16 * 128)

        @block.tensor
        def _(tensor):
            tensor.wait_ge(sm.c_ld, 64)
            for j in range(NDC):
                tensor.wait_ge(DL[j % 2], 32 * (j // 2 + 1))
                if j >= 1:
                    tensor.wait_ge(sm.d_ac, j)
                scl = SCL0 if j % 2 == 0 else SCL1
                for q4 in range(4):
                    op = tensor.matmul(PSA[:, q4 * 512:(q4 + 1) * 512],
                                       E8S[:, :],
                                       scl[0:8, q4 * 512:(q4 + 1) * 512],
                                       start=True, stop=True)
                op.then_inc(sm.d_pe, 1)
            tensor.wait_ge(sm.c_ld, 64)
            tensor.wait_ge(sm.q_gw, 16 * 16)
            for qt in range(16):
                tensor.wait_ge(QG[qt % 2], 16 * (qt // 2 + 1))
                if qt >= 1:
                    tensor.wait_ge(sm.q_ac, 4 * (qt - 1) + 2)
                xin = BB1[:, (qt % 2) * 2048:(qt % 2) * 2048 + 2048]
                for kc in range(16):
                    op = tensor.transpose(PSBb[:, kc * 128:(kc + 1) * 128],
                                          xin[:, kc * 128:(kc + 1) * 128],
                                          IDB[:, :])
                op.then_inc(sm.q_pe2, 1)
                tensor.wait_ge(sm.q_ac, 4 * qt + 1)
                xt = BB2[:, (qt % 2) * 2048:(qt % 2) * 2048 + 2048]
                for dqc in range(4):
                    for kc in range(16):
                        op = tensor.matmul(PSA[:, dqc * 512:(dqc + 1) * 512],
                                           xt[:, kc * 128:(kc + 1) * 128],
                                           WQS[:, kc * 2048 + dqc * 512:
                                               kc * 2048 + (dqc + 1) * 512],
                                           start=(kc == 0), stop=(kc == 15))
                op.then_inc(sm.q_pe, 1)
                tensor.wait_ge(sm.q_dv, 4 * qt + 3)
                for hh in range(16):
                    op = tensor.transpose(
                        PSBb[:, 2048 + hh * 128:2048 + (hh + 1) * 128],
                        xt[:, hh * 128:(hh + 1) * 128], IDB[:, :])
                op.then_inc(sm.q_pe, 1)
            tensor.wait_ge(sm.k_gw, 32)
            for tt in range(32):
                tensor.wait_ge(KG[tt % 2], 16 * (tt // 2 + 1))
                if tt == 0:
                    tensor.wait_ge(sm.q_ac, 64)
                if tt >= 1:
                    tensor.wait_ge(sm.k_ac, 5 * (tt - 1) + 1)
                xin = BB1[:, (tt % 2) * 2048:(tt % 2) * 2048 + 2048]
                for kc in range(16):
                    op = tensor.transpose(PSBb[:, kc * 128:(kc + 1) * 128],
                                          xin[:, kc * 128:(kc + 1) * 128],
                                          IDB[:, :])
                op.then_inc(sm.k_pe2, 1)
                tensor.wait_ge(sm.k_ac, 5 * tt + 1)
                xt = BB2[:, (tt % 2) * 2048:(tt % 2) * 2048 + 2048]
                for kc in range(16):
                    tensor.matmul(PSA[:, 0:512], xt[:, kc * 128:(kc + 1) * 128],
                                  WKS[:, kc * 512:(kc + 1) * 512],
                                  start=(kc == 0), stop=(kc == 15))
                for kc in range(16):
                    op = tensor.matmul(PSA[:, 512:1024],
                                       xt[:, kc * 128:(kc + 1) * 128],
                                       WVS[:, kc * 512:(kc + 1) * 512],
                                       start=(kc == 0), stop=(kc == 15))
                op.then_inc(sm.k_pe, 1)
                tensor.wait_ge(sm.k_dv, 4 * tt + 3)
                for sgi in range(4):
                    op = tensor.transpose(
                        PSBb[:, 2048 + sgi * 128:2048 + (sgi + 1) * 128],
                        xt[:, sgi * 128:(sgi + 1) * 128], IDB[:, :])
                op.then_inc(sm.k_pe, 1)
            tensor.wait_ge(sm.k_dv, 4 * 32)
            tensor.wait_ge(sm.k_ac, 5 * 32)
            for u in A3S:
                i, sl, h = u["i"], u["sl"], u["h"]
                kvh = h // 4
                qtl = QTS[:, (i % 2) * 512:(i % 2) * 512 + 512]
                if h == 0 and sl > 0:
                    emit_a4_tensor(tensor, sl - 1)
                    tensor.wait_ge(sm.o_ac, 12 * sl)
                tensor.wait_ge(sm.a_lm, 32 * (sl + 1))
                tensor.wait_ge(AL[i % 2], 16 * (i // 2 + 1))
                n1 = u["n1_0"]
                for qt in range(4):
                    for k5 in range(KT512[sl]):
                        if n1 >= 2:
                            tensor.wait_ge(sm.a_a1, n1 - 1)
                        tensor.matmul(PSA[:, (n1 % 2) * 512:(n1 % 2) * 512 + 512],
                                      qtl[:, qt * 128:(qt + 1) * 128],
                                      KTS[:, kvh * 4096 + k5 * 512:
                                          kvh * 4096 + (k5 + 1) * 512],
                                      start=True, stop=True
                                      ).then_inc(sm.a_p1, 1)
                        n1 += 1
                tensor.wait_ge(sm.a_dm, i + 1)
                if i >= 1:
                    tensor.wait_ge(sm.a_df, i)
                for qt in range(4):
                    op = tensor.matmul(PSB[0:1, 1024 + qt * 128:
                                           1024 + (qt + 1) * 128],
                                       MCB[:, qt:qt + 1], IDB[:, :],
                                       start=(qt == 0), stop=(qt == 3))
                op.then_inc(sm.a_pr, 1)
                tensor.wait_ge(sm.a_dn, i + 1)
                n2 = u["n2_0"]
                for kt in range(KT128[sl]):
                    if n2 >= 2:
                        tensor.wait_ge(sm.a_ae, n2 - 1)
                    bank = PSA[:, 1024 + (n2 % 2) * 512:
                               1024 + (n2 % 2) * 512 + 512]
                    tensor.matmul(bank,
                                  KTS[:, kvh * 4096 + kt * 128:
                                      kvh * 4096 + (kt + 1) * 128],
                                  qtl[:, :], start=True, stop=False)
                    tensor.matmul(bank, ONES1[:, :], NEGM[0:1, :],
                                  start=False, stop=True
                                  ).then_inc(sm.a_p2, 1)
                    tensor.wait_ge(sm.a_ae, n2 + 1)
                    pt = PTB[:, (n2 % 2) * 512:(n2 % 2) * 512 + 512]
                    last = (kt == KT128[sl] - 1)
                    for qt in range(4):
                        oacc = PSB[:, qt * 512:qt * 512 + 512]
                        tensor.matmul(oacc[:, 0:128],
                                      pt[:, qt * 128:(qt + 1) * 128],
                                      VS[:, kt * 512 + kvh * 128:
                                         kt * 512 + (kvh + 1) * 128],
                                      start=(kt == 0), stop=False)
                        op = tensor.matmul(oacc[:, 128:129],
                                           pt[:, qt * 128:(qt + 1) * 128],
                                           ONESC[:, :], start=False, stop=last)
                    n2 += 1
                op.then_inc(sm.a_pe, 1)
            emit_a4_tensor(tensor, 3)
            tensor.wait_ge(sm.o_ac, 48)
            tensor.wait_ge(sm.f_ly, 16 * 16)
            for s in range(16):
                if s >= 1:
                    tensor.wait_ge(sm.f_ac2, s * 16 - 1)
                for sub in range(8):
                    tensor.wait_ge(FLG[sub % 2], 32 * (s * 4 + sub // 2 + 1))
                    gslot = GUBF[:, ((sub % 2) * 2) * 2048:
                                 ((sub % 2) * 2) * 2048 + 2048]
                    uslot = GUBF[:, ((sub % 2) * 2 + 1) * 2048:
                                 ((sub % 2) * 2 + 1) * 2048 + 2048]
                    for tq in range(4):
                        k = (s * 8 + sub) * 4 + tq
                        if k >= 2:
                            tensor.wait_ge(sm.f_dv, k - 1)
                        gb = PSA[:, (tq % 2) * 512:(tq % 2) * 512 + 512]
                        ub = PSA[:, 1024 + (tq % 2) * 512:
                                 1024 + (tq % 2) * 512 + 512]
                        for hs in range(16):
                            tensor.matmul(gb, gslot[:, hs * 128:(hs + 1) * 128],
                                          Y1TS[:, hs * 2048 + tq * 512:
                                               hs * 2048 + (tq + 1) * 512],
                                          start=(hs == 0), stop=(hs == 15))
                        for hs in range(16):
                            op = tensor.matmul(ub,
                                               uslot[:, hs * 128:(hs + 1) * 128],
                                               Y1TS[:, hs * 2048 + tq * 512:
                                                    hs * 2048 + (tq + 1) * 512],
                                               start=(hs == 0), stop=(hs == 15))
                        op.then_inc(sm.f_pe, 1)
                tensor.wait_ge(sm.f_lw, 16 * 8 * (s + 1))
                for tg in range(16):
                    it = s * 16 + tg
                    accb = PSA if tg % 2 == 0 else PSB
                    tensor.wait_ge(sm.f_dv, 32 * s + 29 + tg // 4)
                    if tg == 0:
                        tensor.wait_ge(sm.f_ac, 64 * (s + 1))
                    if it >= 2:
                        tensor.wait_ge(sm.f_ac2, it - 1)
                    for sub in range(8):
                        for hc in range(4):
                            op = tensor.matmul(accb[:, hc * 512:(hc + 1) * 512],
                                               INTERT[:, sub * 2048 + tg * 128:
                                                      sub * 2048 + (tg + 1) * 128],
                                               WD8[:, sub * 2048 + hc * 512:
                                                   sub * 2048 + (hc + 1) * 512],
                                               start=(sub == 0), stop=(sub == 7))
                    op.then_inc(sm.f_pe2, 1)

        @block.scalar
        def _(scalar):
            def d_piece(jj):
                m, wt, ch = DCH[jj]
                scalar.wait_ge(sm.d_dv, jj + 1)
                if m in ("g", "u"):
                    dst = pg if m == "g" else pu
                    o_lo = bass.AP(dst.tensor, ch * 2048,
                                   [[2 * MLP, 128], [1, 2048]])
                    o_hi = bass.AP(dst.tensor, MLP + ch * 2048,
                                   [[2 * MLP, 128], [1, 2048]])
                else:
                    o_lo = bass.AP(pd.tensor, (2 * wt * 128) * HB,
                                   [[2 * HB, 128], [1, 2048]])
                    o_hi = bass.AP(pd.tensor, (2 * wt * 128 + 1) * HB,
                                   [[2 * HB, 128], [1, 2048]])
                scalar.dma_start(out=o_lo, in_=DLO[:, :]).then_inc(sm.d_sc, 16)
                scalar.dma_start(out=o_hi, in_=DHI[:, :]).then_inc(sm.d_sc, 16)

            for j in range(NDC):
                scalar.wait_ge(sm.d_pe, j + 1)
                scalar.activation(DNSX[:, (j % 2) * 2048:(j % 2) * 2048 + 2048],
                                  PSA[:, :], AF.Identity).then_inc(sm.d_ac, 1)
                if j >= 1:
                    d_piece(j - 1)
            d_piece(NDC - 1)
            for qt in range(16):
                scalar.wait_ge(sm.q_pe2, qt + 1)
                if qt >= 2:
                    scalar.wait_ge(sm.q_pe, 2 * (qt - 2) + 2)
                scalar.activation(BB2[:, (qt % 2) * 2048:(qt % 2) * 2048 + 2048],
                                  PSBb[:, 0:2048], AF.Identity)
                scalar.activation(DUMW[:, 128:256], IDB[:, 0:128], AF.Identity).then_inc(sm.q_ac, 1)
                scalar.wait_ge(sm.q_pe, 2 * qt + 1)
                if qt >= 1:
                    scalar.wait_ge(sm.q_dv, 4 * (qt - 1) + 4)
                scalar.activation(FB3[:, :], PSA[:, :], AF.Identity)
                scalar.activation(DUMW[:, 128:256], IDB[:, 0:128], AF.Identity).then_inc(sm.q_ac, 1)
                scalar.wait_ge(sm.q_dv, 4 * qt + 1)
                scalar.activation(COLC[:, 0:16], COLA[:, 0:16], AF.Sqrt,
                                  bias=EPSC[:, 0:1], scale=1.0 / D)
                scalar.activation(DUM[:, 0:16], COLC[:, 0:16], AF.Sqrt
                                  ).then_inc(sm.q_ac, 1)
                scalar.wait_ge(sm.q_pe, 2 * qt + 2)
                scalar.wait_ge(sm.q_dv, 4 * qt + 3)
                scalar.activation(FB3[:, :].bitcast(BF16)[:, 0:2048],
                                  PSBb[:, 2048:4096], AF.Identity)
                scalar.activation(DUMW[:, 128:256], IDB[:, 0:128], AF.Identity).then_inc(sm.q_ac, 1)
                scalar.wait_ge(sm.q_dv, 4 * qt + 4)
                if DEBUG and qt == 0:
                    scalar.dma_start(out=qf_dbg[:, :], in_=FB3[:, :]
                                     ).then_inc(sm.c_ld, 16)
                    scalar.dma_start(out=col_dbg[:, 0:16], in_=COLA[:, :]
                                     ).then_inc(sm.c_ld, 16)
                    scalar.dma_start(out=col_dbg[:, 16:32], in_=COLB[:, :]
                                     ).then_inc(sm.c_ld, 16)
                    scalar.dma_start(out=col_dbg[:, 32:48], in_=COLC[:, :]
                                     ).then_inc(sm.c_ld, 16)
                    scalar.dma_start(out=qn_dbg[:, :], in_=BB3[:, 0:2048]
                                     ).then_inc(sm.c_ld, 16)
                    scalar.dma_start(out=qn2_dbg[:, :], in_=BB2[:, 0:2048]
                                     ).then_inc(sm.c_ld, 16)
                    scalar.dma_start(out=qe_dbg[:, :],
                                     in_=FB3[:, :].bitcast(BF16)[:, 0:2048]
                                     ).then_inc(sm.c_ld, 16)
                    scalar.wait_ge(sm.c_ld, 64 + 7 * 16)
                scalar.dma_start(
                    out=bass.AP(qt_dram.tensor, qt * 128,
                                [[TL, 128], [128 * TL, 16], [1, 128]]),
                    in_=BB3[:, 0:2048].rearrange("p (h j) -> p h j", h=16)
                ).then_inc(sm.q_st, 16)
            for i in range(16):
                scalar.wait_ge(WG[i % 2], 16 * (i // 2 + 1))
                scalar.dma_start(out=wo_bf[i * 128:(i + 1) * 128, :],
                                 in_=P32B[:, (i % 2) * 2048:(i % 2) * 2048 + 2048]
                                 ).then_inc(WS[i % 2], 16)
            for tt in range(32):
                scalar.wait_ge(sm.k_pe2, tt + 1)
                if tt >= 2:
                    scalar.wait_ge(sm.k_pe, 2 * (tt - 2) + 2)
                scalar.activation(BB2[:, (tt % 2) * 2048:(tt % 2) * 2048 + 2048],
                                  PSBb[:, 0:2048], AF.Identity)
                scalar.activation(DUMW[:, 128:256], IDB[:, 0:128], AF.Identity).then_inc(sm.k_ac, 1)
                scalar.wait_ge(sm.k_pe, 2 * tt + 1)
                if tt >= 1:
                    scalar.wait_ge(sm.k_dv, 4 * (tt - 1) + 1)
                scalar.activation(FB3[:, 0:512], PSA[:, 0:512], AF.Identity)
                scalar.activation(DUMW[:, 128:256], IDB[:, 0:128], AF.Identity).then_inc(sm.k_ac, 1)
                scalar.activation(VS[:, tt * 512:(tt + 1) * 512], PSA[:, 512:1024],
                                  AF.Identity)
                scalar.activation(DUMW[:, 128:256], IDB[:, 0:128], AF.Identity).then_inc(sm.k_ac, 1)
                scalar.wait_ge(sm.k_dv, 4 * tt + 1)
                scalar.activation(COLC[:, 0:4], COLA[:, 0:4], AF.Sqrt,
                                  bias=EPSC[:, 0:1], scale=1.0 / D)
                scalar.activation(DUM[:, 0:4], COLC[:, 0:4], AF.Sqrt
                                  ).then_inc(sm.k_ac, 1)
                scalar.wait_ge(sm.k_pe, 2 * tt + 2)
                scalar.wait_ge(sm.k_dv, 4 * tt + 3)
                scalar.activation(FB4[:, 0:512].bitcast(BF16)[:, 0:512],
                                  PSBb[:, 2048:2560], AF.Identity)
                scalar.activation(DUMW[:, 128:256], IDB[:, 0:128], AF.Identity).then_inc(sm.k_ac, 1)
            for u in A3S:
                i, sl, h = u["i"], u["sl"], u["h"]
                if h == 0 and sl > 0:
                    emit_a4_scalar(scalar, sl - 1)
                n1 = u["n1_0"]
                for qt in range(4):
                    for k5 in range(KT512[sl]):
                        scalar.wait_ge(sm.a_p1, n1 + 1)
                        if n1 >= 2:
                            scalar.wait_ge(sm.a_d1, n1 - 1)
                        scalar.activation(
                            PS1F[:, (n1 % 2) * 512:(n1 % 2) * 512 + 512],
                            PSA[:, (n1 % 2) * 512:(n1 % 2) * 512 + 512],
                            AF.Identity)
                        scalar.activation(DUMW[:, 128:256], IDB[:, 0:128], AF.Identity).then_inc(sm.a_a1, 1)
                        n1 += 1
                scalar.wait_ge(sm.a_pr, i + 1)
                scalar.activation(NEGF[0:1, 0:512], PSB[0:1, 1024:1536],
                                  AF.Identity)
                scalar.activation(DUMW[:, 128:256], IDB[:, 0:128], AF.Identity).then_inc(sm.a_an, 1)
                n2 = u["n2_0"]
                for kt in range(KT128[sl]):
                    scalar.wait_ge(sm.a_p2, n2 + 1)
                    if kt >= KT128[sl] - 8:
                        jj = kt - (KT128[sl] - 8)
                        cnt = 8 * i + jj + 1
                        scalar.activation(
                            PS2F[:, (n2 % 2) * 512:(n2 % 2) * 512 + 512],
                            PSA[:, 1024 + (n2 % 2) * 512:
                                1024 + (n2 % 2) * 512 + 512],
                            AF.Identity)
                        scalar.activation(DUMW[:, 128:256], IDB[:, 0:128], AF.Identity).then_inc(sm.a_e2, 1)
                        scalar.wait_ge(sm.a_d2, cnt)
                        scalar.activation(
                            PTB[:, (n2 % 2) * 512:(n2 % 2) * 512 + 512],
                            PS2F[:, (n2 % 2) * 512:(n2 % 2) * 512 + 512],
                            AF.Exp).then_inc(sm.a_ae, 1)
                    else:
                        scalar.activation(
                            PTB[:, (n2 % 2) * 512:(n2 % 2) * 512 + 512],
                            PSA[:, 1024 + (n2 % 2) * 512:
                                1024 + (n2 % 2) * 512 + 512],
                            AF.Exp).then_inc(sm.a_ae, 1)
                    n2 += 1
                scalar.wait_ge(sm.a_pe, i + 1)
                if i >= 1:
                    scalar.wait_ge(sm.a_df, i)
                for qt in range(4):
                    scalar.activation(OFN[:, qt * 160:qt * 160 + 129],
                                      PSB[:, qt * 512:qt * 512 + 129],
                                      AF.Identity)
                scalar.activation(DUMW[:, 128:256], IDB[:, 0:128], AF.Identity).then_inc(sm.a_ao, 1)
                if DEBUG and i == 0:
                    scalar.wait_ge(sm.a_dm, 1)
                    scalar.wait_ge(sm.a_dn, 1)
                    scalar.wait_ge(sm.a_df, 1)
                    scalar.dma_start(out=ofn_dbg[:, :], in_=OFN[:, 0:640]
                                     ).then_inc(sm.c_ld, 16)
                    scalar.dma_start(out=mcb_dbg[:, 0:4], in_=MCOL[:, :]
                                     ).then_inc(sm.c_ld, 16)
                    scalar.dma_start(out=negm_dbg[:, :], in_=NEGM[0:1, :]
                                     ).then_inc(sm.c_ld, 16)
                    for dq in range(4):
                        scalar.dma_start(
                            out=os_dbg[:, dq * 128:(dq + 1) * 128],
                            in_=OS[:, dq * 2048:dq * 2048 + 128]
                        ).then_inc(sm.c_ld, 16)
                    scalar.wait_ge(sm.c_ld, 64 + 14 * 16)
            emit_a4_scalar(scalar, 3)
            for s in range(16):
                for sub in range(8):
                    for tq in range(4):
                        k = (s * 8 + sub) * 4 + tq
                        scalar.wait_ge(sm.f_pe, k + 1)
                        if k >= 1:
                            scalar.wait_ge(sm.f_dv, k)
                        gb = PSA[:, (tq % 2) * 512:(tq % 2) * 512 + 512]
                        ub = PSA[:, 1024 + (tq % 2) * 512:
                                 1024 + (tq % 2) * 512 + 512]
                        scalar.activation(FB3[:, 0:512], gb,
                                          AF.Identity).then_inc(sm.f_ac, 1)
                        scalar.activation(FB3[:, 512:1024], ub,
                                          AF.Identity).then_inc(sm.f_ac, 1)
                        scalar.wait_ge(sm.f_tg, k + 1)
                        scalar.activation(FB2[:, 512:1024], FB2[:, 0:512],
                                          AF.Tanh).then_inc(sm.f_th, 1)
                for tg in range(16):
                    it = s * 16 + tg
                    scalar.wait_ge(sm.f_pe2, it + 1)
                    accb = PSA if tg % 2 == 0 else PSB
                    stg = FB4 if tg % 2 == 0 else FB1
                    if it >= 2:
                        scalar.wait_ge(FG[it % 2], 16 * (it // 2))
                    scalar.activation(stg[:, :], accb[:, :],
                                      AF.Identity).then_inc(sm.f_ac2, 1)
            for tg in range(16):
                scalar.wait_ge(sm.l_dv, 2 * tg + 1)
                scalar.activation(COLC[:, 0:1], COLA[:, 5:6], AF.Sqrt,
                                  bias=EPSC[:, 1:2], scale=1.0)
                scalar.activation(DUM[:, 0:1], COLC[:, 0:1], AF.Sqrt
                                  ).then_inc(sm.l_ac, 1)
                scalar.wait_ge(sm.l_dv, 2 * tg + 2)
                scalar.dma_start(out=y2out[tg * 128:(tg + 1) * 128, :],
                                 in_=FB3[:, :]).then_inc(sm.l_st, 16)
            scalar.wait_ge(sm.l_st, 16 * 16)
            if DEBUG:
                scalar.dma_start(out=it_dbg[:, :], in_=INTERT[:, :]
                                 ).then_inc(sm.c_ld, 16)
                scalar.dma_start(out=y1t_dbg[:, :], in_=P64A[:, :]
                                 ).then_inc(sm.c_ld, 16)
                scalar.dma_start(out=y1td_dbg[:, :], in_=y1t_dram[:, :]
                                 ).then_inc(sm.c_ld, 16)
                scalar.wait_ge(sm.c_ld, 64 + 19 * 16)

        @block.vector
        def _(vector):
            vector.memset(ONES1[:, :], 1.0)
            vector.memset(ONESC[:, :], 1.0)
            vector.memset(EPSC[:, 0:1], EPS_RMS)
            vector.memset(EPSC[:, 1:2], EPS_LN)
            vector.wait_ge(sm.c_ld, 64)
            vector.tensor_scalar(GQS[:, :], GQC[:, :], QK_SCALE, None, ALU.mult)
            for j in range(NDC):
                vector.wait_ge(DL[j % 2], 32 * (j // 2 + 1))
                vector.wait_ge(sm.d_ac, j + 1)
                if j >= 1:
                    vector.wait_ge(sm.d_sc, 32 * j)
                cb = CSD[:, (j % 2) * 2048:(j % 2) * 2048 + 2048]
                nx = DNSX[:, (j % 2) * 2048:(j % 2) * 2048 + 2048]
                vector.tensor_scalar(DI1[:, :], cb, 15, None, ALU.bitwise_and)
                vector.tensor_scalar(DI2[:, :], cb, 4, None,
                                     ALU.logical_shift_right)
                vector.tensor_scalar(DI3[:, :], cb, 3, 1,
                                     ALU.logical_shift_right, ALU.bitwise_and)
                vector.tensor_scalar(DI4[:, :], cb, 7, 1,
                                     ALU.logical_shift_right, ALU.bitwise_and)
                for half, (xx, ss, dst) in enumerate(
                        ((DI1, DI3, DLO), (DI2, DI4, DHI))):
                    sl2 = slice(half * 2048, half * 2048 + 2048)
                    vector.tensor_scalar(DT1[:, :], xx[:, :], 64.0, 16128.0,
                                         ALU.mult, ALU.add)
                    vector.scalar_tensor_tensor(DBITS[:, sl2], ss[:, :],
                                                -33280.0, DT1[:, :],
                                                ALU.mult, ALU.add)
                    uu = DBITS[:, sl2].bitcast(BF16)
                    vector.tensor_scalar(DC1[:, :], uu, 1.0, -1.0,
                                         ALU.min, ALU.max)
                    vector.tensor_scalar(DC2[:, :], uu, 2.0, 1.0,
                                         ALU.mult, ALU.min)
                    vector.tensor_tensor(DT2[:, :], uu, DC1[:, :], ALU.add)
                    vector.scalar_tensor_tensor(DT3[:, :], DC2[:, :], -1.0,
                                                DT2[:, :], ALU.max,
                                                ALU.subtract)
                    if half == 0:
                        vector.tensor_tensor(dst[:, :], DT3[:, :], nx, ALU.mult)
                    else:
                        vector.tensor_tensor(dst[:, :], DT3[:, :], nx, ALU.mult
                                             ).then_inc(sm.d_dv, 1)
            for qt in range(16):
                vector.wait_ge(sm.q_ac, 4 * qt + 2)
                for hh in range(16):
                    op = vector.scalar_tensor_tensor(
                        FB2[:, hh * 128:(hh + 1) * 128],
                        FB3[:, hh * 128:(hh + 1) * 128], 0.0,
                        FB3[:, hh * 128:(hh + 1) * 128],
                        ALU.add, ALU.mult, accum_out=COLA[:, hh:hh + 1])
                vector.tensor_copy(out=DUM[:, 16:32], in_=COLA[:, 0:16]
                                   ).then_inc(sm.q_dv, 1)
                vector.wait_ge(sm.q_ac, 4 * qt + 3)
                vector.reciprocal(COLB[:, 0:16], COLC[:, 0:16]).then_inc(sm.q_dv, 1)
                vector.tensor_copy(out=DUM[:, 32:48], in_=COLB[:, 0:16])
                vector.tensor_copy(out=DUM[:, 48:64], in_=DUM[:, 32:48])
                xt = BB2[:, (qt % 2) * 2048:(qt % 2) * 2048 + 2048]
                for hh in range(16):
                    vector.tensor_scalar(xt[:, hh * 128:(hh + 1) * 128],
                                         FB3[:, hh * 128:(hh + 1) * 128],
                                         COLB[:, hh:hh + 1], None, ALU.mult)
                vector.tensor_copy(out=DUMW[:, 0:128], in_=IDB[:, 0:128]).then_inc(sm.q_dv, 1)
                vector.wait_ge(sm.q_ac, 4 * qt + 4)
                if qt >= 1:
                    vector.wait_ge(sm.q_st, 16 * qt)
                vector.tensor_scalar(BB3[:, 0:2048],
                                     FB3[:, :].bitcast(BF16)[:, 0:2048],
                                     GQS[:, 0:1], None, ALU.mult)
                vector.tensor_copy(out=DUMW[:, 0:128], in_=IDB[:, 0:128]).then_inc(sm.q_dv, 1)
            vector.wait_ge(sm.d_sc, 16 * 48)
            for tt in range(32):
                vector.wait_ge(sm.k_ac, 5 * tt + 2)
                for sgi in range(4):
                    op = vector.scalar_tensor_tensor(
                        FB2[:, sgi * 128:(sgi + 1) * 128],
                        FB3[:, sgi * 128:(sgi + 1) * 128], 0.0,
                        FB3[:, sgi * 128:(sgi + 1) * 128],
                        ALU.add, ALU.mult, accum_out=COLA[:, sgi:sgi + 1])
                vector.tensor_copy(out=DUM[:, 16:20], in_=COLA[:, 0:4]
                                   ).then_inc(sm.k_dv, 1)
                vector.wait_ge(sm.k_ac, 5 * tt + 4)
                vector.reciprocal(COLB[:, 0:4], COLC[:, 0:4]).then_inc(sm.k_dv, 1)
                vector.tensor_copy(out=DUM[:, 32:36], in_=COLB[:, 0:4])
                vector.tensor_copy(out=DUM[:, 48:52], in_=DUM[:, 32:36])
                xt = BB2[:, (tt % 2) * 2048:(tt % 2) * 2048 + 2048]
                for sgi in range(4):
                    vector.tensor_scalar(xt[:, sgi * 128:(sgi + 1) * 128],
                                         FB3[:, sgi * 128:(sgi + 1) * 128],
                                         COLB[:, sgi:sgi + 1], None,
                                         ALU.mult)
                vector.tensor_copy(out=DUMW[:, 0:128], in_=IDB[:, 0:128]).then_inc(sm.k_dv, 1)
                vector.wait_ge(sm.k_ac, 5 * tt + 5)
                for sgi in range(4):
                    vector.tensor_scalar(
                        KTS[:, sgi * 4096 + tt * 128:sgi * 4096 + (tt + 1) * 128],
                        FB4[:, 0:512].bitcast(BF16)[:, sgi * 128:(sgi + 1) * 128],
                        GKC[:, 0:1], None, ALU.mult)
                vector.tensor_copy(out=DUMW[:, 0:128], in_=IDB[:, 0:128]).then_inc(sm.k_dv, 1)
            for u in A3S:
                i, sl, h = u["i"], u["sl"], u["h"]
                if h == 0 and sl > 0:
                    emit_a4_vector(vector, sl - 1)
                n1 = u["n1_0"]
                for qt in range(4):
                    for k5 in range(KT512[sl]):
                        vector.wait_ge(sm.a_a1, n1 + 1)
                        bank = PS1F[:, (n1 % 2) * 512:(n1 % 2) * 512 + 512]
                        if k5 >= KT512[sl] - 2:
                            jj = k5 - (KT512[sl] - 2)
                            vector.tensor_tensor(
                                bank, bank,
                                M1S[:, (jj * 4 + qt) * 512:
                                    (jj * 4 + qt + 1) * 512], ALU.add)
                        vector.reduce_max(COLB[:, k5:k5 + 1], bank,
                                          axis=AX.X).then_inc(sm.a_d1, 1)
                        n1 += 1
                    vector.tensor_copy(out=DUM[:, 32:48], in_=IDB[:, 0:16])
                    vector.tensor_copy(out=DUM[:, 48:64], in_=IDB[:, 16:32])
                    vector.reduce_max(MCOL[:, qt:qt + 1],
                                      COLB[:, 0:KT512[sl]], axis=AX.X)
                vector.tensor_copy(out=DUM[:, 32:48], in_=IDB[:, 0:16])
                vector.tensor_copy(out=DUM[:, 48:64], in_=IDB[:, 16:32])
                vector.tensor_copy(out=MCB[:, :], in_=MCOL[:, :])
                vector.tensor_copy(out=DUM[:, 16:20], in_=MCB[:, 0:4])
                vector.tensor_copy(out=DUMW[:, 0:128], in_=IDB[:, 0:128]).then_inc(sm.a_dm, 1)
                vector.wait_ge(sm.a_an, i + 1)
                vector.tensor_scalar(NEGM[0:1, :], NEGF[0:1, 0:512], -1.0,
                                     None, ALU.mult)
                vector.tensor_copy(out=DUMW[:, 0:128], in_=IDB[:, 0:128]).then_inc(sm.a_dn, 1)
                n2 = u["n2_0"]
                for kt in range(KT128[sl]):
                    if kt >= KT128[sl] - 8:
                        jj = kt - (KT128[sl] - 8)
                        cnt = 8 * i + jj + 1
                        vector.wait_ge(sm.a_e2, cnt)
                        bank = PS2F[:, (n2 % 2) * 512:(n2 % 2) * 512 + 512]
                        vector.tensor_tensor(bank, bank,
                                             M2S[:, jj * 512:(jj + 1) * 512],
                                             ALU.add)
                        vector.tensor_copy(out=DUMW[:, 0:128], in_=IDB[:, 0:128]).then_inc(sm.a_d2, 1)
                    n2 += 1
                vector.wait_ge(sm.a_ao, i + 1)
                for qt in range(4):
                    vector.reciprocal(RC[:, qt:qt + 1],
                                      OFN[:, qt * 160 + 128:qt * 160 + 129])
                vector.tensor_copy(out=DUM[:, 32:36], in_=RC[:, 0:4])
                vector.tensor_copy(out=DUM[:, 48:52], in_=DUM[:, 32:36])
                for qt in range(4):
                    vector.tensor_scalar(
                        OS[:, qt * 2048 + h * 128:qt * 2048 + (h + 1) * 128],
                        OFN[:, qt * 160:qt * 160 + 128],
                        RC[:, qt:qt + 1], None, ALU.mult)
                vector.tensor_copy(out=DUMW[:, 0:128], in_=IDB[:, 0:128]).then_inc(sm.a_df, 1)
            emit_a4_vector(vector, 3)
            for s in range(16):
                for sub in range(8):
                    for tq in range(4):
                        k = (s * 8 + sub) * 4 + tq
                        vector.wait_ge(sm.f_ac, 2 * k + 2)
                        if s > 0 and sub == 0 and tq == 0:
                            vector.wait_ge(sm.f_pe2, 16 * s)
                        gf = FB3[:, 0:512]
                        upf = FB3[:, 512:1024]
                        g2 = FB3[:, 1024:1536]
                        u1 = FB3[:, 1536:2048]
                        targ = FB2[:, 0:512]
                        th = FB2[:, 512:1024]
                        vv = FB2[:, 1024:1536]
                        vector.tensor_tensor(g2, gf, gf, ALU.mult)
                        vector.tensor_scalar(u1, g2, G_C1, G_C2,
                                             ALU.mult, ALU.add)
                        vector.tensor_tensor(targ, u1, gf, ALU.mult
                                             ).then_inc(sm.f_tg, 1)
                        vector.wait_ge(sm.f_th, k + 1)
                        vector.scalar_tensor_tensor(vv, th, 1.0, gf,
                                                    ALU.add, ALU.mult)
                        op = vector.scalar_tensor_tensor(
                            INTERT[:, sub * 2048 + tq * 512:
                                   sub * 2048 + (tq + 1) * 512],
                            vv, 0.5, upf, ALU.mult, ALU.mult)
                        if sub == 7 and tq == 3:
                            op = vector.tensor_copy(out=DUMW[:, 0:128], in_=IDB[:, 0:128])
                        op.then_inc(sm.f_dv, 1)
            for tg in range(16):
                vector.wait_ge(sm.l_ld, 32 * (tg + 1))
                if tg >= 1:
                    vector.wait_ge(sm.l_st, 16 * tg)
                vector.scalar_tensor_tensor(FB4[:, :], FB1[:, :], 0.0,
                                            FB2[:, :], ALU.add, ALU.add,
                                            accum_out=COLA[:, 0:1])
                vector.scalar_tensor_tensor(FB3[:, :], FB4[:, :], 0.0,
                                            FB4[:, :], ALU.add, ALU.mult,
                                            accum_out=COLA[:, 1:2])
                vector.tensor_scalar(COLA[:, 2:3], COLA[:, 0:1], 1.0 / HB,
                                     None, ALU.mult)
                vector.tensor_copy(out=DUM[:, 32:48], in_=IDB[:, 0:16])
                vector.tensor_copy(out=DUM[:, 48:64], in_=IDB[:, 16:32])
                vector.tensor_tensor(COLA[:, 3:4], COLA[:, 2:3], COLA[:, 2:3],
                                     ALU.mult)
                vector.tensor_scalar(COLA[:, 4:5], COLA[:, 1:2], 1.0 / HB,
                                     None, ALU.mult)
                vector.tensor_copy(out=DUM[:, 32:48], in_=IDB[:, 0:16])
                vector.tensor_copy(out=DUM[:, 48:64], in_=IDB[:, 16:32])
                vector.tensor_tensor(COLA[:, 5:6], COLA[:, 4:5], COLA[:, 3:4],
                                     ALU.subtract)
                vector.tensor_copy(out=DUM[:, 16:17], in_=COLA[:, 5:6]
                                   ).then_inc(sm.l_dv, 1)
                vector.wait_ge(sm.l_ac, tg + 1)
                vector.reciprocal(COLB[:, 0:1], COLC[:, 0:1])
                vector.tensor_copy(out=DUM[:, 32:33], in_=COLB[:, 0:1])
                vector.tensor_copy(out=DUM[:, 48:49], in_=DUM[:, 32:33])
                vector.tensor_scalar(FB3[:, :], FB4[:, :], COLA[:, 2:3],
                                     COLB[:, 0:1], ALU.subtract, ALU.mult)
                vector.tensor_copy(out=DUMW[:, 0:128], in_=IDB[:, 0:128]).then_inc(sm.l_dv, 1)

    stack.close()
    return nc


def _build_masks(sub):
    m1 = np.zeros((4, 128, 8 * 512), np.float32)
    m2 = np.zeros((4, 128, 8 * 512), np.float32)
    for sl in range(4):
        P = SLOT_CHUNKS[sub][sl]
        for j in range(2):
            k5 = KT512[sl] - 2 + j
            for qt in range(4):
                q = np.arange(128)[:, None] + qt * 128 + P * 512
                k = np.arange(512)[None, :] + k5 * 512
                m1[sl][:, (j * 4 + qt) * 512:(j * 4 + qt + 1) * 512] = \
                    np.where(k <= q, 0.0, NEGV)
        for j in range(8):
            kt = KT128[sl] - 8 + j
            k = np.arange(128)[:, None] + kt * 128
            q = np.arange(512)[None, :] + P * 512
            m2[sl][:, j * 512:(j + 1) * 512] = np.where(k <= q, 0.0, NEGV)
    m1 = m1.reshape(4 * 128, 8 * 512).astype(ml_dtypes.bfloat16)
    m2 = m2.reshape(4 * 128, 8 * 512).astype(ml_dtypes.bfloat16)
    return m1, m2


_NC_CACHE = [None]


def kernel(L0_attn_x, L0_attn_Wq, L0_attn_Wk, L0_attn_Wv, L0_attn_Wo,
           L0_attn_gq, L0_attn_gk, L0_ffn_gate_W_packed, L0_ffn_gate_scales,
           L0_ffn_up_W_packed, L0_ffn_up_scales, L0_ffn_down_W_packed,
           L0_ffn_down_scales):
    x = np.ascontiguousarray(np.asarray(L0_attn_x, dtype=np.float32))
    Wq = np.ascontiguousarray(np.asarray(L0_attn_Wq, dtype=np.float32))
    Wk = np.asarray(L0_attn_Wk, dtype=np.float32)
    Wv = np.asarray(L0_attn_Wv, dtype=np.float32)
    Wkv = np.ascontiguousarray(np.concatenate([Wk, Wv], axis=0))
    Wo = np.ascontiguousarray(np.asarray(L0_attn_Wo, dtype=np.float32))
    gq = np.ascontiguousarray(np.asarray(L0_attn_gq, dtype=np.float32
                                         ).reshape(D, 1))
    gk = np.ascontiguousarray(np.asarray(L0_attn_gk, dtype=np.float32
                                         ).reshape(D, 1))
    gW = np.ascontiguousarray(np.asarray(L0_ffn_gate_W_packed, np.int32).T)
    uW = np.ascontiguousarray(np.asarray(L0_ffn_up_W_packed, np.int32).T)
    dW = np.ascontiguousarray(np.asarray(L0_ffn_down_W_packed, np.int32).T)
    gS = np.ascontiguousarray(np.asarray(L0_ffn_gate_scales, np.float32).T)
    uS = np.ascontiguousarray(np.asarray(L0_ffn_up_scales, np.float32).T)
    dS = np.ascontiguousarray(np.asarray(L0_ffn_down_scales, np.float32).T)
    e8v = np.zeros((8, 128), np.float32)
    for jj in range(8):
        e8v[jj, jj * 16:(jj + 1) * 16] = -1.0
    identv = np.eye(128, dtype=np.float32).astype(ml_dtypes.bfloat16)

    if _NC_CACHE[0] is None:
        _NC_CACHE[0] = build_nc()
    nc = _NC_CACHE[0]
    in_maps = []
    for c in range(8):
        b, sub = c // 2, c % 2
        chunks = SLOT_CHUNKS[sub]
        xqv = np.ascontiguousarray(
            np.concatenate([x[b, ch * 512:(ch + 1) * 512] for ch in chunks], 0))
        m1, m2 = _build_masks(sub)
        in_maps.append({
            "xb": np.ascontiguousarray(x[b]),
            "xq": xqv,
            "wq": Wq, "wkv": Wkv, "wo": Wo, "gqv": gq, "gkv": gk,
            "gsh": np.ascontiguousarray(gW[c * 128:(c + 1) * 128]),
            "ush": np.ascontiguousarray(uW[c * 128:(c + 1) * 128]),
            "dsh": np.ascontiguousarray(dW[c * 1024:(c + 1) * 1024]),
            "gsc": np.ascontiguousarray(gS[c * 8:(c + 1) * 8]),
            "usc": np.ascontiguousarray(uS[c * 8:(c + 1) * 8]),
            "dsc": np.ascontiguousarray(dS[c * 64:(c + 1) * 64]),
            "e8": e8v, "identin": identv, "m1in": m1, "m2in": m2,
        })
    res = run_bass_kernel_spmd(nc, in_maps, core_ids=list(range(8)))
    out = np.empty((4, S, HB), np.float32)
    for c in range(8):
        b, sub = c // 2, c % 2
        y2 = res.results[c]["y2out"]
        for sl, ch in enumerate(SLOT_CHUNKS[sub]):
            out[b, ch * 512:(ch + 1) * 512] = y2[sl * 512:(sl + 1) * 512]
    return out


# version 08
# speedup vs baseline: 1.0454x; 1.0454x over previous
import numpy as np
import ml_dtypes
import concourse.bass as bass
import concourse.mybir as mybir
from concourse.bass_utils import run_bass_kernel_spmd

F32 = mybir.dt.float32
BF16 = mybir.dt.bfloat16
I32 = mybir.dt.int32
I16 = mybir.dt.int16
ALU = mybir.AluOpType
AF = mybir.ActivationFunctionType
AX = mybir.AxisListType

HB = 2048
S = 4096
TL = 2048
NH = 16
KVH = 4
D = 128
MLP = 16384
EPS_RMS = 1e-6
EPS_LN = 1e-5
QK_SCALE = float(D) ** -0.5
G_C1 = 0.044715 * 0.7978845608028654
G_C2 = 0.7978845608028654
KT512 = [8, 6, 4, 2]
KT128 = [32, 24, 16, 8]
SLOT_CHUNKS = {0: [7, 4, 3, 0], 1: [6, 5, 2, 1]}
NEGV = -60000.0
DEBUG = False

DCH = ([("g", 0, j) for j in range(8)] + [("u", 0, j) for j in range(8)]
       + [("d", t, 0) for t in range(8)])
NDC = len(DCH)
A3L = [(sl, h) for sl in range(4) for h in range(16)]


def _a3_sched():
    n1 = n2 = 0
    out = []
    for i, (sl, h) in enumerate(A3L):
        u = {"i": i, "sl": sl, "h": h, "n1_0": n1, "n2_0": n2}
        n1 += 4 * KT512[sl]
        n2 += KT128[sl]
        u["n1_1"], u["n2_1"] = n1, n2
        out.append(u)
    return out


A3S = _a3_sched()

A4_BASE = 40


def _a4_blk_cnt(sl, pr, ks):
    return sl * A4_BASE + 4 + pr * 18 + ks + 1


def _a4_y1t_cnt(sl, qt):
    return sl * A4_BASE + 4 + (qt // 2) * 18 + 16 + (qt % 2) + 1


def build_nc():
    nc = bass.Bass("TRN2", target_bir_lowering=False,
                   detect_race_conditions=False)
    xb = nc.dram_tensor("xb", [S, HB], F32, kind="ExternalInput").ap()
    xq = nc.dram_tensor("xq", [TL, HB], F32, kind="ExternalInput").ap()
    wq = nc.dram_tensor("wq", [HB, NH * D], F32, kind="ExternalInput").ap()
    wkv = nc.dram_tensor("wkv", [2 * HB, KVH * D], F32, kind="ExternalInput").ap()
    wo = nc.dram_tensor("wo", [NH * D, HB], F32, kind="ExternalInput").ap()
    gqv = nc.dram_tensor("gqv", [D, 1], F32, kind="ExternalInput").ap()
    gkv = nc.dram_tensor("gkv", [D, 1], F32, kind="ExternalInput").ap()
    gsh = nc.dram_tensor("gsh", [128, MLP], I32, kind="ExternalInput").ap()
    ush = nc.dram_tensor("ush", [128, MLP], I32, kind="ExternalInput").ap()
    dsh = nc.dram_tensor("dsh", [1024, HB], I32, kind="ExternalInput").ap()
    gsc = nc.dram_tensor("gsc", [8, MLP], F32, kind="ExternalInput").ap()
    usc = nc.dram_tensor("usc", [8, MLP], F32, kind="ExternalInput").ap()
    dsc = nc.dram_tensor("dsc", [64, HB], F32, kind="ExternalInput").ap()
    e8 = nc.dram_tensor("e8", [8, 128], F32, kind="ExternalInput").ap()
    identin = nc.dram_tensor("identin", [128, 128], BF16, kind="ExternalInput").ap()
    m1in = nc.dram_tensor("m1in", [4 * 128, 8 * 512], BF16, kind="ExternalInput").ap()
    m2in = nc.dram_tensor("m2in", [4 * 128, 8 * 512], BF16, kind="ExternalInput").ap()
    y2out = nc.dram_tensor("y2out", [TL, HB], F32, kind="ExternalOutput").ap()
    if DEBUG:
        y1d_out = nc.dram_tensor("y1d_out", [TL, HB], F32, kind="ExternalOutput").ap()
        qt_out = nc.dram_tensor("qt_out", [NH * D, TL], BF16, kind="ExternalOutput").ap()
        gg_out = nc.dram_tensor("gg_out", [HB, MLP], BF16, kind="ExternalOutput").ap()
        gd_out = nc.dram_tensor("gd_out", [MLP, HB], BF16, kind="ExternalOutput").ap()
        pt_out = nc.dram_tensor("pt_out", [TL, HB], F32, kind="ExternalOutput").ap()
        qf_dbg = nc.dram_tensor("qf_dbg", [128, HB], F32, kind="ExternalOutput").ap()
        qn2_dbg = nc.dram_tensor("qn2_dbg", [128, HB], BF16, kind="ExternalOutput").ap()
        ofn_dbg = nc.dram_tensor("ofn_dbg", [128, 640], F32, kind="ExternalOutput").ap()
        os_dbg = nc.dram_tensor("os_dbg", [128, 512], BF16, kind="ExternalOutput").ap()
        mcb_dbg = nc.dram_tensor("mcb_dbg", [128, 8], F32, kind="ExternalOutput").ap()
        negm_dbg = nc.dram_tensor("negm_dbg", [1, 512], BF16, kind="ExternalOutput").ap()
        a4c_dbg = nc.dram_tensor("a4c_dbg", [128, 8], F32, kind="ExternalOutput").ap()
        fb2_dbg = nc.dram_tensor("fb2_dbg", [128, HB], F32, kind="ExternalOutput").ap()
        it_dbg = nc.dram_tensor("it_dbg", [128, 16384], BF16, kind="ExternalOutput").ap()
        y1t_dbg = nc.dram_tensor("y1t_dbg", [128, 32768], BF16, kind="ExternalOutput").ap()
        y1td_dbg = nc.dram_tensor("y1td_dbg", [2048, 2048], BF16, kind="ExternalOutput").ap()
        qe_dbg = nc.dram_tensor("qe_dbg", [128, HB], BF16, kind="ExternalOutput").ap()
        col_dbg = nc.dram_tensor("col_dbg", [128, 48], F32, kind="ExternalOutput").ap()
        qn_dbg = nc.dram_tensor("qn_dbg", [128, HB], BF16, kind="ExternalOutput").ap()

    wo_bf = nc.dram_tensor("wo_bf", [NH * D, HB], BF16, kind="Internal").ap()
    qt_dram = nc.dram_tensor("qt_dram", [NH * D, TL], BF16, kind="Internal").ap()
    y1_dram = nc.dram_tensor("y1_dram", [TL, HB], F32, kind="Internal").ap()
    y1t_dram = nc.dram_tensor("y1t_dram", [HB, TL], BF16, kind="Internal").ap()
    part = nc.dram_tensor("part", [TL, HB], F32, kind="Internal").ap()
    pg = nc.dram_tensor("pg", [256, MLP], BF16, kind="Internal").ap()
    pu = nc.dram_tensor("pu", [256, MLP], BF16, kind="Internal").ap()
    pd = nc.dram_tensor("pd", [2048, HB], BF16, kind="Internal").ap()
    gg = nc.dram_tensor("gg", [HB, MLP], BF16, kind="Internal").ap()
    gu = nc.dram_tensor("gu", [HB, MLP], BF16, kind="Internal").ap()
    gd = nc.dram_tensor("gd", [MLP, HB], BF16, kind="Internal").ap()

    import contextlib
    stack = contextlib.ExitStack()

    def sbt(name, shape, dt):
        return stack.enter_context(nc.sbuf_tensor(name, shape, dt))

    P64A = sbt("P64A", [128, 32768], BF16)
    P32A = sbt("P32A", [128, 16384], BF16)
    P32B = sbt("P32B", [128, 16384], BF16)
    MISC = sbt("MISC", [128, 8192], BF16)
    FB1 = sbt("FB1", [128, 2048], F32)
    FB2 = sbt("FB2", [128, 2048], F32)
    FB3 = sbt("FB3", [128, 2048], F32)
    FB4 = sbt("FB4", [128, 2048], F32)
    BB1 = sbt("BB1", [128, 4096], BF16)
    BB2 = sbt("BB2", [128, 4096], BF16)
    BB3 = sbt("BB3", [128, 4096], BF16)
    E8S = sbt("E8S", [8, 128], F32)
    IDB = sbt("IDB", [128, 128], BF16)
    ONES1 = sbt("ONES1", [1, 128], BF16)
    ONESC = sbt("ONESC", [128, 1], BF16)
    GQC = sbt("GQC", [128, 1], F32)
    GKC = sbt("GKC", [128, 1], F32)
    GQS = sbt("GQS", [128, 1], F32)
    COLA = sbt("COLA", [128, 16], F32)
    COLB = sbt("COLB", [128, 16], F32)
    COLC = sbt("COLC", [128, 16], F32)
    MCOL = sbt("MCOL", [128, 4], F32)
    MCB = sbt("MCB", [128, 4], BF16)
    DUM = sbt("DUM", [128, 64], F32)
    DUMW = sbt("DUMW", [128, 256], F32)
    NEGM = sbt("NEGM", [1, 512], BF16)
    PTB = sbt("PTB", [128, 1024], BF16)
    QTS = sbt("QTS", [128, 1024], BF16)
    RC = sbt("RC", [128, 4], F32)
    EPSC = sbt("EPSC", [128, 2], F32)

    CSD = P64A[:, 0:8192].bitcast(I32)
    DI1 = P64A[:, 8192:12288].bitcast(I32)
    DI2 = P64A[:, 12288:16384].bitcast(I32)
    DI3 = P64A[:, 16384:20480].bitcast(I32)
    DI4 = P64A[:, 20480:24576].bitcast(I32)
    DBITS = P64A[:, 24576:28672].bitcast(I16)
    DT1 = P64A[:, 28672:30720].bitcast(I16)
    DNSX = P32A[:, 0:4096]
    DLO = P32A[:, 4096:6144]
    DHI = P32A[:, 6144:8192]
    DC1 = P32A[:, 8192:10240]
    DC2 = P32A[:, 10240:12288]
    DT2 = P32A[:, 12288:14336]
    DT3 = P32A[:, 14336:16384]
    WQS = P64A
    WKS = P64A[:, 0:8192]
    WVS = P64A[:, 8192:16384]
    KTS = P32A
    VS = P32B
    OS = P64A[:, 16384:24576]
    OTS = P64A[:, 24576:32768]
    M1S = MISC[:, 0:4096]
    M2S = MISC[:, 4096:8192]
    SCL0 = FB1[0:8, 0:2048]
    SCL1 = FB2[0:8, 0:2048]
    PS1F = P64A[:, 0:2048].bitcast(F32)
    NEGF = P64A[0:1, 2048:3072].bitcast(F32)
    OFN = P64A[:, 3072:4352].bitcast(F32)
    PS2F = P64A[:, 4352:6400].bitcast(F32)
    AOF = P64A[:, 8192:16384].bitcast(F32)
    Y1TS = P64A
    INTERT = P32A
    GUBF = MISC
    WD8 = P32B

    PSA = stack.enter_context(nc.psum_tensor("PSA", [128, 2048], F32))
    PSB = stack.enter_context(nc.psum_tensor("PSB", [128, 2048], F32))
    PSAb = PSA[:, :].bitcast(BF16)
    PSBb = PSB[:, :].bitcast(BF16)

    sems = {}
    for n in ("c_ld", "d_l0", "d_l1", "d_sc", "d_pe", "d_ac", "d_dv", "d_cc",
              "w_g0", "w_g1", "w_s0", "w_s1",
              "q_gw", "q_g0", "q_g1", "q_pe", "q_pe2", "q_ac", "q_dv", "q_st",
              "k_gw", "k_g0", "k_g1", "k_pe", "k_pe2", "k_ac", "k_dv",
              "a_lm", "a_l0", "a_l1", "a_p1", "a_a1", "a_d1", "a_dm",
              "a_pr", "a_an", "a_dn", "a_ao",
              "a_p2", "a_e2", "a_d2", "a_ae", "a_pe", "a_df",
              "o_l0", "o_l1", "o_lx", "o_pe", "o_oa", "o_ac", "o_dv",
              "o_sy", "o_sz",
              "f_ly", "f_lw", "f_lg0", "f_lg1", "f_pe", "f_pe2",
              "f_ac", "f_ac2", "f_tg", "f_th", "f_dv", "f_g0", "f_g1",
              "l_ld", "l_ac", "l_dv", "l_st"):
        sems[n] = stack.enter_context(nc.semaphore(n))
    sm = type("SM", (), sems)
    DL = [sems["d_l0"], sems["d_l1"]]
    WG = [sems["w_g0"], sems["w_g1"]]
    WS = [sems["w_s0"], sems["w_s1"]]
    QG = [sems["q_g0"], sems["q_g1"]]
    KG = [sems["k_g0"], sems["k_g1"]]
    AL = [sems["a_l0"], sems["a_l1"]]
    OL = [sems["o_l0"], sems["o_l1"]]
    FLG = [sems["f_lg0"], sems["f_lg1"]]
    FG = [sems["f_g0"], sems["f_g1"]]

    with nc.Block() as block:

        def emit_a4_tensor(tensor, sl):
            tensor.wait_ge(sm.a_df, 16 * (sl + 1))
            tensor.wait_ge(sm.a_ae, A3S[16 * (sl + 1) - 1]["n2_1"])
            tensor.wait_ge(sm.a_d1, A3S[16 * (sl + 1) - 1]["n1_1"])
            for r in range(4):
                if r >= 1:
                    tensor.wait_ge(sm.o_oa, sl * 4 + r)
                for jj in range(16):
                    f = r * 16 + jj
                    ks, qt = f // 4, f % 4
                    op = tensor.transpose(PSAb[:, jj * 128:(jj + 1) * 128],
                                          OS[:, qt * 2048 + ks * 128:
                                             qt * 2048 + (ks + 1) * 128],
                                          IDB[:, :])
                op.then_inc(sm.o_pe, 1)
            tensor.wait_ge(sm.o_oa, sl * 4 + 4)
            for pr in range(2):
                if pr == 1:
                    tensor.wait_ge(sm.o_dv, 3 * (sl * 4 + 2))
                    tensor.wait_ge(sm.o_ac, 3 * (sl * 4 + 1) + 3)
                for ks in range(16):
                    it = sl * 32 + pr * 16 + ks
                    tensor.wait_ge(OL[it % 2], 16 * (it // 2 + 1))
                    wob = BB1[:, (it % 2) * 2048:(it % 2) * 2048 + 2048]
                    for qt2 in range(2):
                        qt = pr * 2 + qt2
                        aop = PSA if qt2 == 0 else PSB
                        for hc in range(4):
                            op = tensor.matmul(aop[:, hc * 512:(hc + 1) * 512],
                                               OTS[:, (ks * 4 + qt) * 128:
                                                   (ks * 4 + qt + 1) * 128],
                                               wob[:, hc * 512:(hc + 1) * 512],
                                               start=(ks == 0), stop=(ks == 15))
                    op.then_inc(sm.o_pe, 1)
                for qt2 in range(2):
                    qt = pr * 2 + qt2
                    it2 = sl * 4 + qt
                    tensor.wait_ge(sm.o_dv, 3 * it2 + 3)
                    if it2 >= 2:
                        tensor.wait_ge(sm.o_ac, 3 * (it2 - 2) + 3)
                    y1b = BB3[:, (qt % 2) * 2048:(qt % 2) * 2048 + 2048]
                    dstb = PSAb if qt2 == 0 else PSBb
                    for hh in range(16):
                        op = tensor.transpose(dstb[:, hh * 128:(hh + 1) * 128],
                                              y1b[:, hh * 128:(hh + 1) * 128],
                                              IDB[:, :])
                    op.then_inc(sm.o_pe, 1)

        def emit_a4_scalar(scalar, sl):
            for r in range(4):
                scalar.wait_ge(sm.o_pe, sl * A4_BASE + r + 1)
                op = scalar.activation(OTS[:, r * 2048:(r + 1) * 2048],
                                       PSAb[:, 0:2048], AF.Identity)
                if r == 3:
                    op = scalar.activation(DUMW[:, 128:256], IDB[:, 0:128], AF.Identity)
                op.then_inc(sm.o_oa, 1)
            def y1t_store(j):
                scalar.dma_start(
                    out=bass.AP(y1t_dram.tensor, j * 128,
                                [[TL, 128], [128 * TL, 16], [1, 128]]),
                    in_=BB2[:, (j % 2) * 2048:(j % 2) * 2048 + 2048
                            ].rearrange("p (h j) -> p h j", h=16)
                ).then_inc(sm.o_sz, 16)

            for qt in range(4):
                it2 = sl * 4 + qt
                pr, qt2 = qt // 2, qt % 2
                scalar.wait_ge(sm.o_pe, _a4_blk_cnt(sl, pr, 15))
                if it2 >= 2:
                    scalar.wait_ge(sm.o_dv, 3 * (it2 - 2) + 1)
                aop = PSA if qt2 == 0 else PSB
                scalar.activation(AOF[:, (qt % 2) * 2048:(qt % 2) * 2048 + 2048],
                                  aop[:, :], AF.Identity)
                scalar.activation(DUMW[:, 128:256], IDB[:, 0:128], AF.Identity).then_inc(sm.o_ac, 1)
                if it2 >= 2:
                    scalar.wait_ge(sm.o_sz, 16 * (it2 - 1))
                scalar.wait_ge(sm.o_dv, 3 * it2 + 2)
                if DEBUG and it2 == 0:
                    scalar.dma_start(out=a4c_dbg[:, :], in_=COLA[:, 0:8]
                                     ).then_inc(sm.c_ld, 16)
                    scalar.dma_start(out=fb2_dbg[:, :], in_=FB2[:, :]
                                     ).then_inc(sm.c_ld, 16)
                    scalar.wait_ge(sm.c_ld, 64 + 16 * 16)
                scalar.activation(COLC[:, 0:1], COLA[:, 5:6], AF.Sqrt,
                                  bias=EPSC[:, 1:2], scale=1.0)
                scalar.activation(DUM[:, 0:1], COLC[:, 0:1], AF.Sqrt
                                  ).then_inc(sm.o_ac, 1)
                scalar.wait_ge(sm.o_dv, 3 * it2 + 3)
                scalar.dma_start(out=y1_dram[it2 * 128:(it2 + 1) * 128, :],
                                 in_=FB4[:, :]).then_inc(sm.o_sy, 16)
                scalar.wait_ge(sm.o_pe, _a4_y1t_cnt(sl, qt))
                pb = PSAb if qt % 2 == 0 else PSBb
                scalar.activation(BB2[:, (qt % 2) * 2048:(qt % 2) * 2048 + 2048],
                                  pb[:, 0:2048], AF.Identity).then_inc(sm.o_ac, 1)
                if it2 >= 1:
                    y1t_store(it2 - 1)
            if sl == 3:
                scalar.wait_ge(sm.o_sy, 16 * 16)
                scalar.wait_ge(sm.o_sz, 16 * 15)
                y1t_store(15)

        def emit_a4_vector(vector, sl):
            for pr in range(2):
                for qt2 in range(2):
                    qt = pr * 2 + qt2
                    it2 = sl * 4 + qt
                    vector.wait_ge(sm.o_ac, 3 * it2 + 1)
                    vector.wait_ge(sm.o_lx, 16 * (it2 + 1))
                    vector.scalar_tensor_tensor(FB2[:, :],
                                                AOF[:, (qt % 2) * 2048:
                                                    (qt % 2) * 2048 + 2048],
                                                0.0, FB1[:, :],
                                                ALU.add, ALU.add,
                                                accum_out=COLA[:, 0:1]
                                                ).then_inc(sm.o_dv, 1)
                    vector.scalar_tensor_tensor(FB3[:, :], FB2[:, :], 0.0,
                                                FB2[:, :], ALU.add, ALU.mult,
                                                accum_out=COLA[:, 1:2])
                    vector.tensor_scalar(COLA[:, 2:3], COLA[:, 0:1], 1.0 / HB,
                                         None, ALU.mult)
                    vector.tensor_copy(out=DUM[:, 32:48], in_=IDB[:, 0:16])
                    vector.tensor_copy(out=DUM[:, 48:64], in_=IDB[:, 16:32])
                    vector.tensor_tensor(COLA[:, 3:4], COLA[:, 2:3],
                                         COLA[:, 2:3], ALU.mult)
                    vector.tensor_scalar(COLA[:, 4:5], COLA[:, 1:2], 1.0 / HB,
                                         None, ALU.mult)
                    vector.tensor_copy(out=DUM[:, 32:48], in_=IDB[:, 0:16])
                    vector.tensor_copy(out=DUM[:, 48:64], in_=IDB[:, 16:32])
                    vector.tensor_tensor(COLA[:, 5:6], COLA[:, 4:5],
                                         COLA[:, 3:4], ALU.subtract)
                    vector.tensor_copy(out=DUM[:, 16:17], in_=COLA[:, 5:6]
                                       ).then_inc(sm.o_dv, 1)
                    vector.wait_ge(sm.o_ac, 3 * it2 + 2)
                    vector.reciprocal(COLB[:, 0:1], COLC[:, 0:1])
                    vector.tensor_copy(out=DUM[:, 32:33], in_=COLB[:, 0:1])
                    vector.tensor_copy(out=DUM[:, 48:49], in_=DUM[:, 32:33])
                    if it2 >= 1:
                        vector.wait_ge(sm.o_sy, 16 * it2)
                    vector.tensor_scalar(FB4[:, :], FB2[:, :], COLA[:, 2:3],
                                         COLB[:, 0:1], ALU.subtract, ALU.mult)
                    if it2 >= 2:
                        vector.wait_ge(sm.o_pe, _a4_y1t_cnt(sl, qt - 2)
                                       if qt >= 2 else
                                       _a4_y1t_cnt(sl - 1, qt + 2))
                    vector.tensor_copy(out=BB3[:, (qt % 2) * 2048:
                                               (qt % 2) * 2048 + 2048],
                                       in_=FB4[:, :])
                    vector.tensor_copy(out=DUMW[:, 0:128], in_=IDB[:, 0:128]).then_inc(sm.o_dv, 1)

        @block.sync
        def _(sync):
            sync.dma_start(out=E8S[:, :], in_=e8[:, :]).then_inc(sm.c_ld, 16)
            sync.dma_start(out=GQC[:, :], in_=gqv[:, :]).then_inc(sm.c_ld, 16)
            sync.dma_start(out=GKC[:, :], in_=gkv[:, :]).then_inc(sm.c_ld, 16)
            sync.dma_start(out=IDB[:, :], in_=identin[:, :]).then_inc(sm.c_ld, 16)
            for j, (m, wt, ch) in enumerate(DCH):
                if j >= 2:
                    sync.wait_ge(sm.d_dv, j - 1)
                    sync.wait_ge(sm.d_pe, j - 1)
                buf = CSD[:, (j % 2) * 2048:(j % 2) * 2048 + 2048]
                sc = (SCL0 if j % 2 == 0 else SCL1)[:, :]
                dl = DL[j % 2]
                if m == "g":
                    sync.dma_start(out=buf, in_=gsh[:, ch * 2048:(ch + 1) * 2048]
                                   ).then_inc(dl, 16)
                    sync.dma_start(out=sc, in_=gsc[:, ch * 2048:(ch + 1) * 2048]
                                   ).then_inc(dl, 16)
                elif m == "u":
                    sync.dma_start(out=buf, in_=ush[:, ch * 2048:(ch + 1) * 2048]
                                   ).then_inc(dl, 16)
                    sync.dma_start(out=sc, in_=usc[:, ch * 2048:(ch + 1) * 2048]
                                   ).then_inc(dl, 16)
                else:
                    sync.dma_start(out=buf, in_=dsh[wt * 128:(wt + 1) * 128, :]
                                   ).then_inc(dl, 16)
                    sync.dma_start(out=sc, in_=dsc[wt * 8:(wt + 1) * 8, :]
                                   ).then_inc(dl, 16)
            def emit_xq(sl, qt):
                it2 = sl * 4 + qt
                if it2 >= 1:
                    sync.wait_ge(sm.o_dv, 3 * it2 - 2)
                sync.dma_start(
                    out=FB1[:, :],
                    in_=xq[(sl * 4 + qt) * 128:(sl * 4 + qt + 1) * 128, :]
                ).then_inc(sm.o_lx, 16)

            def emit_a4_sync(sl):
                emit_xq(sl, 0)
                for pr in range(2):
                    for ks in range(16):
                        it = sl * 32 + pr * 16 + ks
                        if it == 0:
                            sync.wait_ge(WS[0], 16 * 8)
                            sync.wait_ge(WS[1], 16 * 8)
                        if it >= 2:
                            it_prev = it - 2
                            slp, rem = divmod(it_prev, 32)
                            prp, ksp = divmod(rem, 16)
                            sync.wait_ge(sm.o_pe, _a4_blk_cnt(slp, prp, ksp))
                        sync.wait_ge(sm.a_df, 16 * (sl + 1))
                        sync.dma_start(
                            out=BB1[:, (it % 2) * 2048:(it % 2) * 2048 + 2048],
                            in_=wo_bf[ks * 128:(ks + 1) * 128, :]
                        ).then_inc(OL[it % 2], 16)
                    emit_xq(sl, 1 + pr)
                emit_xq(sl, 3)

            for u in A3S:
                i, sl, h = u["i"], u["sl"], u["h"]
                if h == 0:
                    if sl > 0:
                        emit_a4_sync(sl - 1)
                        sync.wait_ge(sm.a_pe, i)
                    sync.dma_start(out=M1S[:, :],
                                   in_=m1in[sl * 128:(sl + 1) * 128, :]
                                   ).then_inc(sm.a_lm, 16)
                    sync.dma_start(out=M2S[:, :],
                                   in_=m2in[sl * 128:(sl + 1) * 128, :]
                                   ).then_inc(sm.a_lm, 16)
                if i == 0:
                    sync.wait_ge(sm.q_st, 16 * 16)
                if i >= 2:
                    sync.wait_ge(sm.a_pe, i - 1)
                sync.dma_start(
                    out=QTS[:, (i % 2) * 512:(i % 2) * 512 + 512],
                    in_=qt_dram[h * 128:(h + 1) * 128, sl * 512:(sl + 1) * 512]
                ).then_inc(AL[i % 2], 16)
            emit_a4_sync(3)
            sync.wait_ge(sm.o_sy, 16 * 16)
            sync.wait_ge(sm.o_sz, 16 * 16)
            for hc in range(16):
                sync.dma_start(out=Y1TS[:, hc * 2048:(hc + 1) * 2048],
                               in_=y1t_dram[hc * 128:(hc + 1) * 128, :]
                               ).then_inc(sm.f_ly, 16)
            for s in range(16):
                for sub in range(8):
                    if sub == 0:
                        if s == 0:
                            sync.wait_ge(sm.d_cc, 3)
                        else:
                            sync.wait_ge(sm.f_pe2, 16 * s)
                    sync.dma_start(
                        out=WD8[:, sub * 2048:(sub + 1) * 2048],
                        in_=gd[(s * 8 + sub) * 128:(s * 8 + sub + 1) * 128, :]
                    ).then_inc(sm.f_lw, 16)
                for sub in range(8):
                    for mi, gap in enumerate((gg, gu)):
                        gidx = s * 8 + sub
                        if gidx >= 2:
                            sync.wait_ge(sm.f_pe, 4 * (gidx - 1))
                        slot = (sub % 2) * 2 + mi
                        sync.dma_start(
                            out=GUBF[:, slot * 2048:(slot + 1) * 2048],
                            in_=bass.AP(gap.tensor, gidx * 128,
                                        [[MLP, 128], [128 * MLP, 16], [1, 128]])
                        ).then_inc(FLG[sub % 2], 16)
            sync.wait_ge(FG[0], 16 * 128)
            sync.wait_ge(FG[1], 16 * 128)
            for tg in range(16):
                if tg >= 1:
                    sync.wait_ge(sm.l_dv, 2 * tg)
                sync.dma_start(out=FB1[:, :],
                               in_=part[tg * 128:(tg + 1) * 128, :]
                               ).then_inc(sm.l_ld, 16)
                sync.dma_start(out=FB2[:, :],
                               in_=y1_dram[tg * 128:(tg + 1) * 128, :]
                               ).then_inc(sm.l_ld, 16)
            if DEBUG:
                sync.dma_start(out=y1d_out[:, :], in_=y1_dram[:, :]
                               ).then_inc(sm.c_ld, 16)
                sync.dma_start(out=qt_out[:, :], in_=qt_dram[:, :]
                               ).then_inc(sm.c_ld, 16)
                sync.dma_start(out=gg_out[:, :], in_=gg[:, :]
                               ).then_inc(sm.c_ld, 16)
                sync.dma_start(out=gd_out[:, :], in_=gd[:, :]
                               ).then_inc(sm.c_ld, 16)
                sync.dma_start(out=pt_out[:, :], in_=part[:, :]
                               ).then_inc(sm.c_ld, 16)
                sync.wait_ge(sm.c_ld, 64 + 10 * 16)

        @block.gpsimd
        def _(gpsimd):
            gpsimd.wait_ge(sm.d_dv, NDC)
            for i in range(16):
                gpsimd.dma_start(out=WQS[:, i * 2048:(i + 1) * 2048],
                                 in_=wq[i * 128:(i + 1) * 128, :]
                                 ).then_inc(sm.q_gw, 16)
            for qt in range(16):
                if qt >= 2:
                    gpsimd.wait_ge(sm.q_pe2, qt - 1)
                gpsimd.dma_start(out=BB1[:, (qt % 2) * 2048:(qt % 2) * 2048 + 2048],
                                 in_=xq[qt * 128:(qt + 1) * 128, :]
                                 ).then_inc(QG[qt % 2], 16)
            for i in range(16):
                if i >= 2:
                    gpsimd.wait_ge(WS[i % 2], 16 * ((i - 2) // 2 + 1))
                gpsimd.dma_start(out=P32B[:, (i % 2) * 2048:(i % 2) * 2048 + 2048],
                                 in_=wo[i * 128:(i + 1) * 128, :]
                                 ).then_inc(WG[i % 2], 16)
            gpsimd.wait_ge(sm.q_pe, 32)
            gpsimd.dma_start(
                out=WKS[:, :].rearrange("p (t d) -> p t d", t=16),
                in_=bass.AP(wkv.tensor, 0, [[512, 128], [128 * 512, 16], [1, 512]])
            ).then_inc(sm.k_gw, 16)
            gpsimd.dma_start(
                out=WVS[:, :].rearrange("p (t d) -> p t d", t=16),
                in_=bass.AP(wkv.tensor, HB * 512,
                            [[512, 128], [128 * 512, 16], [1, 512]])
            ).then_inc(sm.k_gw, 16)
            for i in range(32):
                if i >= 2:
                    gpsimd.wait_ge(sm.k_pe2, i - 1)
                gpsimd.dma_start(out=BB1[:, (i % 2) * 2048:(i % 2) * 2048 + 2048],
                                 in_=xb[i * 128:(i + 1) * 128, :]
                                 ).then_inc(KG[i % 2], 16)
            for gi, (piece, gath) in enumerate(((pg, gg), (pu, gu), (pd, gd))):
                gpsimd.wait_ge(sm.d_sc, 16 * 16 * (gi + 1))
                gpsimd.collective_compute(
                    "AllGather", ALU.bypass,
                    replica_groups=[[0, 1, 2, 3, 4, 5, 6, 7]],
                    ins=[piece.opt()], outs=[gath.opt()],
                ).then_inc(sm.d_cc)
            for s in range(16):
                for tg in range(16):
                    it = s * 16 + tg
                    gpsimd.wait_ge(sm.f_ac2, it + 1)
                    if s >= 1:
                        prev = (s - 1) * 16 + tg
                        gpsimd.wait_ge(FG[tg % 2], 16 * (prev // 2 + 1))
                    stg = FB4 if tg % 2 == 0 else FB1
                    gpsimd.dma_start(
                        out=part[tg * 128:(tg + 1) * 128, :],
                        in_=stg[:, :],
                        accum_op=(ALU.bypass if s == 0 else ALU.add),
                    ).then_inc(FG[tg % 2], 16)
            gpsimd.wait_ge(FG[0], 16 * 128)
            gpsimd.wait_ge(FG[1], 16 * 128)

        @block.tensor
        def _(tensor):
            tensor.wait_ge(sm.c_ld, 64)
            for j in range(NDC):
                tensor.wait_ge(DL[j % 2], 32 * (j // 2 + 1))
                if j >= 1:
                    tensor.wait_ge(sm.d_ac, j)
                scl = SCL0 if j % 2 == 0 else SCL1
                for q4 in range(4):
                    op = tensor.matmul(PSA[:, q4 * 512:(q4 + 1) * 512],
                                       E8S[:, :],
                                       scl[0:8, q4 * 512:(q4 + 1) * 512],
                                       start=True, stop=True)
                op.then_inc(sm.d_pe, 1)
            tensor.wait_ge(sm.c_ld, 64)
            tensor.wait_ge(sm.q_gw, 16 * 16)
            for qt in range(16):
                tensor.wait_ge(QG[qt % 2], 16 * (qt // 2 + 1))
                if qt >= 1:
                    tensor.wait_ge(sm.q_ac, 4 * (qt - 1) + 2)
                xin = BB1[:, (qt % 2) * 2048:(qt % 2) * 2048 + 2048]
                for kc in range(16):
                    op = tensor.transpose(PSBb[:, kc * 128:(kc + 1) * 128],
                                          xin[:, kc * 128:(kc + 1) * 128],
                                          IDB[:, :])
                op.then_inc(sm.q_pe2, 1)
                tensor.wait_ge(sm.q_ac, 4 * qt + 1)
                xt = BB2[:, (qt % 2) * 2048:(qt % 2) * 2048 + 2048]
                for dqc in range(4):
                    for kc in range(16):
                        op = tensor.matmul(PSA[:, dqc * 512:(dqc + 1) * 512],
                                           xt[:, kc * 128:(kc + 1) * 128],
                                           WQS[:, kc * 2048 + dqc * 512:
                                               kc * 2048 + (dqc + 1) * 512],
                                           start=(kc == 0), stop=(kc == 15))
                op.then_inc(sm.q_pe, 1)
                tensor.wait_ge(sm.q_dv, 4 * qt + 3)
                for hh in range(16):
                    op = tensor.transpose(
                        PSBb[:, 2048 + hh * 128:2048 + (hh + 1) * 128],
                        xt[:, hh * 128:(hh + 1) * 128], IDB[:, :])
                op.then_inc(sm.q_pe, 1)
            tensor.wait_ge(sm.k_gw, 32)
            for tt in range(32):
                tensor.wait_ge(KG[tt % 2], 16 * (tt // 2 + 1))
                if tt == 0:
                    tensor.wait_ge(sm.q_ac, 64)
                if tt >= 1:
                    tensor.wait_ge(sm.k_ac, 5 * (tt - 1) + 1)
                xin = BB1[:, (tt % 2) * 2048:(tt % 2) * 2048 + 2048]
                for kc in range(16):
                    op = tensor.transpose(PSBb[:, kc * 128:(kc + 1) * 128],
                                          xin[:, kc * 128:(kc + 1) * 128],
                                          IDB[:, :])
                op.then_inc(sm.k_pe2, 1)
                tensor.wait_ge(sm.k_ac, 5 * tt + 1)
                xt = BB2[:, (tt % 2) * 2048:(tt % 2) * 2048 + 2048]
                for kc in range(16):
                    tensor.matmul(PSA[:, 0:512], xt[:, kc * 128:(kc + 1) * 128],
                                  WKS[:, kc * 512:(kc + 1) * 512],
                                  start=(kc == 0), stop=(kc == 15))
                for kc in range(16):
                    op = tensor.matmul(PSA[:, 512:1024],
                                       xt[:, kc * 128:(kc + 1) * 128],
                                       WVS[:, kc * 512:(kc + 1) * 512],
                                       start=(kc == 0), stop=(kc == 15))
                op.then_inc(sm.k_pe, 1)
                tensor.wait_ge(sm.k_dv, 4 * tt + 3)
                for sgi in range(4):
                    op = tensor.transpose(
                        PSBb[:, 2048 + sgi * 128:2048 + (sgi + 1) * 128],
                        xt[:, sgi * 128:(sgi + 1) * 128], IDB[:, :])
                op.then_inc(sm.k_pe, 1)
            tensor.wait_ge(sm.k_dv, 4 * 32)
            tensor.wait_ge(sm.k_ac, 5 * 32)
            for u in A3S:
                i, sl, h = u["i"], u["sl"], u["h"]
                kvh = h // 4
                qtl = QTS[:, (i % 2) * 512:(i % 2) * 512 + 512]
                if h == 0 and sl > 0:
                    emit_a4_tensor(tensor, sl - 1)
                    tensor.wait_ge(sm.o_ac, 12 * sl)
                tensor.wait_ge(sm.a_lm, 32 * (sl + 1))
                tensor.wait_ge(AL[i % 2], 16 * (i // 2 + 1))
                n1 = u["n1_0"]
                for qt in range(4):
                    for k5 in range(KT512[sl]):
                        if n1 >= 2:
                            tensor.wait_ge(sm.a_d1, n1 - 1)
                        tensor.matmul(PSA[:, (n1 % 2) * 512:(n1 % 2) * 512 + 512],
                                      qtl[:, qt * 128:(qt + 1) * 128],
                                      KTS[:, kvh * 4096 + k5 * 512:
                                          kvh * 4096 + (k5 + 1) * 512],
                                      start=True, stop=True
                                      ).then_inc(sm.a_p1, 1)
                        n1 += 1
                tensor.wait_ge(sm.a_dm, i + 1)
                if i >= 1:
                    tensor.wait_ge(sm.a_df, i)
                for qt in range(4):
                    op = tensor.matmul(PSB[0:1, 1024 + qt * 128:
                                           1024 + (qt + 1) * 128],
                                       MCB[:, qt:qt + 1], IDB[:, :],
                                       start=(qt == 0), stop=(qt == 3))
                op.then_inc(sm.a_pr, 1)
                tensor.wait_ge(sm.a_dn, i + 1)
                n2 = u["n2_0"]
                for kt in range(KT128[sl]):
                    if n2 >= 2:
                        tensor.wait_ge(sm.a_ae, n2 - 1)
                    bank = PSA[:, 1024 + (n2 % 2) * 512:
                               1024 + (n2 % 2) * 512 + 512]
                    tensor.matmul(bank,
                                  KTS[:, kvh * 4096 + kt * 128:
                                      kvh * 4096 + (kt + 1) * 128],
                                  qtl[:, :], start=True, stop=False)
                    tensor.matmul(bank, ONES1[:, :], NEGM[0:1, :],
                                  start=False, stop=True
                                  ).then_inc(sm.a_p2, 1)
                    tensor.wait_ge(sm.a_ae, n2 + 1)
                    pt = PTB[:, (n2 % 2) * 512:(n2 % 2) * 512 + 512]
                    last = (kt == KT128[sl] - 1)
                    for qt in range(4):
                        oacc = PSB[:, qt * 512:qt * 512 + 512]
                        tensor.matmul(oacc[:, 0:128],
                                      pt[:, qt * 128:(qt + 1) * 128],
                                      VS[:, kt * 512 + kvh * 128:
                                         kt * 512 + (kvh + 1) * 128],
                                      start=(kt == 0), stop=False)
                        op = tensor.matmul(oacc[:, 128:129],
                                           pt[:, qt * 128:(qt + 1) * 128],
                                           ONESC[:, :], start=False, stop=last)
                    n2 += 1
                op.then_inc(sm.a_pe, 1)
            emit_a4_tensor(tensor, 3)
            tensor.wait_ge(sm.o_ac, 48)
            tensor.wait_ge(sm.f_ly, 16 * 16)
            for s in range(16):
                if s >= 1:
                    tensor.wait_ge(sm.f_ac2, s * 16 - 1)
                for sub in range(8):
                    tensor.wait_ge(FLG[sub % 2], 32 * (s * 4 + sub // 2 + 1))
                    gslot = GUBF[:, ((sub % 2) * 2) * 2048:
                                 ((sub % 2) * 2) * 2048 + 2048]
                    uslot = GUBF[:, ((sub % 2) * 2 + 1) * 2048:
                                 ((sub % 2) * 2 + 1) * 2048 + 2048]
                    for tq in range(4):
                        k = (s * 8 + sub) * 4 + tq
                        if k >= 2:
                            tensor.wait_ge(sm.f_dv, k - 1)
                        gb = PSA[:, (tq % 2) * 512:(tq % 2) * 512 + 512]
                        ub = PSA[:, 1024 + (tq % 2) * 512:
                                 1024 + (tq % 2) * 512 + 512]
                        for hs in range(16):
                            tensor.matmul(gb, gslot[:, hs * 128:(hs + 1) * 128],
                                          Y1TS[:, hs * 2048 + tq * 512:
                                               hs * 2048 + (tq + 1) * 512],
                                          start=(hs == 0), stop=(hs == 15))
                        for hs in range(16):
                            op = tensor.matmul(ub,
                                               uslot[:, hs * 128:(hs + 1) * 128],
                                               Y1TS[:, hs * 2048 + tq * 512:
                                                    hs * 2048 + (tq + 1) * 512],
                                               start=(hs == 0), stop=(hs == 15))
                        op.then_inc(sm.f_pe, 1)
                tensor.wait_ge(sm.f_lw, 16 * 8 * (s + 1))
                for tg in range(16):
                    it = s * 16 + tg
                    accb = PSA if tg % 2 == 0 else PSB
                    tensor.wait_ge(sm.f_dv, 32 * s + 29 + tg // 4)
                    if tg == 0:
                        tensor.wait_ge(sm.f_ac, 64 * (s + 1))
                    if it >= 2:
                        tensor.wait_ge(sm.f_ac2, it - 1)
                    for sub in range(8):
                        for hc in range(4):
                            op = tensor.matmul(accb[:, hc * 512:(hc + 1) * 512],
                                               INTERT[:, sub * 2048 + tg * 128:
                                                      sub * 2048 + (tg + 1) * 128],
                                               WD8[:, sub * 2048 + hc * 512:
                                                   sub * 2048 + (hc + 1) * 512],
                                               start=(sub == 0), stop=(sub == 7))
                    op.then_inc(sm.f_pe2, 1)

        @block.scalar
        def _(scalar):
            def d_piece(jj):
                m, wt, ch = DCH[jj]
                scalar.wait_ge(sm.d_dv, jj + 1)
                if m in ("g", "u"):
                    dst = pg if m == "g" else pu
                    o_lo = bass.AP(dst.tensor, ch * 2048,
                                   [[2 * MLP, 128], [1, 2048]])
                    o_hi = bass.AP(dst.tensor, MLP + ch * 2048,
                                   [[2 * MLP, 128], [1, 2048]])
                else:
                    o_lo = bass.AP(pd.tensor, (2 * wt * 128) * HB,
                                   [[2 * HB, 128], [1, 2048]])
                    o_hi = bass.AP(pd.tensor, (2 * wt * 128 + 1) * HB,
                                   [[2 * HB, 128], [1, 2048]])
                scalar.dma_start(out=o_lo, in_=DLO[:, :]).then_inc(sm.d_sc, 16)
                scalar.dma_start(out=o_hi, in_=DHI[:, :]).then_inc(sm.d_sc, 16)

            for j in range(NDC):
                scalar.wait_ge(sm.d_pe, j + 1)
                scalar.activation(DNSX[:, (j % 2) * 2048:(j % 2) * 2048 + 2048],
                                  PSA[:, :], AF.Identity).then_inc(sm.d_ac, 1)
                if j >= 1:
                    d_piece(j - 1)
            d_piece(NDC - 1)
            for qt in range(16):
                scalar.wait_ge(sm.q_pe2, qt + 1)
                if qt >= 2:
                    scalar.wait_ge(sm.q_pe, 2 * (qt - 2) + 2)
                scalar.activation(BB2[:, (qt % 2) * 2048:(qt % 2) * 2048 + 2048],
                                  PSBb[:, 0:2048], AF.Identity)
                scalar.activation(DUMW[:, 128:256], IDB[:, 0:128], AF.Identity).then_inc(sm.q_ac, 1)
                scalar.wait_ge(sm.q_pe, 2 * qt + 1)
                if qt >= 1:
                    scalar.wait_ge(sm.q_dv, 4 * (qt - 1) + 4)
                scalar.activation(FB3[:, :], PSA[:, :], AF.Identity)
                scalar.activation(DUMW[:, 128:256], IDB[:, 0:128], AF.Identity).then_inc(sm.q_ac, 1)
                scalar.wait_ge(sm.q_dv, 4 * qt + 1)
                scalar.activation(COLC[:, 0:16], COLA[:, 0:16], AF.Sqrt,
                                  bias=EPSC[:, 0:1], scale=1.0 / D)
                scalar.activation(DUM[:, 0:16], COLC[:, 0:16], AF.Sqrt
                                  ).then_inc(sm.q_ac, 1)
                scalar.wait_ge(sm.q_pe, 2 * qt + 2)
                scalar.wait_ge(sm.q_dv, 4 * qt + 3)
                scalar.activation(FB3[:, :].bitcast(BF16)[:, 0:2048],
                                  PSBb[:, 2048:4096], AF.Identity)
                scalar.activation(DUMW[:, 128:256], IDB[:, 0:128], AF.Identity).then_inc(sm.q_ac, 1)
                scalar.wait_ge(sm.q_dv, 4 * qt + 4)
                if DEBUG and qt == 0:
                    scalar.dma_start(out=qf_dbg[:, :], in_=FB3[:, :]
                                     ).then_inc(sm.c_ld, 16)
                    scalar.dma_start(out=col_dbg[:, 0:16], in_=COLA[:, :]
                                     ).then_inc(sm.c_ld, 16)
                    scalar.dma_start(out=col_dbg[:, 16:32], in_=COLB[:, :]
                                     ).then_inc(sm.c_ld, 16)
                    scalar.dma_start(out=col_dbg[:, 32:48], in_=COLC[:, :]
                                     ).then_inc(sm.c_ld, 16)
                    scalar.dma_start(out=qn_dbg[:, :], in_=BB3[:, 0:2048]
                                     ).then_inc(sm.c_ld, 16)
                    scalar.dma_start(out=qn2_dbg[:, :], in_=BB2[:, 0:2048]
                                     ).then_inc(sm.c_ld, 16)
                    scalar.dma_start(out=qe_dbg[:, :],
                                     in_=FB3[:, :].bitcast(BF16)[:, 0:2048]
                                     ).then_inc(sm.c_ld, 16)
                    scalar.wait_ge(sm.c_ld, 64 + 7 * 16)
                scalar.dma_start(
                    out=bass.AP(qt_dram.tensor, qt * 128,
                                [[TL, 128], [128 * TL, 16], [1, 128]]),
                    in_=BB3[:, 0:2048].rearrange("p (h j) -> p h j", h=16)
                ).then_inc(sm.q_st, 16)
            for i in range(16):
                scalar.wait_ge(WG[i % 2], 16 * (i // 2 + 1))
                scalar.dma_start(out=wo_bf[i * 128:(i + 1) * 128, :],
                                 in_=P32B[:, (i % 2) * 2048:(i % 2) * 2048 + 2048]
                                 ).then_inc(WS[i % 2], 16)
            for tt in range(32):
                scalar.wait_ge(sm.k_pe2, tt + 1)
                if tt >= 2:
                    scalar.wait_ge(sm.k_pe, 2 * (tt - 2) + 2)
                scalar.activation(BB2[:, (tt % 2) * 2048:(tt % 2) * 2048 + 2048],
                                  PSBb[:, 0:2048], AF.Identity)
                scalar.activation(DUMW[:, 128:256], IDB[:, 0:128], AF.Identity).then_inc(sm.k_ac, 1)
                scalar.wait_ge(sm.k_pe, 2 * tt + 1)
                if tt >= 1:
                    scalar.wait_ge(sm.k_dv, 4 * (tt - 1) + 1)
                scalar.activation(FB3[:, 0:512], PSA[:, 0:512], AF.Identity)
                scalar.activation(DUMW[:, 128:256], IDB[:, 0:128], AF.Identity).then_inc(sm.k_ac, 1)
                scalar.activation(VS[:, tt * 512:(tt + 1) * 512], PSA[:, 512:1024],
                                  AF.Identity)
                scalar.activation(DUMW[:, 128:256], IDB[:, 0:128], AF.Identity).then_inc(sm.k_ac, 1)
                scalar.wait_ge(sm.k_dv, 4 * tt + 1)
                scalar.activation(COLC[:, 0:4], COLA[:, 0:4], AF.Sqrt,
                                  bias=EPSC[:, 0:1], scale=1.0 / D)
                scalar.activation(DUM[:, 0:4], COLC[:, 0:4], AF.Sqrt
                                  ).then_inc(sm.k_ac, 1)
                scalar.wait_ge(sm.k_pe, 2 * tt + 2)
                scalar.wait_ge(sm.k_dv, 4 * tt + 3)
                scalar.activation(FB4[:, 0:512].bitcast(BF16)[:, 0:512],
                                  PSBb[:, 2048:2560], AF.Identity)
                scalar.activation(DUMW[:, 128:256], IDB[:, 0:128], AF.Identity).then_inc(sm.k_ac, 1)
            for u in A3S:
                i, sl, h = u["i"], u["sl"], u["h"]
                if h == 0 and sl > 0:
                    emit_a4_scalar(scalar, sl - 1)
                scalar.wait_ge(sm.a_pr, i + 1)
                scalar.activation(NEGF[0:1, 0:512], PSB[0:1, 1024:1536],
                                  AF.Identity)
                scalar.activation(DUMW[:, 128:256], IDB[:, 0:128], AF.Identity).then_inc(sm.a_an, 1)
                n2 = u["n2_0"]
                for kt in range(KT128[sl]):
                    scalar.wait_ge(sm.a_p2, n2 + 1)
                    if kt >= KT128[sl] - 8:
                        jj = kt - (KT128[sl] - 8)
                        cnt = 8 * i + jj + 1
                        scalar.wait_ge(sm.a_d2, cnt)
                        scalar.activation(
                            PTB[:, (n2 % 2) * 512:(n2 % 2) * 512 + 512],
                            PSA[:, 1024 + (n2 % 2) * 512:
                                1024 + (n2 % 2) * 512 + 512],
                            AF.Exp).then_inc(sm.a_ae, 1)
                    else:
                        scalar.activation(
                            PTB[:, (n2 % 2) * 512:(n2 % 2) * 512 + 512],
                            PSA[:, 1024 + (n2 % 2) * 512:
                                1024 + (n2 % 2) * 512 + 512],
                            AF.Exp).then_inc(sm.a_ae, 1)
                    n2 += 1
                scalar.wait_ge(sm.a_pe, i + 1)
                if i >= 1:
                    scalar.wait_ge(sm.a_df, i)
                for qt in range(4):
                    scalar.activation(OFN[:, qt * 160:qt * 160 + 129],
                                      PSB[:, qt * 512:qt * 512 + 129],
                                      AF.Identity)
                scalar.activation(DUMW[:, 128:256], IDB[:, 0:128], AF.Identity).then_inc(sm.a_ao, 1)
                if DEBUG and i == 0:
                    scalar.wait_ge(sm.a_dm, 1)
                    scalar.wait_ge(sm.a_dn, 1)
                    scalar.wait_ge(sm.a_df, 1)
                    scalar.dma_start(out=ofn_dbg[:, :], in_=OFN[:, 0:640]
                                     ).then_inc(sm.c_ld, 16)
                    scalar.dma_start(out=mcb_dbg[:, 0:4], in_=MCOL[:, :]
                                     ).then_inc(sm.c_ld, 16)
                    scalar.dma_start(out=negm_dbg[:, :], in_=NEGM[0:1, :]
                                     ).then_inc(sm.c_ld, 16)
                    for dq in range(4):
                        scalar.dma_start(
                            out=os_dbg[:, dq * 128:(dq + 1) * 128],
                            in_=OS[:, dq * 2048:dq * 2048 + 128]
                        ).then_inc(sm.c_ld, 16)
                    scalar.wait_ge(sm.c_ld, 64 + 14 * 16)
            emit_a4_scalar(scalar, 3)
            for s in range(16):
                for sub in range(8):
                    for tq in range(4):
                        k = (s * 8 + sub) * 4 + tq
                        scalar.wait_ge(sm.f_pe, k + 1)
                        if k >= 1:
                            scalar.wait_ge(sm.f_dv, k)
                        gb = PSA[:, (tq % 2) * 512:(tq % 2) * 512 + 512]
                        ub = PSA[:, 1024 + (tq % 2) * 512:
                                 1024 + (tq % 2) * 512 + 512]
                        scalar.activation(FB3[:, 0:512], gb,
                                          AF.Identity).then_inc(sm.f_ac, 1)
                        scalar.activation(FB3[:, 512:1024], ub,
                                          AF.Identity).then_inc(sm.f_ac, 1)
                        scalar.wait_ge(sm.f_tg, k + 1)
                        scalar.activation(FB2[:, 512:1024], FB2[:, 0:512],
                                          AF.Tanh).then_inc(sm.f_th, 1)
                for tg in range(16):
                    it = s * 16 + tg
                    scalar.wait_ge(sm.f_pe2, it + 1)
                    accb = PSA if tg % 2 == 0 else PSB
                    stg = FB4 if tg % 2 == 0 else FB1
                    if it >= 2:
                        scalar.wait_ge(FG[it % 2], 16 * (it // 2))
                    scalar.activation(stg[:, :], accb[:, :],
                                      AF.Identity).then_inc(sm.f_ac2, 1)
            for tg in range(16):
                scalar.wait_ge(sm.l_dv, 2 * tg + 1)
                scalar.activation(COLC[:, 0:1], COLA[:, 5:6], AF.Sqrt,
                                  bias=EPSC[:, 1:2], scale=1.0)
                scalar.activation(DUM[:, 0:1], COLC[:, 0:1], AF.Sqrt
                                  ).then_inc(sm.l_ac, 1)
                scalar.wait_ge(sm.l_dv, 2 * tg + 2)
                scalar.dma_start(out=y2out[tg * 128:(tg + 1) * 128, :],
                                 in_=FB3[:, :]).then_inc(sm.l_st, 16)
            scalar.wait_ge(sm.l_st, 16 * 16)
            if DEBUG:
                scalar.dma_start(out=it_dbg[:, :], in_=INTERT[:, :]
                                 ).then_inc(sm.c_ld, 16)
                scalar.dma_start(out=y1t_dbg[:, :], in_=P64A[:, :]
                                 ).then_inc(sm.c_ld, 16)
                scalar.dma_start(out=y1td_dbg[:, :], in_=y1t_dram[:, :]
                                 ).then_inc(sm.c_ld, 16)
                scalar.wait_ge(sm.c_ld, 64 + 19 * 16)

        @block.vector
        def _(vector):
            vector.memset(ONES1[:, :], 1.0)
            vector.memset(ONESC[:, :], 1.0)
            vector.memset(EPSC[:, 0:1], EPS_RMS)
            vector.memset(EPSC[:, 1:2], EPS_LN)
            vector.wait_ge(sm.c_ld, 64)
            vector.tensor_scalar(GQS[:, :], GQC[:, :], QK_SCALE, None, ALU.mult)
            for j in range(NDC):
                vector.wait_ge(DL[j % 2], 32 * (j // 2 + 1))
                vector.wait_ge(sm.d_ac, j + 1)
                if j >= 1:
                    vector.wait_ge(sm.d_sc, 32 * j)
                cb = CSD[:, (j % 2) * 2048:(j % 2) * 2048 + 2048]
                nx = DNSX[:, (j % 2) * 2048:(j % 2) * 2048 + 2048]
                vector.tensor_scalar(DI1[:, :], cb, 15, None, ALU.bitwise_and)
                vector.tensor_scalar(DI2[:, :], cb, 4, None,
                                     ALU.logical_shift_right)
                vector.tensor_scalar(DI3[:, :], cb, 3, 1,
                                     ALU.logical_shift_right, ALU.bitwise_and)
                vector.tensor_scalar(DI4[:, :], cb, 7, 1,
                                     ALU.logical_shift_right, ALU.bitwise_and)
                for half, (xx, ss, dst) in enumerate(
                        ((DI1, DI3, DLO), (DI2, DI4, DHI))):
                    sl2 = slice(half * 2048, half * 2048 + 2048)
                    vector.tensor_scalar(DT1[:, :], xx[:, :], 64.0, 16128.0,
                                         ALU.mult, ALU.add)
                    vector.scalar_tensor_tensor(DBITS[:, sl2], ss[:, :],
                                                -33280.0, DT1[:, :],
                                                ALU.mult, ALU.add)
                    uu = DBITS[:, sl2].bitcast(BF16)
                    vector.tensor_scalar(DC1[:, :], uu, 1.0, -1.0,
                                         ALU.min, ALU.max)
                    vector.tensor_scalar(DC2[:, :], uu, 2.0, 1.0,
                                         ALU.mult, ALU.min)
                    vector.tensor_tensor(DT2[:, :], uu, DC1[:, :], ALU.add)
                    vector.scalar_tensor_tensor(DT3[:, :], DC2[:, :], -1.0,
                                                DT2[:, :], ALU.max,
                                                ALU.subtract)
                    if half == 0:
                        vector.tensor_tensor(dst[:, :], DT3[:, :], nx, ALU.mult)
                    else:
                        vector.tensor_tensor(dst[:, :], DT3[:, :], nx, ALU.mult
                                             ).then_inc(sm.d_dv, 1)
            for qt in range(16):
                vector.wait_ge(sm.q_ac, 4 * qt + 2)
                for hh in range(16):
                    op = vector.scalar_tensor_tensor(
                        FB2[:, hh * 128:(hh + 1) * 128],
                        FB3[:, hh * 128:(hh + 1) * 128], 0.0,
                        FB3[:, hh * 128:(hh + 1) * 128],
                        ALU.add, ALU.mult, accum_out=COLA[:, hh:hh + 1])
                vector.tensor_copy(out=DUM[:, 16:32], in_=COLA[:, 0:16]
                                   ).then_inc(sm.q_dv, 1)
                vector.wait_ge(sm.q_ac, 4 * qt + 3)
                vector.reciprocal(COLB[:, 0:16], COLC[:, 0:16]).then_inc(sm.q_dv, 1)
                vector.tensor_copy(out=DUM[:, 32:48], in_=COLB[:, 0:16])
                vector.tensor_copy(out=DUM[:, 48:64], in_=DUM[:, 32:48])
                xt = BB2[:, (qt % 2) * 2048:(qt % 2) * 2048 + 2048]
                for hh in range(16):
                    vector.tensor_scalar(xt[:, hh * 128:(hh + 1) * 128],
                                         FB3[:, hh * 128:(hh + 1) * 128],
                                         COLB[:, hh:hh + 1], None, ALU.mult)
                vector.tensor_copy(out=DUMW[:, 0:128], in_=IDB[:, 0:128]).then_inc(sm.q_dv, 1)
                vector.wait_ge(sm.q_ac, 4 * qt + 4)
                if qt >= 1:
                    vector.wait_ge(sm.q_st, 16 * qt)
                vector.tensor_scalar(BB3[:, 0:2048],
                                     FB3[:, :].bitcast(BF16)[:, 0:2048],
                                     GQS[:, 0:1], None, ALU.mult)
                vector.tensor_copy(out=DUMW[:, 0:128], in_=IDB[:, 0:128]).then_inc(sm.q_dv, 1)
            vector.wait_ge(sm.d_sc, 16 * 48)
            for tt in range(32):
                vector.wait_ge(sm.k_ac, 5 * tt + 2)
                for sgi in range(4):
                    op = vector.scalar_tensor_tensor(
                        FB2[:, sgi * 128:(sgi + 1) * 128],
                        FB3[:, sgi * 128:(sgi + 1) * 128], 0.0,
                        FB3[:, sgi * 128:(sgi + 1) * 128],
                        ALU.add, ALU.mult, accum_out=COLA[:, sgi:sgi + 1])
                vector.tensor_copy(out=DUM[:, 16:20], in_=COLA[:, 0:4]
                                   ).then_inc(sm.k_dv, 1)
                vector.wait_ge(sm.k_ac, 5 * tt + 4)
                vector.reciprocal(COLB[:, 0:4], COLC[:, 0:4]).then_inc(sm.k_dv, 1)
                vector.tensor_copy(out=DUM[:, 32:36], in_=COLB[:, 0:4])
                vector.tensor_copy(out=DUM[:, 48:52], in_=DUM[:, 32:36])
                xt = BB2[:, (tt % 2) * 2048:(tt % 2) * 2048 + 2048]
                for sgi in range(4):
                    vector.tensor_scalar(xt[:, sgi * 128:(sgi + 1) * 128],
                                         FB3[:, sgi * 128:(sgi + 1) * 128],
                                         COLB[:, sgi:sgi + 1], None,
                                         ALU.mult)
                vector.tensor_copy(out=DUMW[:, 0:128], in_=IDB[:, 0:128]).then_inc(sm.k_dv, 1)
                vector.wait_ge(sm.k_ac, 5 * tt + 5)
                for sgi in range(4):
                    vector.tensor_scalar(
                        KTS[:, sgi * 4096 + tt * 128:sgi * 4096 + (tt + 1) * 128],
                        FB4[:, 0:512].bitcast(BF16)[:, sgi * 128:(sgi + 1) * 128],
                        GKC[:, 0:1], None, ALU.mult)
                vector.tensor_copy(out=DUMW[:, 0:128], in_=IDB[:, 0:128]).then_inc(sm.k_dv, 1)
            for u in A3S:
                i, sl, h = u["i"], u["sl"], u["h"]
                if h == 0 and sl > 0:
                    emit_a4_vector(vector, sl - 1)
                n1 = u["n1_0"]
                for qt in range(4):
                    for k5 in range(KT512[sl]):
                        vector.wait_ge(sm.a_p1, n1 + 1)
                        bank = PSA[:, (n1 % 2) * 512:(n1 % 2) * 512 + 512]
                        if k5 >= KT512[sl] - 2:
                            jj = k5 - (KT512[sl] - 2)
                            vector.tensor_tensor(
                                bank, bank,
                                M1S[:, (jj * 4 + qt) * 512:
                                    (jj * 4 + qt + 1) * 512], ALU.add)
                        vector.reduce_max(COLB[:, k5:k5 + 1], bank,
                                          axis=AX.X).then_inc(sm.a_d1, 1)
                        n1 += 1
                    vector.tensor_copy(out=DUM[:, 32:48], in_=IDB[:, 0:16])
                    vector.tensor_copy(out=DUM[:, 48:64], in_=IDB[:, 16:32])
                    vector.reduce_max(MCOL[:, qt:qt + 1],
                                      COLB[:, 0:KT512[sl]], axis=AX.X)
                vector.tensor_copy(out=DUM[:, 32:48], in_=IDB[:, 0:16])
                vector.tensor_copy(out=DUM[:, 48:64], in_=IDB[:, 16:32])
                vector.tensor_copy(out=MCB[:, :], in_=MCOL[:, :])
                vector.tensor_copy(out=DUM[:, 16:20], in_=MCB[:, 0:4])
                vector.tensor_copy(out=DUMW[:, 0:128], in_=IDB[:, 0:128]).then_inc(sm.a_dm, 1)
                vector.wait_ge(sm.a_an, i + 1)
                vector.tensor_scalar(NEGM[0:1, :], NEGF[0:1, 0:512], -1.0,
                                     None, ALU.mult)
                vector.tensor_copy(out=DUMW[:, 0:128], in_=IDB[:, 0:128]).then_inc(sm.a_dn, 1)
                n2 = u["n2_0"]
                for kt in range(KT128[sl]):
                    if kt >= KT128[sl] - 8:
                        jj = kt - (KT128[sl] - 8)
                        cnt = 8 * i + jj + 1
                        vector.wait_ge(sm.a_p2, n2 + 1)
                        bank = PSA[:, 1024 + (n2 % 2) * 512:
                                   1024 + (n2 % 2) * 512 + 512]
                        vector.tensor_tensor(bank, bank,
                                             M2S[:, jj * 512:(jj + 1) * 512],
                                             ALU.add)
                        vector.tensor_copy(out=DUMW[:, 0:128], in_=IDB[:, 0:128]).then_inc(sm.a_d2, 1)
                    n2 += 1
                vector.wait_ge(sm.a_ao, i + 1)
                for qt in range(4):
                    vector.reciprocal(RC[:, qt:qt + 1],
                                      OFN[:, qt * 160 + 128:qt * 160 + 129])
                vector.tensor_copy(out=DUM[:, 32:36], in_=RC[:, 0:4])
                vector.tensor_copy(out=DUM[:, 48:52], in_=DUM[:, 32:36])
                for qt in range(4):
                    vector.tensor_scalar(
                        OS[:, qt * 2048 + h * 128:qt * 2048 + (h + 1) * 128],
                        OFN[:, qt * 160:qt * 160 + 128],
                        RC[:, qt:qt + 1], None, ALU.mult)
                vector.tensor_copy(out=DUMW[:, 0:128], in_=IDB[:, 0:128]).then_inc(sm.a_df, 1)
            emit_a4_vector(vector, 3)
            for s in range(16):
                for sub in range(8):
                    for tq in range(4):
                        k = (s * 8 + sub) * 4 + tq
                        vector.wait_ge(sm.f_ac, 2 * k + 2)
                        if s > 0 and sub == 0 and tq == 0:
                            vector.wait_ge(sm.f_pe2, 16 * s)
                        gf = FB3[:, 0:512]
                        upf = FB3[:, 512:1024]
                        g2 = FB3[:, 1024:1536]
                        u1 = FB3[:, 1536:2048]
                        targ = FB2[:, 0:512]
                        th = FB2[:, 512:1024]
                        vv = FB2[:, 1024:1536]
                        vector.tensor_tensor(g2, gf, gf, ALU.mult)
                        vector.tensor_scalar(u1, g2, G_C1, G_C2,
                                             ALU.mult, ALU.add)
                        vector.tensor_tensor(targ, u1, gf, ALU.mult
                                             ).then_inc(sm.f_tg, 1)
                        vector.wait_ge(sm.f_th, k + 1)
                        vector.scalar_tensor_tensor(vv, th, 1.0, gf,
                                                    ALU.add, ALU.mult)
                        op = vector.scalar_tensor_tensor(
                            INTERT[:, sub * 2048 + tq * 512:
                                   sub * 2048 + (tq + 1) * 512],
                            vv, 0.5, upf, ALU.mult, ALU.mult)
                        if sub == 7 and tq == 3:
                            op = vector.tensor_copy(out=DUMW[:, 0:128], in_=IDB[:, 0:128])
                        op.then_inc(sm.f_dv, 1)
            for tg in range(16):
                vector.wait_ge(sm.l_ld, 32 * (tg + 1))
                if tg >= 1:
                    vector.wait_ge(sm.l_st, 16 * tg)
                vector.scalar_tensor_tensor(FB4[:, :], FB1[:, :], 0.0,
                                            FB2[:, :], ALU.add, ALU.add,
                                            accum_out=COLA[:, 0:1])
                vector.scalar_tensor_tensor(FB3[:, :], FB4[:, :], 0.0,
                                            FB4[:, :], ALU.add, ALU.mult,
                                            accum_out=COLA[:, 1:2])
                vector.tensor_scalar(COLA[:, 2:3], COLA[:, 0:1], 1.0 / HB,
                                     None, ALU.mult)
                vector.tensor_copy(out=DUM[:, 32:48], in_=IDB[:, 0:16])
                vector.tensor_copy(out=DUM[:, 48:64], in_=IDB[:, 16:32])
                vector.tensor_tensor(COLA[:, 3:4], COLA[:, 2:3], COLA[:, 2:3],
                                     ALU.mult)
                vector.tensor_scalar(COLA[:, 4:5], COLA[:, 1:2], 1.0 / HB,
                                     None, ALU.mult)
                vector.tensor_copy(out=DUM[:, 32:48], in_=IDB[:, 0:16])
                vector.tensor_copy(out=DUM[:, 48:64], in_=IDB[:, 16:32])
                vector.tensor_tensor(COLA[:, 5:6], COLA[:, 4:5], COLA[:, 3:4],
                                     ALU.subtract)
                vector.tensor_copy(out=DUM[:, 16:17], in_=COLA[:, 5:6]
                                   ).then_inc(sm.l_dv, 1)
                vector.wait_ge(sm.l_ac, tg + 1)
                vector.reciprocal(COLB[:, 0:1], COLC[:, 0:1])
                vector.tensor_copy(out=DUM[:, 32:33], in_=COLB[:, 0:1])
                vector.tensor_copy(out=DUM[:, 48:49], in_=DUM[:, 32:33])
                vector.tensor_scalar(FB3[:, :], FB4[:, :], COLA[:, 2:3],
                                     COLB[:, 0:1], ALU.subtract, ALU.mult)
                vector.tensor_copy(out=DUMW[:, 0:128], in_=IDB[:, 0:128]).then_inc(sm.l_dv, 1)

    stack.close()
    return nc


def _build_masks(sub):
    m1 = np.zeros((4, 128, 8 * 512), np.float32)
    m2 = np.zeros((4, 128, 8 * 512), np.float32)
    for sl in range(4):
        P = SLOT_CHUNKS[sub][sl]
        for j in range(2):
            k5 = KT512[sl] - 2 + j
            for qt in range(4):
                q = np.arange(128)[:, None] + qt * 128 + P * 512
                k = np.arange(512)[None, :] + k5 * 512
                m1[sl][:, (j * 4 + qt) * 512:(j * 4 + qt + 1) * 512] = \
                    np.where(k <= q, 0.0, NEGV)
        for j in range(8):
            kt = KT128[sl] - 8 + j
            k = np.arange(128)[:, None] + kt * 128
            q = np.arange(512)[None, :] + P * 512
            m2[sl][:, j * 512:(j + 1) * 512] = np.where(k <= q, 0.0, NEGV)
    m1 = m1.reshape(4 * 128, 8 * 512).astype(ml_dtypes.bfloat16)
    m2 = m2.reshape(4 * 128, 8 * 512).astype(ml_dtypes.bfloat16)
    return m1, m2


_NC_CACHE = [None]


def kernel(L0_attn_x, L0_attn_Wq, L0_attn_Wk, L0_attn_Wv, L0_attn_Wo,
           L0_attn_gq, L0_attn_gk, L0_ffn_gate_W_packed, L0_ffn_gate_scales,
           L0_ffn_up_W_packed, L0_ffn_up_scales, L0_ffn_down_W_packed,
           L0_ffn_down_scales):
    x = np.ascontiguousarray(np.asarray(L0_attn_x, dtype=np.float32))
    Wq = np.ascontiguousarray(np.asarray(L0_attn_Wq, dtype=np.float32))
    Wk = np.asarray(L0_attn_Wk, dtype=np.float32)
    Wv = np.asarray(L0_attn_Wv, dtype=np.float32)
    Wkv = np.ascontiguousarray(np.concatenate([Wk, Wv], axis=0))
    Wo = np.ascontiguousarray(np.asarray(L0_attn_Wo, dtype=np.float32))
    gq = np.ascontiguousarray(np.asarray(L0_attn_gq, dtype=np.float32
                                         ).reshape(D, 1))
    gk = np.ascontiguousarray(np.asarray(L0_attn_gk, dtype=np.float32
                                         ).reshape(D, 1))
    gW = np.ascontiguousarray(np.asarray(L0_ffn_gate_W_packed, np.int32).T)
    uW = np.ascontiguousarray(np.asarray(L0_ffn_up_W_packed, np.int32).T)
    dW = np.ascontiguousarray(np.asarray(L0_ffn_down_W_packed, np.int32).T)
    gS = np.ascontiguousarray(np.asarray(L0_ffn_gate_scales, np.float32).T)
    uS = np.ascontiguousarray(np.asarray(L0_ffn_up_scales, np.float32).T)
    dS = np.ascontiguousarray(np.asarray(L0_ffn_down_scales, np.float32).T)
    e8v = np.zeros((8, 128), np.float32)
    for jj in range(8):
        e8v[jj, jj * 16:(jj + 1) * 16] = -1.0
    identv = np.eye(128, dtype=np.float32).astype(ml_dtypes.bfloat16)

    if _NC_CACHE[0] is None:
        _NC_CACHE[0] = build_nc()
    nc = _NC_CACHE[0]
    in_maps = []
    for c in range(8):
        b, sub = c // 2, c % 2
        chunks = SLOT_CHUNKS[sub]
        xqv = np.ascontiguousarray(
            np.concatenate([x[b, ch * 512:(ch + 1) * 512] for ch in chunks], 0))
        m1, m2 = _build_masks(sub)
        in_maps.append({
            "xb": np.ascontiguousarray(x[b]),
            "xq": xqv,
            "wq": Wq, "wkv": Wkv, "wo": Wo, "gqv": gq, "gkv": gk,
            "gsh": np.ascontiguousarray(gW[c * 128:(c + 1) * 128]),
            "ush": np.ascontiguousarray(uW[c * 128:(c + 1) * 128]),
            "dsh": np.ascontiguousarray(dW[c * 1024:(c + 1) * 1024]),
            "gsc": np.ascontiguousarray(gS[c * 8:(c + 1) * 8]),
            "usc": np.ascontiguousarray(uS[c * 8:(c + 1) * 8]),
            "dsc": np.ascontiguousarray(dS[c * 64:(c + 1) * 64]),
            "e8": e8v, "identin": identv, "m1in": m1, "m2in": m2,
        })
    res = run_bass_kernel_spmd(nc, in_maps, core_ids=list(range(8)))
    out = np.empty((4, S, HB), np.float32)
    for c in range(8):
        b, sub = c // 2, c % 2
        y2 = res.results[c]["y2out"]
        for sl, ch in enumerate(SLOT_CHUNKS[sub]):
            out[b, ch * 512:(ch + 1) * 512] = y2[sl * 512:(sl + 1) * 512]
    return out


# version 09
# speedup vs baseline: 1.0548x; 1.0090x over previous
import numpy as np
import ml_dtypes
import concourse.bass as bass
import concourse.mybir as mybir
from concourse.bass_utils import run_bass_kernel_spmd

F32 = mybir.dt.float32
BF16 = mybir.dt.bfloat16
I32 = mybir.dt.int32
I16 = mybir.dt.int16
ALU = mybir.AluOpType
AF = mybir.ActivationFunctionType
AX = mybir.AxisListType

HB = 2048
S = 4096
TL = 2048
NH = 16
KVH = 4
D = 128
MLP = 16384
EPS_RMS = 1e-6
EPS_LN = 1e-5
QK_SCALE = float(D) ** -0.5
G_C1 = 0.044715 * 0.7978845608028654
G_C2 = 0.7978845608028654
KT512 = [8, 6, 4, 2]
KT128 = [32, 24, 16, 8]
SLOT_CHUNKS = {0: [7, 4, 3, 0], 1: [6, 5, 2, 1]}
NEGV = -60000.0
DEBUG = False

DCH = ([("g", 0, j) for j in range(8)] + [("u", 0, j) for j in range(8)]
       + [("d", t, 0) for t in range(8)])
NDC = len(DCH)
A3L = [(sl, h) for sl in range(4) for h in range(16)]


def _a3_sched():
    n1 = n2 = 0
    out = []
    for i, (sl, h) in enumerate(A3L):
        u = {"i": i, "sl": sl, "h": h, "n1_0": n1, "n2_0": n2}
        n1 += 4 * KT512[sl]
        n2 += KT128[sl]
        u["n1_1"], u["n2_1"] = n1, n2
        out.append(u)
    return out


A3S = _a3_sched()

A4_BASE = 40


def _a4_blk_cnt(sl, pr, ks):
    return sl * A4_BASE + 4 + pr * 18 + ks + 1


def _a4_y1t_cnt(sl, qt):
    return sl * A4_BASE + 4 + (qt // 2) * 18 + 16 + (qt % 2) + 1


def build_nc():
    nc = bass.Bass("TRN2", target_bir_lowering=False,
                   detect_race_conditions=False)
    xb = nc.dram_tensor("xb", [S, HB], F32, kind="ExternalInput").ap()
    xq = nc.dram_tensor("xq", [TL, HB], F32, kind="ExternalInput").ap()
    wq = nc.dram_tensor("wq", [HB, NH * D], F32, kind="ExternalInput").ap()
    wkv = nc.dram_tensor("wkv", [2 * HB, KVH * D], F32, kind="ExternalInput").ap()
    wo = nc.dram_tensor("wo", [NH * D, HB], F32, kind="ExternalInput").ap()
    gqv = nc.dram_tensor("gqv", [D, 1], F32, kind="ExternalInput").ap()
    gkv = nc.dram_tensor("gkv", [D, 1], F32, kind="ExternalInput").ap()
    gsh = nc.dram_tensor("gsh", [128, MLP], I32, kind="ExternalInput").ap()
    ush = nc.dram_tensor("ush", [128, MLP], I32, kind="ExternalInput").ap()
    dsh = nc.dram_tensor("dsh", [1024, HB], I32, kind="ExternalInput").ap()
    gsc = nc.dram_tensor("gsc", [8, MLP], F32, kind="ExternalInput").ap()
    usc = nc.dram_tensor("usc", [8, MLP], F32, kind="ExternalInput").ap()
    dsc = nc.dram_tensor("dsc", [64, HB], F32, kind="ExternalInput").ap()
    e8 = nc.dram_tensor("e8", [8, 128], F32, kind="ExternalInput").ap()
    identin = nc.dram_tensor("identin", [128, 128], BF16, kind="ExternalInput").ap()
    m1in = nc.dram_tensor("m1in", [4 * 128, 8 * 512], BF16, kind="ExternalInput").ap()
    m2in = nc.dram_tensor("m2in", [4 * 128, 8 * 512], BF16, kind="ExternalInput").ap()
    y2out = nc.dram_tensor("y2out", [TL, HB], F32, kind="ExternalOutput").ap()
    if DEBUG:
        y1d_out = nc.dram_tensor("y1d_out", [TL, HB], F32, kind="ExternalOutput").ap()
        qt_out = nc.dram_tensor("qt_out", [NH * D, TL], BF16, kind="ExternalOutput").ap()
        gg_out = nc.dram_tensor("gg_out", [HB, MLP], BF16, kind="ExternalOutput").ap()
        gd_out = nc.dram_tensor("gd_out", [MLP, HB], BF16, kind="ExternalOutput").ap()
        pt_out = nc.dram_tensor("pt_out", [TL, HB], F32, kind="ExternalOutput").ap()
        qf_dbg = nc.dram_tensor("qf_dbg", [128, HB], F32, kind="ExternalOutput").ap()
        qn2_dbg = nc.dram_tensor("qn2_dbg", [128, HB], BF16, kind="ExternalOutput").ap()
        ofn_dbg = nc.dram_tensor("ofn_dbg", [128, 640], F32, kind="ExternalOutput").ap()
        os_dbg = nc.dram_tensor("os_dbg", [128, 512], BF16, kind="ExternalOutput").ap()
        mcb_dbg = nc.dram_tensor("mcb_dbg", [128, 8], F32, kind="ExternalOutput").ap()
        negm_dbg = nc.dram_tensor("negm_dbg", [1, 512], BF16, kind="ExternalOutput").ap()
        a4c_dbg = nc.dram_tensor("a4c_dbg", [128, 8], F32, kind="ExternalOutput").ap()
        fb2_dbg = nc.dram_tensor("fb2_dbg", [128, HB], F32, kind="ExternalOutput").ap()
        it_dbg = nc.dram_tensor("it_dbg", [128, 16384], BF16, kind="ExternalOutput").ap()
        y1t_dbg = nc.dram_tensor("y1t_dbg", [128, 32768], BF16, kind="ExternalOutput").ap()
        y1td_dbg = nc.dram_tensor("y1td_dbg", [2048, 2048], BF16, kind="ExternalOutput").ap()
        qe_dbg = nc.dram_tensor("qe_dbg", [128, HB], BF16, kind="ExternalOutput").ap()
        col_dbg = nc.dram_tensor("col_dbg", [128, 48], F32, kind="ExternalOutput").ap()
        qn_dbg = nc.dram_tensor("qn_dbg", [128, HB], BF16, kind="ExternalOutput").ap()

    wo_bf = nc.dram_tensor("wo_bf", [NH * D, HB], BF16, kind="Internal").ap()
    qt_dram = nc.dram_tensor("qt_dram", [NH * D, TL], BF16, kind="Internal").ap()
    y1_dram = nc.dram_tensor("y1_dram", [TL, HB], F32, kind="Internal").ap()
    y1t_dram = nc.dram_tensor("y1t_dram", [HB, TL], BF16, kind="Internal").ap()
    part = nc.dram_tensor("part", [TL, HB], F32, kind="Internal").ap()
    pg = nc.dram_tensor("pg", [256, MLP], BF16, kind="Internal").ap()
    pu = nc.dram_tensor("pu", [256, MLP], BF16, kind="Internal").ap()
    pd = nc.dram_tensor("pd", [2048, HB], BF16, kind="Internal").ap()
    gg = nc.dram_tensor("gg", [HB, MLP], BF16, kind="Internal").ap()
    gu = nc.dram_tensor("gu", [HB, MLP], BF16, kind="Internal").ap()
    gd = nc.dram_tensor("gd", [MLP, HB], BF16, kind="Internal").ap()

    import contextlib
    stack = contextlib.ExitStack()

    def sbt(name, shape, dt):
        return stack.enter_context(nc.sbuf_tensor(name, shape, dt))

    P64A = sbt("P64A", [128, 32768], BF16)
    P32A = sbt("P32A", [128, 16384], BF16)
    P32B = sbt("P32B", [128, 16384], BF16)
    MISC = sbt("MISC", [128, 8192], BF16)
    FB1 = sbt("FB1", [128, 2048], F32)
    FB2 = sbt("FB2", [128, 2048], F32)
    FB3 = sbt("FB3", [128, 2048], F32)
    FB4 = sbt("FB4", [128, 2048], F32)
    BB1 = sbt("BB1", [128, 4096], BF16)
    BB2 = sbt("BB2", [128, 4096], BF16)
    BB3 = sbt("BB3", [128, 4096], BF16)
    E8S = sbt("E8S", [8, 128], F32)
    IDB = sbt("IDB", [128, 128], BF16)
    ONES1 = sbt("ONES1", [1, 128], BF16)
    ONESC = sbt("ONESC", [128, 1], BF16)
    GQC = sbt("GQC", [128, 1], F32)
    GKC = sbt("GKC", [128, 1], F32)
    GQS = sbt("GQS", [128, 1], F32)
    COLA = sbt("COLA", [128, 16], F32)
    COLB = sbt("COLB", [128, 16], F32)
    COLC = sbt("COLC", [128, 16], F32)
    MCOL = sbt("MCOL", [128, 4], F32)
    MCB = sbt("MCB", [128, 4], BF16)
    DUM = sbt("DUM", [128, 64], F32)
    DUMW = sbt("DUMW", [128, 256], F32)
    NEGM = sbt("NEGM", [1, 512], BF16)
    PTB = sbt("PTB", [128, 1024], BF16)
    QTS = sbt("QTS", [128, 1024], BF16)
    RC = sbt("RC", [128, 4], F32)
    EPSC = sbt("EPSC", [128, 2], F32)

    CSD = P64A[:, 0:8192].bitcast(I32)
    DI1 = P64A[:, 8192:12288].bitcast(I32)
    DI2 = P64A[:, 12288:16384].bitcast(I32)
    DI3 = P64A[:, 16384:20480].bitcast(I32)
    DI4 = P64A[:, 20480:24576].bitcast(I32)
    DBITS = P64A[:, 24576:28672].bitcast(I16)
    DT1 = P64A[:, 28672:30720].bitcast(I16)
    DNSX = P32A[:, 0:4096]
    DLO = P32A[:, 4096:6144]
    DHI = P32A[:, 6144:8192]
    DC1 = P32A[:, 8192:10240]
    DC2 = P32A[:, 10240:12288]
    DT2 = P32A[:, 12288:14336]
    DT3 = P32A[:, 14336:16384]
    WQS = P64A
    WKS = P64A[:, 0:8192]
    WVS = P64A[:, 8192:16384]
    KTS = P32A
    VS = P32B
    OS = P64A[:, 16384:24576]
    OTS = P64A[:, 24576:32768]
    M1S = MISC[:, 0:4096]
    M2S = MISC[:, 4096:8192]
    SCL0 = FB1[0:8, 0:2048]
    SCL1 = FB2[0:8, 0:2048]
    PS1F = P64A[:, 0:2048].bitcast(F32)
    NEGF = P64A[0:1, 2048:3072].bitcast(F32)
    OFN = P64A[:, 3072:4352].bitcast(F32)
    PS2F = P64A[:, 4352:6400].bitcast(F32)
    AOF = P64A[:, 8192:16384].bitcast(F32)
    Y1TS = P64A
    INTERT = P32A
    GUBF = MISC
    WD8 = P32B

    PSA = stack.enter_context(nc.psum_tensor("PSA", [128, 2048], F32))
    PSB = stack.enter_context(nc.psum_tensor("PSB", [128, 2048], F32))
    PSAb = PSA[:, :].bitcast(BF16)
    PSBb = PSB[:, :].bitcast(BF16)

    sems = {}
    for n in ("c_ld", "d_l0", "d_l1", "d_sc", "d_pe", "d_ac", "d_dv", "d_cc",
              "w_g0", "w_g1", "w_s0", "w_s1",
              "q_gw", "q_g0", "q_g1", "q_pe", "q_pe2", "q_ac", "q_dv", "q_st",
              "k_gw", "k_g0", "k_g1", "k_pe", "k_pe2", "k_ac", "k_dv",
              "a_lm", "a_l0", "a_l1", "a_p1", "a_a1", "a_d1", "a_dm",
              "a_pr", "a_an", "a_dn", "a_ao",
              "a_p2", "a_e2", "a_d2", "a_ae", "a_pe", "a_df",
              "o_l0", "o_l1", "o_lx", "o_pe", "o_oa", "o_ac", "o_dv",
              "o_sy", "o_sz",
              "f_ly", "f_lw", "f_lg0", "f_lg1", "f_pe", "f_pe2",
              "f_ac", "f_ac2", "f_tg", "f_th", "f_dv", "f_g0", "f_g1",
              "l_ld", "l_ac", "l_dv", "l_st"):
        sems[n] = stack.enter_context(nc.semaphore(n))
    sm = type("SM", (), sems)
    DL = [sems["d_l0"], sems["d_l1"]]
    WG = [sems["w_g0"], sems["w_g1"]]
    WS = [sems["w_s0"], sems["w_s1"]]
    QG = [sems["q_g0"], sems["q_g1"]]
    KG = [sems["k_g0"], sems["k_g1"]]
    AL = [sems["a_l0"], sems["a_l1"]]
    OL = [sems["o_l0"], sems["o_l1"]]
    FLG = [sems["f_lg0"], sems["f_lg1"]]
    FG = [sems["f_g0"], sems["f_g1"]]

    with nc.Block() as block:

        def emit_a4_tensor(tensor, sl):
            tensor.wait_ge(sm.a_df, 16 * (sl + 1))
            tensor.wait_ge(sm.a_ae, A3S[16 * (sl + 1) - 1]["n2_1"])
            tensor.wait_ge(sm.a_d1, A3S[16 * (sl + 1) - 1]["n1_1"])
            for r in range(4):
                if r >= 1:
                    tensor.wait_ge(sm.o_oa, sl * 4 + r)
                for jj in range(16):
                    f = r * 16 + jj
                    ks, qt = f // 4, f % 4
                    op = tensor.transpose(PSAb[:, jj * 128:(jj + 1) * 128],
                                          OS[:, qt * 2048 + ks * 128:
                                             qt * 2048 + (ks + 1) * 128],
                                          IDB[:, :])
                op.then_inc(sm.o_pe, 1)
            tensor.wait_ge(sm.o_oa, sl * 4 + 4)
            for pr in range(2):
                if pr == 1:
                    tensor.wait_ge(sm.o_dv, 3 * (sl * 4 + 2))
                    tensor.wait_ge(sm.o_ac, 3 * (sl * 4 + 1) + 3)
                for ks in range(16):
                    it = sl * 32 + pr * 16 + ks
                    tensor.wait_ge(OL[it % 2], 16 * (it // 2 + 1))
                    wob = BB1[:, (it % 2) * 2048:(it % 2) * 2048 + 2048]
                    for qt2 in range(2):
                        qt = pr * 2 + qt2
                        aop = PSA if qt2 == 0 else PSB
                        for hc in range(4):
                            op = tensor.matmul(aop[:, hc * 512:(hc + 1) * 512],
                                               OTS[:, (ks * 4 + qt) * 128:
                                                   (ks * 4 + qt + 1) * 128],
                                               wob[:, hc * 512:(hc + 1) * 512],
                                               start=(ks == 0), stop=(ks == 15))
                    op.then_inc(sm.o_pe, 1)
                for qt2 in range(2):
                    qt = pr * 2 + qt2
                    it2 = sl * 4 + qt
                    tensor.wait_ge(sm.o_dv, 3 * it2 + 3)
                    if it2 >= 2:
                        tensor.wait_ge(sm.o_ac, 3 * (it2 - 2) + 3)
                    y1b = BB3[:, (qt % 2) * 2048:(qt % 2) * 2048 + 2048]
                    dstb = PSAb if qt2 == 0 else PSBb
                    for hh in range(16):
                        op = tensor.transpose(dstb[:, hh * 128:(hh + 1) * 128],
                                              y1b[:, hh * 128:(hh + 1) * 128],
                                              IDB[:, :])
                    op.then_inc(sm.o_pe, 1)

        def emit_a4_scalar(scalar, sl):
            for r in range(4):
                scalar.wait_ge(sm.o_pe, sl * A4_BASE + r + 1)
                op = scalar.activation(OTS[:, r * 2048:(r + 1) * 2048],
                                       PSAb[:, 0:2048], AF.Identity)
                if r == 3:
                    op = scalar.activation(DUMW[:, 128:256], IDB[:, 0:128], AF.Identity)
                op.then_inc(sm.o_oa, 1)
            def y1t_store(j):
                scalar.dma_start(
                    out=bass.AP(y1t_dram.tensor, j * 128,
                                [[TL, 128], [128 * TL, 16], [1, 128]]),
                    in_=BB2[:, (j % 2) * 2048:(j % 2) * 2048 + 2048
                            ].rearrange("p (h j) -> p h j", h=16)
                ).then_inc(sm.o_sz, 16)

            for qt in range(4):
                it2 = sl * 4 + qt
                pr, qt2 = qt // 2, qt % 2
                scalar.wait_ge(sm.o_pe, _a4_blk_cnt(sl, pr, 15))
                if it2 >= 2:
                    scalar.wait_ge(sm.o_dv, 3 * (it2 - 2) + 1)
                aop = PSA if qt2 == 0 else PSB
                scalar.activation(AOF[:, (qt % 2) * 2048:(qt % 2) * 2048 + 2048],
                                  aop[:, :], AF.Identity)
                scalar.activation(DUMW[:, 128:256], IDB[:, 0:128], AF.Identity).then_inc(sm.o_ac, 1)
                if it2 >= 2:
                    scalar.wait_ge(sm.o_sz, 16 * (it2 - 1))
                scalar.wait_ge(sm.o_dv, 3 * it2 + 2)
                if DEBUG and it2 == 0:
                    scalar.dma_start(out=a4c_dbg[:, :], in_=COLA[:, 0:8]
                                     ).then_inc(sm.c_ld, 16)
                    scalar.dma_start(out=fb2_dbg[:, :], in_=FB2[:, :]
                                     ).then_inc(sm.c_ld, 16)
                    scalar.wait_ge(sm.c_ld, 64 + 16 * 16)
                scalar.activation(COLC[:, 0:1], COLA[:, 5:6], AF.Sqrt,
                                  bias=EPSC[:, 1:2], scale=1.0)
                scalar.activation(DUM[:, 0:1], COLC[:, 0:1], AF.Sqrt
                                  ).then_inc(sm.o_ac, 1)
                scalar.wait_ge(sm.o_dv, 3 * it2 + 3)
                scalar.dma_start(out=y1_dram[it2 * 128:(it2 + 1) * 128, :],
                                 in_=FB4[:, :]).then_inc(sm.o_sy, 16)
                scalar.wait_ge(sm.o_pe, _a4_y1t_cnt(sl, qt))
                pb = PSAb if qt % 2 == 0 else PSBb
                scalar.activation(BB2[:, (qt % 2) * 2048:(qt % 2) * 2048 + 2048],
                                  pb[:, 0:2048], AF.Identity).then_inc(sm.o_ac, 1)
                if it2 >= 1:
                    y1t_store(it2 - 1)
            if sl == 3:
                scalar.wait_ge(sm.o_sy, 16 * 16)
                scalar.wait_ge(sm.o_sz, 16 * 15)
                y1t_store(15)

        def emit_a4_vector(vector, sl):
            for pr in range(2):
                for qt2 in range(2):
                    qt = pr * 2 + qt2
                    it2 = sl * 4 + qt
                    vector.wait_ge(sm.o_ac, 3 * it2 + 1)
                    vector.wait_ge(sm.o_lx, 16 * (it2 + 1))
                    vector.scalar_tensor_tensor(FB2[:, :],
                                                AOF[:, (qt % 2) * 2048:
                                                    (qt % 2) * 2048 + 2048],
                                                0.0, FB1[:, :],
                                                ALU.add, ALU.add,
                                                accum_out=COLA[:, 0:1]
                                                ).then_inc(sm.o_dv, 1)
                    vector.scalar_tensor_tensor(FB3[:, :], FB2[:, :], 0.0,
                                                FB2[:, :], ALU.add, ALU.mult,
                                                accum_out=COLA[:, 1:2])
                    vector.tensor_scalar(COLA[:, 2:3], COLA[:, 0:1], 1.0 / HB,
                                         None, ALU.mult)
                    vector.tensor_copy(out=DUM[:, 32:48], in_=IDB[:, 0:16])
                    vector.tensor_copy(out=DUM[:, 48:64], in_=IDB[:, 16:32])
                    vector.tensor_tensor(COLA[:, 3:4], COLA[:, 2:3],
                                         COLA[:, 2:3], ALU.mult)
                    vector.tensor_scalar(COLA[:, 4:5], COLA[:, 1:2], 1.0 / HB,
                                         None, ALU.mult)
                    vector.tensor_copy(out=DUM[:, 32:48], in_=IDB[:, 0:16])
                    vector.tensor_copy(out=DUM[:, 48:64], in_=IDB[:, 16:32])
                    vector.tensor_tensor(COLA[:, 5:6], COLA[:, 4:5],
                                         COLA[:, 3:4], ALU.subtract)
                    vector.tensor_copy(out=DUM[:, 16:17], in_=COLA[:, 5:6]
                                       ).then_inc(sm.o_dv, 1)
                    vector.wait_ge(sm.o_ac, 3 * it2 + 2)
                    vector.reciprocal(COLB[:, 0:1], COLC[:, 0:1])
                    vector.tensor_copy(out=DUM[:, 32:33], in_=COLB[:, 0:1])
                    vector.tensor_copy(out=DUM[:, 48:49], in_=DUM[:, 32:33])
                    if it2 >= 1:
                        vector.wait_ge(sm.o_sy, 16 * it2)
                    vector.tensor_scalar(FB4[:, :], FB2[:, :], COLA[:, 2:3],
                                         COLB[:, 0:1], ALU.subtract, ALU.mult)
                    if it2 >= 2:
                        vector.wait_ge(sm.o_pe, _a4_y1t_cnt(sl, qt - 2)
                                       if qt >= 2 else
                                       _a4_y1t_cnt(sl - 1, qt + 2))
                    vector.tensor_copy(out=BB3[:, (qt % 2) * 2048:
                                               (qt % 2) * 2048 + 2048],
                                       in_=FB4[:, :])
                    vector.tensor_copy(out=DUMW[:, 0:128], in_=IDB[:, 0:128]).then_inc(sm.o_dv, 1)

        @block.sync
        def _(sync):
            sync.dma_start(out=E8S[:, :], in_=e8[:, :]).then_inc(sm.c_ld, 16)
            sync.dma_start(out=GQC[:, :], in_=gqv[:, :]).then_inc(sm.c_ld, 16)
            sync.dma_start(out=GKC[:, :], in_=gkv[:, :]).then_inc(sm.c_ld, 16)
            sync.dma_start(out=IDB[:, :], in_=identin[:, :]).then_inc(sm.c_ld, 16)
            for j, (m, wt, ch) in enumerate(DCH):
                if j >= 2:
                    sync.wait_ge(sm.d_dv, j - 1)
                    sync.wait_ge(sm.d_pe, j - 1)
                buf = CSD[:, (j % 2) * 2048:(j % 2) * 2048 + 2048]
                sc = (SCL0 if j % 2 == 0 else SCL1)[:, :]
                dl = DL[j % 2]
                if m == "g":
                    sync.dma_start(out=buf, in_=gsh[:, ch * 2048:(ch + 1) * 2048]
                                   ).then_inc(dl, 16)
                    sync.dma_start(out=sc, in_=gsc[:, ch * 2048:(ch + 1) * 2048]
                                   ).then_inc(dl, 16)
                elif m == "u":
                    sync.dma_start(out=buf, in_=ush[:, ch * 2048:(ch + 1) * 2048]
                                   ).then_inc(dl, 16)
                    sync.dma_start(out=sc, in_=usc[:, ch * 2048:(ch + 1) * 2048]
                                   ).then_inc(dl, 16)
                else:
                    sync.dma_start(out=buf, in_=dsh[wt * 128:(wt + 1) * 128, :]
                                   ).then_inc(dl, 16)
                    sync.dma_start(out=sc, in_=dsc[wt * 8:(wt + 1) * 8, :]
                                   ).then_inc(dl, 16)
            def emit_xq(sl, qt):
                it2 = sl * 4 + qt
                if it2 >= 1:
                    sync.wait_ge(sm.o_dv, 3 * it2 - 2)
                sync.dma_start(
                    out=FB1[:, :],
                    in_=xq[(sl * 4 + qt) * 128:(sl * 4 + qt + 1) * 128, :]
                ).then_inc(sm.o_lx, 16)

            def emit_a4_sync(sl):
                emit_xq(sl, 0)
                for pr in range(2):
                    for ks in range(16):
                        it = sl * 32 + pr * 16 + ks
                        if it == 0:
                            sync.wait_ge(WS[0], 16 * 8)
                            sync.wait_ge(WS[1], 16 * 8)
                        if it >= 2:
                            it_prev = it - 2
                            slp, rem = divmod(it_prev, 32)
                            prp, ksp = divmod(rem, 16)
                            sync.wait_ge(sm.o_pe, _a4_blk_cnt(slp, prp, ksp))
                        sync.wait_ge(sm.a_df, 16 * (sl + 1))
                        sync.dma_start(
                            out=BB1[:, (it % 2) * 2048:(it % 2) * 2048 + 2048],
                            in_=wo_bf[ks * 128:(ks + 1) * 128, :]
                        ).then_inc(OL[it % 2], 16)
                    emit_xq(sl, 1 + pr)
                emit_xq(sl, 3)

            for u in A3S:
                i, sl, h = u["i"], u["sl"], u["h"]
                if h == 0:
                    if sl > 0:
                        emit_a4_sync(sl - 1)
                        sync.wait_ge(sm.a_pe, i)
                    sync.dma_start(out=M1S[:, :],
                                   in_=m1in[sl * 128:(sl + 1) * 128, :]
                                   ).then_inc(sm.a_lm, 16)
                    sync.dma_start(out=M2S[:, :],
                                   in_=m2in[sl * 128:(sl + 1) * 128, :]
                                   ).then_inc(sm.a_lm, 16)
                if i == 0:
                    sync.wait_ge(sm.q_st, 16 * 16)
                if i >= 2:
                    sync.wait_ge(sm.a_pe, i - 1)
                sync.dma_start(
                    out=QTS[:, (i % 2) * 512:(i % 2) * 512 + 512],
                    in_=qt_dram[h * 128:(h + 1) * 128, sl * 512:(sl + 1) * 512]
                ).then_inc(AL[i % 2], 16)
            emit_a4_sync(3)
            sync.wait_ge(sm.o_sy, 16 * 16)
            sync.wait_ge(sm.o_sz, 16 * 16)
            for hc in range(16):
                sync.dma_start(out=Y1TS[:, hc * 2048:(hc + 1) * 2048],
                               in_=y1t_dram[hc * 128:(hc + 1) * 128, :]
                               ).then_inc(sm.f_ly, 16)
            for s in range(16):
                for sub in range(8):
                    if sub == 0:
                        if s == 0:
                            sync.wait_ge(sm.d_cc, 3)
                        else:
                            sync.wait_ge(sm.f_pe2, 16 * s)
                    sync.dma_start(
                        out=WD8[:, sub * 2048:(sub + 1) * 2048],
                        in_=gd[(s * 8 + sub) * 128:(s * 8 + sub + 1) * 128, :]
                    ).then_inc(sm.f_lw, 16)
                for sub in range(8):
                    for mi, gap in enumerate((gg, gu)):
                        gidx = s * 8 + sub
                        if gidx >= 2:
                            sync.wait_ge(sm.f_pe, 4 * (gidx - 1))
                        slot = (sub % 2) * 2 + mi
                        sync.dma_start(
                            out=GUBF[:, slot * 2048:(slot + 1) * 2048],
                            in_=bass.AP(gap.tensor, gidx * 128,
                                        [[MLP, 128], [128 * MLP, 16], [1, 128]])
                        ).then_inc(FLG[sub % 2], 16)
            sync.wait_ge(FG[0], 16 * 128)
            sync.wait_ge(FG[1], 16 * 128)
            for tg in range(16):
                if tg >= 1:
                    sync.wait_ge(sm.l_dv, 2 * tg)
                sync.dma_start(out=FB1[:, :],
                               in_=part[tg * 128:(tg + 1) * 128, :]
                               ).then_inc(sm.l_ld, 16)
                sync.dma_start(out=FB2[:, :],
                               in_=y1_dram[tg * 128:(tg + 1) * 128, :]
                               ).then_inc(sm.l_ld, 16)
            if DEBUG:
                sync.dma_start(out=y1d_out[:, :], in_=y1_dram[:, :]
                               ).then_inc(sm.c_ld, 16)
                sync.dma_start(out=qt_out[:, :], in_=qt_dram[:, :]
                               ).then_inc(sm.c_ld, 16)
                sync.dma_start(out=gg_out[:, :], in_=gg[:, :]
                               ).then_inc(sm.c_ld, 16)
                sync.dma_start(out=gd_out[:, :], in_=gd[:, :]
                               ).then_inc(sm.c_ld, 16)
                sync.dma_start(out=pt_out[:, :], in_=part[:, :]
                               ).then_inc(sm.c_ld, 16)
                sync.wait_ge(sm.c_ld, 64 + 10 * 16)

        @block.gpsimd
        def _(gpsimd):
            gpsimd.wait_ge(sm.d_dv, NDC)
            for i in range(16):
                gpsimd.dma_start(out=WQS[:, i * 2048:(i + 1) * 2048],
                                 in_=wq[i * 128:(i + 1) * 128, :]
                                 ).then_inc(sm.q_gw, 16)
            for qt in range(16):
                if qt >= 2:
                    gpsimd.wait_ge(sm.q_pe2, qt - 1)
                gpsimd.dma_start(out=BB1[:, (qt % 2) * 2048:(qt % 2) * 2048 + 2048],
                                 in_=xq[qt * 128:(qt + 1) * 128, :]
                                 ).then_inc(QG[qt % 2], 16)
            for i in range(16):
                if i >= 2:
                    gpsimd.wait_ge(WS[i % 2], 16 * ((i - 2) // 2 + 1))
                gpsimd.dma_start(out=P32B[:, (i % 2) * 2048:(i % 2) * 2048 + 2048],
                                 in_=wo[i * 128:(i + 1) * 128, :]
                                 ).then_inc(WG[i % 2], 16)
            gpsimd.wait_ge(sm.q_pe, 32)
            gpsimd.dma_start(
                out=WKS[:, :].rearrange("p (t d) -> p t d", t=16),
                in_=bass.AP(wkv.tensor, 0, [[512, 128], [128 * 512, 16], [1, 512]])
            ).then_inc(sm.k_gw, 16)
            gpsimd.dma_start(
                out=WVS[:, :].rearrange("p (t d) -> p t d", t=16),
                in_=bass.AP(wkv.tensor, HB * 512,
                            [[512, 128], [128 * 512, 16], [1, 512]])
            ).then_inc(sm.k_gw, 16)
            for i in range(32):
                if i >= 2:
                    gpsimd.wait_ge(sm.k_pe2, i - 1)
                gpsimd.dma_start(out=BB1[:, (i % 2) * 2048:(i % 2) * 2048 + 2048],
                                 in_=xb[i * 128:(i + 1) * 128, :]
                                 ).then_inc(KG[i % 2], 16)
            for gi, (piece, gath) in enumerate(((pg, gg), (pu, gu), (pd, gd))):
                gpsimd.wait_ge(sm.d_sc, 16 * 16 * (gi + 1))
                gpsimd.collective_compute(
                    "AllGather", ALU.bypass,
                    replica_groups=[[0, 1, 2, 3, 4, 5, 6, 7]],
                    ins=[piece.opt()], outs=[gath.opt()],
                ).then_inc(sm.d_cc)
            for s in range(16):
                for tg in range(16):
                    it = s * 16 + tg
                    gpsimd.wait_ge(sm.f_ac2, it + 1)
                    if s >= 1:
                        prev = (s - 1) * 16 + tg
                        gpsimd.wait_ge(FG[tg % 2], 16 * (prev // 2 + 1))
                    stg = FB4 if tg % 2 == 0 else FB1
                    gpsimd.dma_start(
                        out=part[tg * 128:(tg + 1) * 128, :],
                        in_=stg[:, :],
                        accum_op=(ALU.bypass if s == 0 else ALU.add),
                    ).then_inc(FG[tg % 2], 16)
            gpsimd.wait_ge(FG[0], 16 * 128)
            gpsimd.wait_ge(FG[1], 16 * 128)

        @block.tensor
        def _(tensor):
            tensor.wait_ge(sm.c_ld, 64)
            for j in range(NDC):
                tensor.wait_ge(DL[j % 2], 32 * (j // 2 + 1))
                if j >= 1:
                    tensor.wait_ge(sm.d_ac, j)
                scl = SCL0 if j % 2 == 0 else SCL1
                for q4 in range(4):
                    op = tensor.matmul(PSA[:, q4 * 512:(q4 + 1) * 512],
                                       E8S[:, :],
                                       scl[0:8, q4 * 512:(q4 + 1) * 512],
                                       start=True, stop=True)
                op.then_inc(sm.d_pe, 1)
            tensor.wait_ge(sm.c_ld, 64)
            tensor.wait_ge(sm.q_gw, 16 * 16)
            for qt in range(16):
                tensor.wait_ge(QG[qt % 2], 16 * (qt // 2 + 1))
                if qt >= 1:
                    tensor.wait_ge(sm.q_ac, 3 * (qt - 1) + 2)
                xin = BB1[:, (qt % 2) * 2048:(qt % 2) * 2048 + 2048]
                for kc in range(16):
                    op = tensor.transpose(PSBb[:, kc * 128:(kc + 1) * 128],
                                          xin[:, kc * 128:(kc + 1) * 128],
                                          IDB[:, :])
                op.then_inc(sm.q_pe2, 1)
                tensor.wait_ge(sm.q_ac, 3 * qt + 1)
                xt = BB2[:, (qt % 2) * 2048:(qt % 2) * 2048 + 2048]
                for dqc in range(4):
                    for kc in range(16):
                        op = tensor.matmul(PSA[:, dqc * 512:(dqc + 1) * 512],
                                           xt[:, kc * 128:(kc + 1) * 128],
                                           WQS[:, kc * 2048 + dqc * 512:
                                               kc * 2048 + (dqc + 1) * 512],
                                           start=(kc == 0), stop=(kc == 15))
                op.then_inc(sm.q_pe, 1)
                tensor.wait_ge(sm.q_dv, 4 * qt + 3)
                for hh in range(16):
                    op = tensor.transpose(
                        PSBb[:, 2048 + hh * 128:2048 + (hh + 1) * 128],
                        xt[:, hh * 128:(hh + 1) * 128], IDB[:, :])
                op.then_inc(sm.q_pe, 1)
            tensor.wait_ge(sm.k_gw, 32)
            for tt in range(32):
                tensor.wait_ge(KG[tt % 2], 16 * (tt // 2 + 1))
                if tt == 0:
                    tensor.wait_ge(sm.q_ac, 48)
                    tensor.wait_ge(sm.q_dv, 64)
                if tt >= 1:
                    tensor.wait_ge(sm.k_ac, 4 * (tt - 1) + 1)
                xin = BB1[:, (tt % 2) * 2048:(tt % 2) * 2048 + 2048]
                for kc in range(16):
                    op = tensor.transpose(PSBb[:, kc * 128:(kc + 1) * 128],
                                          xin[:, kc * 128:(kc + 1) * 128],
                                          IDB[:, :])
                op.then_inc(sm.k_pe2, 1)
                tensor.wait_ge(sm.k_ac, 4 * tt + 1)
                xt = BB2[:, (tt % 2) * 2048:(tt % 2) * 2048 + 2048]
                for kc in range(16):
                    tensor.matmul(PSA[:, 0:512], xt[:, kc * 128:(kc + 1) * 128],
                                  WKS[:, kc * 512:(kc + 1) * 512],
                                  start=(kc == 0), stop=(kc == 15))
                for kc in range(16):
                    op = tensor.matmul(PSA[:, 512:1024],
                                       xt[:, kc * 128:(kc + 1) * 128],
                                       WVS[:, kc * 512:(kc + 1) * 512],
                                       start=(kc == 0), stop=(kc == 15))
                op.then_inc(sm.k_pe, 1)
                tensor.wait_ge(sm.k_dv, 4 * tt + 3)
                for sgi in range(4):
                    op = tensor.transpose(
                        PSBb[:, 2048 + sgi * 128:2048 + (sgi + 1) * 128],
                        xt[:, sgi * 128:(sgi + 1) * 128], IDB[:, :])
                op.then_inc(sm.k_pe, 1)
            tensor.wait_ge(sm.k_dv, 4 * 32)
            tensor.wait_ge(sm.k_ac, 4 * 32)
            for u in A3S:
                i, sl, h = u["i"], u["sl"], u["h"]
                kvh = h // 4
                qtl = QTS[:, (i % 2) * 512:(i % 2) * 512 + 512]
                if h == 0 and sl > 0:
                    emit_a4_tensor(tensor, sl - 1)
                    tensor.wait_ge(sm.o_ac, 12 * sl)
                tensor.wait_ge(sm.a_lm, 32 * (sl + 1))
                tensor.wait_ge(AL[i % 2], 16 * (i // 2 + 1))
                n1 = u["n1_0"]
                for qt in range(4):
                    for k5 in range(KT512[sl]):
                        if n1 >= 2:
                            tensor.wait_ge(sm.a_d1, n1 - 1)
                        tensor.matmul(PSA[:, (n1 % 2) * 512:(n1 % 2) * 512 + 512],
                                      qtl[:, qt * 128:(qt + 1) * 128],
                                      KTS[:, kvh * 4096 + k5 * 512:
                                          kvh * 4096 + (k5 + 1) * 512],
                                      start=True, stop=True
                                      ).then_inc(sm.a_p1, 1)
                        n1 += 1
                tensor.wait_ge(sm.a_dm, i + 1)
                if i >= 1:
                    tensor.wait_ge(sm.a_df, i)
                for qt in range(4):
                    op = tensor.matmul(PSB[0:1, 1024 + qt * 128:
                                           1024 + (qt + 1) * 128],
                                       MCB[:, qt:qt + 1], IDB[:, :],
                                       start=(qt == 0), stop=(qt == 3))
                op.then_inc(sm.a_pr, 1)
                tensor.wait_ge(sm.a_dn, i + 1)
                n2 = u["n2_0"]
                for kt in range(KT128[sl]):
                    if n2 >= 2:
                        tensor.wait_ge(sm.a_ae, n2 - 1)
                    bank = PSA[:, 1024 + (n2 % 2) * 512:
                               1024 + (n2 % 2) * 512 + 512]
                    tensor.matmul(bank,
                                  KTS[:, kvh * 4096 + kt * 128:
                                      kvh * 4096 + (kt + 1) * 128],
                                  qtl[:, :], start=True, stop=False)
                    tensor.matmul(bank, ONES1[:, :], NEGM[0:1, :],
                                  start=False, stop=True
                                  ).then_inc(sm.a_p2, 1)
                    tensor.wait_ge(sm.a_ae, n2 + 1)
                    pt = PTB[:, (n2 % 2) * 512:(n2 % 2) * 512 + 512]
                    last = (kt == KT128[sl] - 1)
                    for qt in range(4):
                        oacc = PSB[:, qt * 512:qt * 512 + 512]
                        tensor.matmul(oacc[:, 0:128],
                                      pt[:, qt * 128:(qt + 1) * 128],
                                      VS[:, kt * 512 + kvh * 128:
                                         kt * 512 + (kvh + 1) * 128],
                                      start=(kt == 0), stop=False)
                        op = tensor.matmul(oacc[:, 128:129],
                                           pt[:, qt * 128:(qt + 1) * 128],
                                           ONESC[:, :], start=False, stop=last)
                    n2 += 1
                op.then_inc(sm.a_pe, 1)
            emit_a4_tensor(tensor, 3)
            tensor.wait_ge(sm.o_ac, 48)
            tensor.wait_ge(sm.f_ly, 16 * 16)
            for s in range(16):
                if s >= 1:
                    tensor.wait_ge(sm.f_ac2, s * 16 - 1)
                for sub in range(8):
                    tensor.wait_ge(FLG[sub % 2], 32 * (s * 4 + sub // 2 + 1))
                    gslot = GUBF[:, ((sub % 2) * 2) * 2048:
                                 ((sub % 2) * 2) * 2048 + 2048]
                    uslot = GUBF[:, ((sub % 2) * 2 + 1) * 2048:
                                 ((sub % 2) * 2 + 1) * 2048 + 2048]
                    for tq in range(4):
                        k = (s * 8 + sub) * 4 + tq
                        if k >= 2:
                            tensor.wait_ge(sm.f_dv, k - 1)
                        gb = PSA[:, (tq % 2) * 512:(tq % 2) * 512 + 512]
                        ub = PSA[:, 1024 + (tq % 2) * 512:
                                 1024 + (tq % 2) * 512 + 512]
                        for hs in range(16):
                            tensor.matmul(gb, gslot[:, hs * 128:(hs + 1) * 128],
                                          Y1TS[:, hs * 2048 + tq * 512:
                                               hs * 2048 + (tq + 1) * 512],
                                          start=(hs == 0), stop=(hs == 15))
                        for hs in range(16):
                            op = tensor.matmul(ub,
                                               uslot[:, hs * 128:(hs + 1) * 128],
                                               Y1TS[:, hs * 2048 + tq * 512:
                                                    hs * 2048 + (tq + 1) * 512],
                                               start=(hs == 0), stop=(hs == 15))
                        op.then_inc(sm.f_pe, 1)
                tensor.wait_ge(sm.f_lw, 16 * 8 * (s + 1))
                for tg in range(16):
                    it = s * 16 + tg
                    accb = PSA if tg % 2 == 0 else PSB
                    tensor.wait_ge(sm.f_dv, 32 * s + 29 + tg // 4)
                    if tg == 0:
                        tensor.wait_ge(sm.f_ac, 64 * (s + 1))
                    if it >= 2:
                        tensor.wait_ge(sm.f_ac2, it - 1)
                    for sub in range(8):
                        for hc in range(4):
                            op = tensor.matmul(accb[:, hc * 512:(hc + 1) * 512],
                                               INTERT[:, sub * 2048 + tg * 128:
                                                      sub * 2048 + (tg + 1) * 128],
                                               WD8[:, sub * 2048 + hc * 512:
                                                   sub * 2048 + (hc + 1) * 512],
                                               start=(sub == 0), stop=(sub == 7))
                    op.then_inc(sm.f_pe2, 1)

        @block.scalar
        def _(scalar):
            def d_piece(jj):
                m, wt, ch = DCH[jj]
                scalar.wait_ge(sm.d_dv, jj + 1)
                if m in ("g", "u"):
                    dst = pg if m == "g" else pu
                    o_lo = bass.AP(dst.tensor, ch * 2048,
                                   [[2 * MLP, 128], [1, 2048]])
                    o_hi = bass.AP(dst.tensor, MLP + ch * 2048,
                                   [[2 * MLP, 128], [1, 2048]])
                else:
                    o_lo = bass.AP(pd.tensor, (2 * wt * 128) * HB,
                                   [[2 * HB, 128], [1, 2048]])
                    o_hi = bass.AP(pd.tensor, (2 * wt * 128 + 1) * HB,
                                   [[2 * HB, 128], [1, 2048]])
                scalar.dma_start(out=o_lo, in_=DLO[:, :]).then_inc(sm.d_sc, 16)
                scalar.dma_start(out=o_hi, in_=DHI[:, :]).then_inc(sm.d_sc, 16)

            for j in range(NDC):
                scalar.wait_ge(sm.d_pe, j + 1)
                scalar.activation(DNSX[:, (j % 2) * 2048:(j % 2) * 2048 + 2048],
                                  PSA[:, :], AF.Identity).then_inc(sm.d_ac, 1)
                if j >= 1:
                    d_piece(j - 1)
            d_piece(NDC - 1)
            for qt in range(16):
                scalar.wait_ge(sm.q_pe2, qt + 1)
                if qt >= 2:
                    scalar.wait_ge(sm.q_pe, 2 * (qt - 2) + 2)
                scalar.activation(BB2[:, (qt % 2) * 2048:(qt % 2) * 2048 + 2048],
                                  PSBb[:, 0:2048], AF.Identity)
                scalar.activation(DUMW[:, 128:256], IDB[:, 0:128], AF.Identity).then_inc(sm.q_ac, 1)
                scalar.wait_ge(sm.q_pe, 2 * qt + 1)
                if qt >= 1:
                    scalar.wait_ge(sm.q_dv, 4 * (qt - 1) + 4)
                scalar.activation(FB3[:, :], PSA[:, :], AF.Identity)
                scalar.activation(DUMW[:, 128:256], IDB[:, 0:128], AF.Identity).then_inc(sm.q_ac, 1)
                scalar.wait_ge(sm.q_dv, 4 * qt + 1)
                scalar.activation(COLC[:, 0:16], COLA[:, 0:16], AF.Sqrt,
                                  bias=EPSC[:, 0:1], scale=1.0 / D)
                scalar.activation(DUM[:, 0:16], COLC[:, 0:16], AF.Sqrt
                                  ).then_inc(sm.q_ac, 1)
                scalar.wait_ge(sm.q_dv, 4 * qt + 4)
                if DEBUG and qt == 0:
                    scalar.dma_start(out=qf_dbg[:, :], in_=FB3[:, :]
                                     ).then_inc(sm.c_ld, 16)
                    scalar.dma_start(out=col_dbg[:, 0:16], in_=COLA[:, :]
                                     ).then_inc(sm.c_ld, 16)
                    scalar.dma_start(out=col_dbg[:, 16:32], in_=COLB[:, :]
                                     ).then_inc(sm.c_ld, 16)
                    scalar.dma_start(out=col_dbg[:, 32:48], in_=COLC[:, :]
                                     ).then_inc(sm.c_ld, 16)
                    scalar.dma_start(out=qn_dbg[:, :], in_=BB3[:, 0:2048]
                                     ).then_inc(sm.c_ld, 16)
                    scalar.dma_start(out=qn2_dbg[:, :], in_=BB2[:, 0:2048]
                                     ).then_inc(sm.c_ld, 16)
                    scalar.dma_start(out=qe_dbg[:, :],
                                     in_=FB3[:, :].bitcast(BF16)[:, 0:2048]
                                     ).then_inc(sm.c_ld, 16)
                    scalar.wait_ge(sm.c_ld, 64 + 7 * 16)
                scalar.dma_start(
                    out=bass.AP(qt_dram.tensor, qt * 128,
                                [[TL, 128], [128 * TL, 16], [1, 128]]),
                    in_=BB3[:, 0:2048].rearrange("p (h j) -> p h j", h=16)
                ).then_inc(sm.q_st, 16)
            for i in range(16):
                scalar.wait_ge(WG[i % 2], 16 * (i // 2 + 1))
                scalar.dma_start(out=wo_bf[i * 128:(i + 1) * 128, :],
                                 in_=P32B[:, (i % 2) * 2048:(i % 2) * 2048 + 2048]
                                 ).then_inc(WS[i % 2], 16)
            for tt in range(32):
                scalar.wait_ge(sm.k_pe2, tt + 1)
                if tt >= 2:
                    scalar.wait_ge(sm.k_pe, 2 * (tt - 2) + 2)
                scalar.activation(BB2[:, (tt % 2) * 2048:(tt % 2) * 2048 + 2048],
                                  PSBb[:, 0:2048], AF.Identity)
                scalar.activation(DUMW[:, 128:256], IDB[:, 0:128], AF.Identity).then_inc(sm.k_ac, 1)
                scalar.wait_ge(sm.k_pe, 2 * tt + 1)
                if tt >= 1:
                    scalar.wait_ge(sm.k_dv, 4 * (tt - 1) + 1)
                scalar.activation(FB3[:, 0:512], PSA[:, 0:512], AF.Identity)
                scalar.activation(DUMW[:, 128:256], IDB[:, 0:128], AF.Identity).then_inc(sm.k_ac, 1)
                scalar.activation(VS[:, tt * 512:(tt + 1) * 512], PSA[:, 512:1024],
                                  AF.Identity)
                scalar.activation(DUMW[:, 128:256], IDB[:, 0:128], AF.Identity).then_inc(sm.k_ac, 1)
                scalar.wait_ge(sm.k_dv, 4 * tt + 1)
                scalar.activation(COLC[:, 0:4], COLA[:, 0:4], AF.Sqrt,
                                  bias=EPSC[:, 0:1], scale=1.0 / D)
                scalar.activation(DUM[:, 0:4], COLC[:, 0:4], AF.Sqrt
                                  ).then_inc(sm.k_ac, 1)
            for u in A3S:
                i, sl, h = u["i"], u["sl"], u["h"]
                if h == 0 and sl > 0:
                    emit_a4_scalar(scalar, sl - 1)
                scalar.wait_ge(sm.a_pr, i + 1)
                scalar.activation(NEGF[0:1, 0:512], PSB[0:1, 1024:1536],
                                  AF.Identity)
                scalar.activation(DUMW[:, 128:256], IDB[:, 0:128], AF.Identity).then_inc(sm.a_an, 1)
                n2 = u["n2_0"]
                for kt in range(KT128[sl]):
                    scalar.wait_ge(sm.a_p2, n2 + 1)
                    if kt >= KT128[sl] - 8:
                        jj = kt - (KT128[sl] - 8)
                        cnt = 8 * i + jj + 1
                        scalar.wait_ge(sm.a_d2, cnt)
                        scalar.activation(
                            PTB[:, (n2 % 2) * 512:(n2 % 2) * 512 + 512],
                            PSA[:, 1024 + (n2 % 2) * 512:
                                1024 + (n2 % 2) * 512 + 512],
                            AF.Exp).then_inc(sm.a_ae, 1)
                    else:
                        scalar.activation(
                            PTB[:, (n2 % 2) * 512:(n2 % 2) * 512 + 512],
                            PSA[:, 1024 + (n2 % 2) * 512:
                                1024 + (n2 % 2) * 512 + 512],
                            AF.Exp).then_inc(sm.a_ae, 1)
                    n2 += 1
                if DEBUG and i == 0:
                    scalar.wait_ge(sm.a_pe, i + 1)
                    scalar.wait_ge(sm.a_dm, 1)
                    scalar.wait_ge(sm.a_dn, 1)
                    scalar.wait_ge(sm.a_df, 1)
                    scalar.dma_start(out=ofn_dbg[:, :], in_=OFN[:, 0:640]
                                     ).then_inc(sm.c_ld, 16)
                    scalar.dma_start(out=mcb_dbg[:, 0:4], in_=MCOL[:, :]
                                     ).then_inc(sm.c_ld, 16)
                    scalar.dma_start(out=negm_dbg[:, :], in_=NEGM[0:1, :]
                                     ).then_inc(sm.c_ld, 16)
                    for dq in range(4):
                        scalar.dma_start(
                            out=os_dbg[:, dq * 128:(dq + 1) * 128],
                            in_=OS[:, dq * 2048:dq * 2048 + 128]
                        ).then_inc(sm.c_ld, 16)
                    scalar.wait_ge(sm.c_ld, 64 + 14 * 16)
            emit_a4_scalar(scalar, 3)
            for s in range(16):
                for sub in range(8):
                    for tq in range(4):
                        k = (s * 8 + sub) * 4 + tq
                        scalar.wait_ge(sm.f_pe, k + 1)
                        if k >= 1:
                            scalar.wait_ge(sm.f_dv, k)
                        gb = PSA[:, (tq % 2) * 512:(tq % 2) * 512 + 512]
                        ub = PSA[:, 1024 + (tq % 2) * 512:
                                 1024 + (tq % 2) * 512 + 512]
                        scalar.activation(FB3[:, 0:512], gb,
                                          AF.Identity).then_inc(sm.f_ac, 1)
                        scalar.activation(FB3[:, 512:1024], ub,
                                          AF.Identity).then_inc(sm.f_ac, 1)
                        scalar.wait_ge(sm.f_tg, k + 1)
                        scalar.activation(FB2[:, 512:1024], FB2[:, 0:512],
                                          AF.Tanh).then_inc(sm.f_th, 1)
                for tg in range(16):
                    it = s * 16 + tg
                    scalar.wait_ge(sm.f_pe2, it + 1)
                    accb = PSA if tg % 2 == 0 else PSB
                    stg = FB4 if tg % 2 == 0 else FB1
                    if it >= 2:
                        scalar.wait_ge(FG[it % 2], 16 * (it // 2))
                    scalar.activation(stg[:, :], accb[:, :],
                                      AF.Identity).then_inc(sm.f_ac2, 1)
            for tg in range(16):
                scalar.wait_ge(sm.l_dv, 2 * tg + 1)
                scalar.activation(COLC[:, 0:1], COLA[:, 5:6], AF.Sqrt,
                                  bias=EPSC[:, 1:2], scale=1.0)
                scalar.activation(DUM[:, 0:1], COLC[:, 0:1], AF.Sqrt
                                  ).then_inc(sm.l_ac, 1)
                scalar.wait_ge(sm.l_dv, 2 * tg + 2)
                scalar.dma_start(out=y2out[tg * 128:(tg + 1) * 128, :],
                                 in_=FB3[:, :]).then_inc(sm.l_st, 16)
            scalar.wait_ge(sm.l_st, 16 * 16)
            if DEBUG:
                scalar.dma_start(out=it_dbg[:, :], in_=INTERT[:, :]
                                 ).then_inc(sm.c_ld, 16)
                scalar.dma_start(out=y1t_dbg[:, :], in_=P64A[:, :]
                                 ).then_inc(sm.c_ld, 16)
                scalar.dma_start(out=y1td_dbg[:, :], in_=y1t_dram[:, :]
                                 ).then_inc(sm.c_ld, 16)
                scalar.wait_ge(sm.c_ld, 64 + 19 * 16)

        @block.vector
        def _(vector):
            vector.memset(ONES1[:, :], 1.0)
            vector.memset(ONESC[:, :], 1.0)
            vector.memset(EPSC[:, 0:1], EPS_RMS)
            vector.memset(EPSC[:, 1:2], EPS_LN)
            vector.wait_ge(sm.c_ld, 64)
            vector.tensor_scalar(GQS[:, :], GQC[:, :], QK_SCALE, None, ALU.mult)
            for j in range(NDC):
                vector.wait_ge(DL[j % 2], 32 * (j // 2 + 1))
                vector.wait_ge(sm.d_ac, j + 1)
                if j >= 1:
                    vector.wait_ge(sm.d_sc, 32 * j)
                cb = CSD[:, (j % 2) * 2048:(j % 2) * 2048 + 2048]
                nx = DNSX[:, (j % 2) * 2048:(j % 2) * 2048 + 2048]
                vector.tensor_scalar(DI1[:, :], cb, 15, None, ALU.bitwise_and)
                vector.tensor_scalar(DI2[:, :], cb, 4, None,
                                     ALU.logical_shift_right)
                vector.tensor_scalar(DI3[:, :], cb, 3, 1,
                                     ALU.logical_shift_right, ALU.bitwise_and)
                vector.tensor_scalar(DI4[:, :], cb, 7, 1,
                                     ALU.logical_shift_right, ALU.bitwise_and)
                for half, (xx, ss, dst) in enumerate(
                        ((DI1, DI3, DLO), (DI2, DI4, DHI))):
                    sl2 = slice(half * 2048, half * 2048 + 2048)
                    vector.tensor_scalar(DT1[:, :], xx[:, :], 64.0, 16128.0,
                                         ALU.mult, ALU.add)
                    vector.scalar_tensor_tensor(DBITS[:, sl2], ss[:, :],
                                                -33280.0, DT1[:, :],
                                                ALU.mult, ALU.add)
                    uu = DBITS[:, sl2].bitcast(BF16)
                    vector.tensor_scalar(DC1[:, :], uu, 1.0, -1.0,
                                         ALU.min, ALU.max)
                    vector.tensor_scalar(DC2[:, :], uu, 2.0, 1.0,
                                         ALU.mult, ALU.min)
                    vector.tensor_tensor(DT2[:, :], uu, DC1[:, :], ALU.add)
                    vector.scalar_tensor_tensor(DT3[:, :], DC2[:, :], -1.0,
                                                DT2[:, :], ALU.max,
                                                ALU.subtract)
                    if half == 0:
                        vector.tensor_tensor(dst[:, :], DT3[:, :], nx, ALU.mult)
                    else:
                        vector.tensor_tensor(dst[:, :], DT3[:, :], nx, ALU.mult
                                             ).then_inc(sm.d_dv, 1)
            for qt in range(16):
                vector.wait_ge(sm.q_ac, 3 * qt + 2)
                for hh in range(16):
                    op = vector.scalar_tensor_tensor(
                        FB2[:, hh * 128:(hh + 1) * 128],
                        FB3[:, hh * 128:(hh + 1) * 128], 0.0,
                        FB3[:, hh * 128:(hh + 1) * 128],
                        ALU.add, ALU.mult, accum_out=COLA[:, hh:hh + 1])
                vector.tensor_copy(out=DUM[:, 16:32], in_=COLA[:, 0:16]
                                   ).then_inc(sm.q_dv, 1)
                vector.wait_ge(sm.q_ac, 3 * qt + 3)
                vector.reciprocal(COLB[:, 0:16], COLC[:, 0:16]).then_inc(sm.q_dv, 1)
                vector.tensor_copy(out=DUM[:, 32:48], in_=COLB[:, 0:16])
                vector.tensor_copy(out=DUM[:, 48:64], in_=DUM[:, 32:48])
                xt = BB2[:, (qt % 2) * 2048:(qt % 2) * 2048 + 2048]
                for hh in range(16):
                    vector.tensor_scalar(xt[:, hh * 128:(hh + 1) * 128],
                                         FB3[:, hh * 128:(hh + 1) * 128],
                                         COLB[:, hh:hh + 1], None, ALU.mult)
                vector.tensor_copy(out=DUMW[:, 0:128], in_=IDB[:, 0:128]).then_inc(sm.q_dv, 1)
                vector.wait_ge(sm.q_pe, 2 * qt + 2)
                if qt >= 1:
                    vector.wait_ge(sm.q_st, 16 * qt)
                vector.tensor_scalar(BB3[:, 0:2048],
                                     PSBb[:, 2048:4096],
                                     GQS[:, 0:1], None, ALU.mult)
                vector.tensor_copy(out=DUMW[:, 0:128], in_=IDB[:, 0:128]).then_inc(sm.q_dv, 1)
            vector.wait_ge(sm.d_sc, 16 * 48)
            for tt in range(32):
                vector.wait_ge(sm.k_ac, 4 * tt + 2)
                for sgi in range(4):
                    op = vector.scalar_tensor_tensor(
                        FB2[:, sgi * 128:(sgi + 1) * 128],
                        FB3[:, sgi * 128:(sgi + 1) * 128], 0.0,
                        FB3[:, sgi * 128:(sgi + 1) * 128],
                        ALU.add, ALU.mult, accum_out=COLA[:, sgi:sgi + 1])
                vector.tensor_copy(out=DUM[:, 16:20], in_=COLA[:, 0:4]
                                   ).then_inc(sm.k_dv, 1)
                vector.wait_ge(sm.k_ac, 4 * tt + 4)
                vector.reciprocal(COLB[:, 0:4], COLC[:, 0:4]).then_inc(sm.k_dv, 1)
                vector.tensor_copy(out=DUM[:, 32:36], in_=COLB[:, 0:4])
                vector.tensor_copy(out=DUM[:, 48:52], in_=DUM[:, 32:36])
                xt = BB2[:, (tt % 2) * 2048:(tt % 2) * 2048 + 2048]
                for sgi in range(4):
                    vector.tensor_scalar(xt[:, sgi * 128:(sgi + 1) * 128],
                                         FB3[:, sgi * 128:(sgi + 1) * 128],
                                         COLB[:, sgi:sgi + 1], None,
                                         ALU.mult)
                vector.tensor_copy(out=DUMW[:, 0:128], in_=IDB[:, 0:128]).then_inc(sm.k_dv, 1)
                vector.wait_ge(sm.k_pe, 2 * tt + 2)
                for sgi in range(4):
                    vector.tensor_scalar(
                        KTS[:, sgi * 4096 + tt * 128:sgi * 4096 + (tt + 1) * 128],
                        PSBb[:, 2048 + sgi * 128:2048 + (sgi + 1) * 128],
                        GKC[:, 0:1], None, ALU.mult)
                vector.tensor_copy(out=DUMW[:, 0:128], in_=IDB[:, 0:128]).then_inc(sm.k_dv, 1)
            for u in A3S:
                i, sl, h = u["i"], u["sl"], u["h"]
                if h == 0 and sl > 0:
                    emit_a4_vector(vector, sl - 1)
                n1 = u["n1_0"]
                for qt in range(4):
                    for k5 in range(KT512[sl]):
                        vector.wait_ge(sm.a_p1, n1 + 1)
                        bank = PSA[:, (n1 % 2) * 512:(n1 % 2) * 512 + 512]
                        if k5 >= KT512[sl] - 2:
                            jj = k5 - (KT512[sl] - 2)
                            vector.tensor_tensor(
                                bank, bank,
                                M1S[:, (jj * 4 + qt) * 512:
                                    (jj * 4 + qt + 1) * 512], ALU.add)
                        vector.reduce_max(COLB[:, k5:k5 + 1], bank,
                                          axis=AX.X).then_inc(sm.a_d1, 1)
                        n1 += 1
                    vector.tensor_copy(out=DUM[:, 32:48], in_=IDB[:, 0:16])
                    vector.tensor_copy(out=DUM[:, 48:64], in_=IDB[:, 16:32])
                    vector.reduce_max(MCOL[:, qt:qt + 1],
                                      COLB[:, 0:KT512[sl]], axis=AX.X)
                vector.tensor_copy(out=DUM[:, 32:48], in_=IDB[:, 0:16])
                vector.tensor_copy(out=DUM[:, 48:64], in_=IDB[:, 16:32])
                vector.tensor_copy(out=MCB[:, :], in_=MCOL[:, :])
                vector.tensor_copy(out=DUM[:, 16:20], in_=MCB[:, 0:4])
                vector.tensor_copy(out=DUMW[:, 0:128], in_=IDB[:, 0:128]).then_inc(sm.a_dm, 1)
                vector.wait_ge(sm.a_an, i + 1)
                vector.tensor_scalar(NEGM[0:1, :], NEGF[0:1, 0:512], -1.0,
                                     None, ALU.mult)
                vector.tensor_copy(out=DUMW[:, 0:128], in_=IDB[:, 0:128]).then_inc(sm.a_dn, 1)
                n2 = u["n2_0"]
                for kt in range(KT128[sl]):
                    if kt >= KT128[sl] - 8:
                        jj = kt - (KT128[sl] - 8)
                        cnt = 8 * i + jj + 1
                        vector.wait_ge(sm.a_p2, n2 + 1)
                        bank = PSA[:, 1024 + (n2 % 2) * 512:
                                   1024 + (n2 % 2) * 512 + 512]
                        vector.tensor_tensor(bank, bank,
                                             M2S[:, jj * 512:(jj + 1) * 512],
                                             ALU.add)
                        vector.tensor_copy(out=DUMW[:, 0:128], in_=IDB[:, 0:128]).then_inc(sm.a_d2, 1)
                    n2 += 1
                vector.wait_ge(sm.a_pe, i + 1)
                for qt in range(4):
                    vector.reciprocal(RC[:, qt:qt + 1],
                                      PSB[:, qt * 512 + 128:qt * 512 + 129])
                vector.tensor_copy(out=DUM[:, 32:36], in_=RC[:, 0:4])
                vector.tensor_copy(out=DUM[:, 48:52], in_=DUM[:, 32:36])
                for qt in range(4):
                    vector.tensor_scalar(
                        OS[:, qt * 2048 + h * 128:qt * 2048 + (h + 1) * 128],
                        PSB[:, qt * 512:qt * 512 + 128],
                        RC[:, qt:qt + 1], None, ALU.mult)
                vector.tensor_copy(out=DUMW[:, 0:128], in_=IDB[:, 0:128]).then_inc(sm.a_df, 1)
            emit_a4_vector(vector, 3)
            for s in range(16):
                for sub in range(8):
                    for tq in range(4):
                        k = (s * 8 + sub) * 4 + tq
                        vector.wait_ge(sm.f_ac, 2 * k + 2)
                        if s > 0 and sub == 0 and tq == 0:
                            vector.wait_ge(sm.f_pe2, 16 * s)
                        gf = FB3[:, 0:512]
                        upf = FB3[:, 512:1024]
                        g2 = FB3[:, 1024:1536]
                        u1 = FB3[:, 1536:2048]
                        targ = FB2[:, 0:512]
                        th = FB2[:, 512:1024]
                        vv = FB2[:, 1024:1536]
                        vector.tensor_tensor(g2, gf, gf, ALU.mult)
                        vector.tensor_scalar(u1, g2, G_C1, G_C2,
                                             ALU.mult, ALU.add)
                        vector.tensor_tensor(targ, u1, gf, ALU.mult
                                             ).then_inc(sm.f_tg, 1)
                        vector.wait_ge(sm.f_th, k + 1)
                        vector.scalar_tensor_tensor(vv, th, 1.0, gf,
                                                    ALU.add, ALU.mult)
                        op = vector.scalar_tensor_tensor(
                            INTERT[:, sub * 2048 + tq * 512:
                                   sub * 2048 + (tq + 1) * 512],
                            vv, 0.5, upf, ALU.mult, ALU.mult)
                        if sub == 7 and tq == 3:
                            op = vector.tensor_copy(out=DUMW[:, 0:128], in_=IDB[:, 0:128])
                        op.then_inc(sm.f_dv, 1)
            for tg in range(16):
                vector.wait_ge(sm.l_ld, 32 * (tg + 1))
                if tg >= 1:
                    vector.wait_ge(sm.l_st, 16 * tg)
                vector.scalar_tensor_tensor(FB4[:, :], FB1[:, :], 0.0,
                                            FB2[:, :], ALU.add, ALU.add,
                                            accum_out=COLA[:, 0:1])
                vector.scalar_tensor_tensor(FB3[:, :], FB4[:, :], 0.0,
                                            FB4[:, :], ALU.add, ALU.mult,
                                            accum_out=COLA[:, 1:2])
                vector.tensor_scalar(COLA[:, 2:3], COLA[:, 0:1], 1.0 / HB,
                                     None, ALU.mult)
                vector.tensor_copy(out=DUM[:, 32:48], in_=IDB[:, 0:16])
                vector.tensor_copy(out=DUM[:, 48:64], in_=IDB[:, 16:32])
                vector.tensor_tensor(COLA[:, 3:4], COLA[:, 2:3], COLA[:, 2:3],
                                     ALU.mult)
                vector.tensor_scalar(COLA[:, 4:5], COLA[:, 1:2], 1.0 / HB,
                                     None, ALU.mult)
                vector.tensor_copy(out=DUM[:, 32:48], in_=IDB[:, 0:16])
                vector.tensor_copy(out=DUM[:, 48:64], in_=IDB[:, 16:32])
                vector.tensor_tensor(COLA[:, 5:6], COLA[:, 4:5], COLA[:, 3:4],
                                     ALU.subtract)
                vector.tensor_copy(out=DUM[:, 16:17], in_=COLA[:, 5:6]
                                   ).then_inc(sm.l_dv, 1)
                vector.wait_ge(sm.l_ac, tg + 1)
                vector.reciprocal(COLB[:, 0:1], COLC[:, 0:1])
                vector.tensor_copy(out=DUM[:, 32:33], in_=COLB[:, 0:1])
                vector.tensor_copy(out=DUM[:, 48:49], in_=DUM[:, 32:33])
                vector.tensor_scalar(FB3[:, :], FB4[:, :], COLA[:, 2:3],
                                     COLB[:, 0:1], ALU.subtract, ALU.mult)
                vector.tensor_copy(out=DUMW[:, 0:128], in_=IDB[:, 0:128]).then_inc(sm.l_dv, 1)

    stack.close()
    return nc


def _build_masks(sub):
    m1 = np.zeros((4, 128, 8 * 512), np.float32)
    m2 = np.zeros((4, 128, 8 * 512), np.float32)
    for sl in range(4):
        P = SLOT_CHUNKS[sub][sl]
        for j in range(2):
            k5 = KT512[sl] - 2 + j
            for qt in range(4):
                q = np.arange(128)[:, None] + qt * 128 + P * 512
                k = np.arange(512)[None, :] + k5 * 512
                m1[sl][:, (j * 4 + qt) * 512:(j * 4 + qt + 1) * 512] = \
                    np.where(k <= q, 0.0, NEGV)
        for j in range(8):
            kt = KT128[sl] - 8 + j
            k = np.arange(128)[:, None] + kt * 128
            q = np.arange(512)[None, :] + P * 512
            m2[sl][:, j * 512:(j + 1) * 512] = np.where(k <= q, 0.0, NEGV)
    m1 = m1.reshape(4 * 128, 8 * 512).astype(ml_dtypes.bfloat16)
    m2 = m2.reshape(4 * 128, 8 * 512).astype(ml_dtypes.bfloat16)
    return m1, m2


_NC_CACHE = [None]


def kernel(L0_attn_x, L0_attn_Wq, L0_attn_Wk, L0_attn_Wv, L0_attn_Wo,
           L0_attn_gq, L0_attn_gk, L0_ffn_gate_W_packed, L0_ffn_gate_scales,
           L0_ffn_up_W_packed, L0_ffn_up_scales, L0_ffn_down_W_packed,
           L0_ffn_down_scales):
    x = np.ascontiguousarray(np.asarray(L0_attn_x, dtype=np.float32))
    Wq = np.ascontiguousarray(np.asarray(L0_attn_Wq, dtype=np.float32))
    Wk = np.asarray(L0_attn_Wk, dtype=np.float32)
    Wv = np.asarray(L0_attn_Wv, dtype=np.float32)
    Wkv = np.ascontiguousarray(np.concatenate([Wk, Wv], axis=0))
    Wo = np.ascontiguousarray(np.asarray(L0_attn_Wo, dtype=np.float32))
    gq = np.ascontiguousarray(np.asarray(L0_attn_gq, dtype=np.float32
                                         ).reshape(D, 1))
    gk = np.ascontiguousarray(np.asarray(L0_attn_gk, dtype=np.float32
                                         ).reshape(D, 1))
    gW = np.ascontiguousarray(np.asarray(L0_ffn_gate_W_packed, np.int32).T)
    uW = np.ascontiguousarray(np.asarray(L0_ffn_up_W_packed, np.int32).T)
    dW = np.ascontiguousarray(np.asarray(L0_ffn_down_W_packed, np.int32).T)
    gS = np.ascontiguousarray(np.asarray(L0_ffn_gate_scales, np.float32).T)
    uS = np.ascontiguousarray(np.asarray(L0_ffn_up_scales, np.float32).T)
    dS = np.ascontiguousarray(np.asarray(L0_ffn_down_scales, np.float32).T)
    e8v = np.zeros((8, 128), np.float32)
    for jj in range(8):
        e8v[jj, jj * 16:(jj + 1) * 16] = -1.0
    identv = np.eye(128, dtype=np.float32).astype(ml_dtypes.bfloat16)

    if _NC_CACHE[0] is None:
        _NC_CACHE[0] = build_nc()
    nc = _NC_CACHE[0]
    in_maps = []
    for c in range(8):
        b, sub = c // 2, c % 2
        chunks = SLOT_CHUNKS[sub]
        xqv = np.ascontiguousarray(
            np.concatenate([x[b, ch * 512:(ch + 1) * 512] for ch in chunks], 0))
        m1, m2 = _build_masks(sub)
        in_maps.append({
            "xb": np.ascontiguousarray(x[b]),
            "xq": xqv,
            "wq": Wq, "wkv": Wkv, "wo": Wo, "gqv": gq, "gkv": gk,
            "gsh": np.ascontiguousarray(gW[c * 128:(c + 1) * 128]),
            "ush": np.ascontiguousarray(uW[c * 128:(c + 1) * 128]),
            "dsh": np.ascontiguousarray(dW[c * 1024:(c + 1) * 1024]),
            "gsc": np.ascontiguousarray(gS[c * 8:(c + 1) * 8]),
            "usc": np.ascontiguousarray(uS[c * 8:(c + 1) * 8]),
            "dsc": np.ascontiguousarray(dS[c * 64:(c + 1) * 64]),
            "e8": e8v, "identin": identv, "m1in": m1, "m2in": m2,
        })
    res = run_bass_kernel_spmd(nc, in_maps, core_ids=list(range(8)))
    out = np.empty((4, S, HB), np.float32)
    for c in range(8):
        b, sub = c // 2, c % 2
        y2 = res.results[c]["y2out"]
        for sl, ch in enumerate(SLOT_CHUNKS[sub]):
            out[b, ch * 512:(ch + 1) * 512] = y2[sl * 512:(sl + 1) * 512]
    return out


# version 10
# speedup vs baseline: 1.1707x; 1.1099x over previous
import numpy as np
import ml_dtypes
import concourse.bass as bass
import concourse.mybir as mybir
from concourse.bass_utils import run_bass_kernel_spmd

F32 = mybir.dt.float32
BF16 = mybir.dt.bfloat16
I32 = mybir.dt.int32
I16 = mybir.dt.int16
ALU = mybir.AluOpType
AF = mybir.ActivationFunctionType
AX = mybir.AxisListType

HB = 2048
S = 4096
TL = 2048
NH = 16
KVH = 4
D = 128
MLP = 16384
EPS_RMS = 1e-6
EPS_LN = 1e-5
QK_SCALE = float(D) ** -0.5
G_C1 = 0.044715 * 0.7978845608028654
G_C2 = 0.7978845608028654
KT512 = [8, 6, 4, 2]
KT128 = [32, 24, 16, 8]
SLOT_CHUNKS = {0: [7, 4, 3, 0], 1: [6, 5, 2, 1]}
NEGV = -60000.0
DEBUG = False

DCH = ([("g", 0, j) for j in range(8)] + [("u", 0, j) for j in range(8)]
       + [("d", t, 0) for t in range(8)])
NDC = len(DCH)
A3L = [(sl, h) for sl in range(4) for h in range(16)]


def _a3_sched():
    n1 = n2 = 0
    out = []
    for i, (sl, h) in enumerate(A3L):
        u = {"i": i, "sl": sl, "h": h, "n1_0": n1, "n2_0": n2}
        n1 += 4 * KT512[sl]
        n2 += KT128[sl]
        u["n1_1"], u["n2_1"] = n1, n2
        out.append(u)
    return out


A3S = _a3_sched()

A4_BASE = 40


def _a4_blk_cnt(sl, pr, ks):
    return sl * A4_BASE + 4 + pr * 18 + ks + 1


def _a4_y1t_cnt(sl, qt):
    return sl * A4_BASE + 4 + (qt // 2) * 18 + 16 + (qt % 2) + 1


def build_nc():
    nc = bass.Bass("TRN2", target_bir_lowering=False,
                   detect_race_conditions=False)
    xb = nc.dram_tensor("xb", [S, HB], F32, kind="ExternalInput").ap()
    xq = nc.dram_tensor("xq", [TL, HB], F32, kind="ExternalInput").ap()
    wq = nc.dram_tensor("wq", [HB, NH * D], F32, kind="ExternalInput").ap()
    wkv = nc.dram_tensor("wkv", [2 * HB, KVH * D], F32, kind="ExternalInput").ap()
    wo = nc.dram_tensor("wo", [NH * D, HB], F32, kind="ExternalInput").ap()
    gqv = nc.dram_tensor("gqv", [D, 1], F32, kind="ExternalInput").ap()
    gkv = nc.dram_tensor("gkv", [D, 1], F32, kind="ExternalInput").ap()
    gsh = nc.dram_tensor("gsh", [128, MLP], I32, kind="ExternalInput").ap()
    ush = nc.dram_tensor("ush", [128, MLP], I32, kind="ExternalInput").ap()
    dsh = nc.dram_tensor("dsh", [1024, HB], I32, kind="ExternalInput").ap()
    gsc = nc.dram_tensor("gsc", [8, MLP], F32, kind="ExternalInput").ap()
    usc = nc.dram_tensor("usc", [8, MLP], F32, kind="ExternalInput").ap()
    dsc = nc.dram_tensor("dsc", [64, HB], F32, kind="ExternalInput").ap()
    e8 = nc.dram_tensor("e8", [8, 128], F32, kind="ExternalInput").ap()
    identin = nc.dram_tensor("identin", [128, 128], BF16, kind="ExternalInput").ap()
    m1in = nc.dram_tensor("m1in", [4 * 128, 8 * 512], BF16, kind="ExternalInput").ap()
    m2in = nc.dram_tensor("m2in", [4 * 128, 8 * 512], BF16, kind="ExternalInput").ap()
    y2out = nc.dram_tensor("y2out", [TL, HB], F32, kind="ExternalOutput").ap()
    if DEBUG:
        y1d_out = nc.dram_tensor("y1d_out", [TL, HB], F32, kind="ExternalOutput").ap()
        qt_out = nc.dram_tensor("qt_out", [NH * D, TL], BF16, kind="ExternalOutput").ap()
        gg_out = nc.dram_tensor("gg_out", [HB, MLP], BF16, kind="ExternalOutput").ap()
        gd_out = nc.dram_tensor("gd_out", [MLP, HB], BF16, kind="ExternalOutput").ap()
        pt_out = nc.dram_tensor("pt_out", [TL, HB], F32, kind="ExternalOutput").ap()
        qf_dbg = nc.dram_tensor("qf_dbg", [128, HB], F32, kind="ExternalOutput").ap()
        qn2_dbg = nc.dram_tensor("qn2_dbg", [128, HB], BF16, kind="ExternalOutput").ap()
        ofn_dbg = nc.dram_tensor("ofn_dbg", [128, 640], F32, kind="ExternalOutput").ap()
        os_dbg = nc.dram_tensor("os_dbg", [128, 512], BF16, kind="ExternalOutput").ap()
        mcb_dbg = nc.dram_tensor("mcb_dbg", [128, 8], F32, kind="ExternalOutput").ap()
        negm_dbg = nc.dram_tensor("negm_dbg", [1, 512], BF16, kind="ExternalOutput").ap()
        a4c_dbg = nc.dram_tensor("a4c_dbg", [128, 8], F32, kind="ExternalOutput").ap()
        fb2_dbg = nc.dram_tensor("fb2_dbg", [128, HB], F32, kind="ExternalOutput").ap()
        it_dbg = nc.dram_tensor("it_dbg", [128, 16384], BF16, kind="ExternalOutput").ap()
        y1t_dbg = nc.dram_tensor("y1t_dbg", [128, 32768], BF16, kind="ExternalOutput").ap()
        y1td_dbg = nc.dram_tensor("y1td_dbg", [2048, 2048], BF16, kind="ExternalOutput").ap()
        qe_dbg = nc.dram_tensor("qe_dbg", [128, HB], BF16, kind="ExternalOutput").ap()
        col_dbg = nc.dram_tensor("col_dbg", [128, 48], F32, kind="ExternalOutput").ap()
        qn_dbg = nc.dram_tensor("qn_dbg", [128, HB], BF16, kind="ExternalOutput").ap()

    wo_bf = nc.dram_tensor("wo_bf", [NH * D, HB], BF16, kind="Internal").ap()
    qt_dram = nc.dram_tensor("qt_dram", [NH * D, TL], BF16, kind="Internal").ap()
    y1_dram = nc.dram_tensor("y1_dram", [TL, HB], F32, kind="Internal").ap()
    y1t_dram = nc.dram_tensor("y1t_dram", [HB, TL], BF16, kind="Internal").ap()
    part = nc.dram_tensor("part", [TL, HB], F32, kind="Internal").ap()
    pg = nc.dram_tensor("pg", [256, MLP], BF16, kind="Internal").ap()
    pu = nc.dram_tensor("pu", [256, MLP], BF16, kind="Internal").ap()
    pd = nc.dram_tensor("pd", [2048, HB], BF16, kind="Internal").ap()
    gg = nc.dram_tensor("gg", [HB, MLP], BF16, kind="Internal").ap()
    gu = nc.dram_tensor("gu", [HB, MLP], BF16, kind="Internal").ap()
    gd = nc.dram_tensor("gd", [MLP, HB], BF16, kind="Internal").ap()

    import contextlib
    stack = contextlib.ExitStack()

    def sbt(name, shape, dt):
        return stack.enter_context(nc.sbuf_tensor(name, shape, dt))

    P64A = sbt("P64A", [128, 32768], BF16)
    P32A = sbt("P32A", [128, 16384], BF16)
    P32B = sbt("P32B", [128, 16384], BF16)
    MISC = sbt("MISC", [128, 8192], BF16)
    FB1 = sbt("FB1", [128, 2048], F32)
    FB2 = sbt("FB2", [128, 2048], F32)
    FB3 = sbt("FB3", [128, 2048], F32)
    FB4 = sbt("FB4", [128, 2048], F32)
    BB1 = sbt("BB1", [128, 4096], BF16)
    BB2 = sbt("BB2", [128, 4096], BF16)
    BB3 = sbt("BB3", [128, 4096], BF16)
    E8S = sbt("E8S", [8, 128], F32)
    IDB = sbt("IDB", [128, 128], BF16)
    ONES1 = sbt("ONES1", [1, 128], BF16)
    ONESC = sbt("ONESC", [128, 1], BF16)
    GQC = sbt("GQC", [128, 1], F32)
    GKC = sbt("GKC", [128, 1], F32)
    GQS = sbt("GQS", [128, 1], F32)
    COLA = sbt("COLA", [128, 16], F32)
    COLB = sbt("COLB", [128, 16], F32)
    COLC = sbt("COLC", [128, 16], F32)
    MCOL = sbt("MCOL", [128, 4], F32)
    MCB = sbt("MCB", [128, 4], BF16)
    DUM = sbt("DUM", [128, 64], F32)
    DUMW = sbt("DUMW", [128, 256], F32)
    NEGM = sbt("NEGM", [1, 512], BF16)
    PTB = sbt("PTB", [128, 1024], BF16)
    QTS = sbt("QTS", [128, 1024], BF16)
    RC = sbt("RC", [128, 4], F32)
    EPSC = sbt("EPSC", [128, 2], F32)

    CSD = P64A[:, 0:8192].bitcast(I32)
    DI1 = P64A[:, 8192:12288].bitcast(I32)
    DI2 = P64A[:, 12288:16384].bitcast(I32)
    DI3 = P64A[:, 16384:20480].bitcast(I32)
    DI4 = P64A[:, 20480:24576].bitcast(I32)
    DBITS = P64A[:, 24576:28672].bitcast(I16)
    DT1 = P64A[:, 28672:30720].bitcast(I16)
    DNSX = P32A[:, 0:4096]
    DLO = P32A[:, 4096:6144]
    DHI = P32A[:, 6144:8192]
    DC1 = P32A[:, 8192:10240]
    DC2 = P32A[:, 10240:12288]
    DT2 = P32A[:, 12288:14336]
    DT3 = P32A[:, 14336:16384]
    WQS = P64A
    WKS = P64A[:, 0:8192]
    WVS = P64A[:, 8192:16384]
    KTS = P32A
    VS = P32B
    OS = P64A[:, 16384:24576]
    OTS = P64A[:, 24576:32768]
    M1S = MISC[:, 0:4096]
    M2S = MISC[:, 4096:8192]
    SCL0 = FB1[0:8, 0:2048]
    SCL1 = FB2[0:8, 0:2048]
    PS1F = P64A[:, 0:2048].bitcast(F32)
    NEGF = P64A[0:1, 2048:3072].bitcast(F32)
    OFN = P64A[:, 3072:4352].bitcast(F32)
    PS2F = P64A[:, 4352:6400].bitcast(F32)
    AOF = P64A[:, 8192:16384].bitcast(F32)
    Y1TS = P64A
    INTERT = P32A
    GUBF = MISC
    WD8 = P32B

    PSA = stack.enter_context(nc.psum_tensor("PSA", [128, 2048], F32))
    PSB = stack.enter_context(nc.psum_tensor("PSB", [128, 2048], F32))
    PSAb = PSA[:, :].bitcast(BF16)
    PSBb = PSB[:, :].bitcast(BF16)

    sems = {}
    for n in ("c_ld", "d_l0", "d_l1", "d_sc", "d_pe", "d_ac", "d_dv", "d_cc",
              "w_g0", "w_g1", "w_s0", "w_s1",
              "q_gw", "q_g0", "q_g1", "q_pe", "q_pe2", "q_ac", "q_dv", "q_st",
              "k_gw", "k_g0", "k_g1", "k_pe", "k_pe2", "k_ac", "k_dv",
              "a_lm", "a_l0", "a_l1", "a_p1", "a_a1", "a_d1", "a_dm",
              "a_pr", "a_an", "a_dn", "a_ao",
              "a_p2", "a_e2", "a_d2", "a_ae", "a_pe", "a_av", "a_df",
              "o_l0", "o_l1", "o_lx", "o_pe", "o_oa", "o_ac", "o_dv",
              "o_sy", "o_sz",
              "f_ly", "f_lw", "f_lg0", "f_lg1", "f_pe", "f_pe2",
              "f_ac", "f_ac2", "f_tg", "f_th", "f_dv", "f_g0", "f_g1",
              "l_ld", "l_ac", "l_dv", "l_st"):
        sems[n] = stack.enter_context(nc.semaphore(n))
    sm = type("SM", (), sems)
    DL = [sems["d_l0"], sems["d_l1"]]
    WG = [sems["w_g0"], sems["w_g1"]]
    WS = [sems["w_s0"], sems["w_s1"]]
    QG = [sems["q_g0"], sems["q_g1"]]
    KG = [sems["k_g0"], sems["k_g1"]]
    AL = [sems["a_l0"], sems["a_l1"]]
    OL = [sems["o_l0"], sems["o_l1"]]
    FLG = [sems["f_lg0"], sems["f_lg1"]]
    FG = [sems["f_g0"], sems["f_g1"]]

    with nc.Block() as block:

        def emit_a4_tensor(tensor, sl):
            tensor.wait_ge(sm.a_df, 16 * (sl + 1))
            tensor.wait_ge(sm.a_ae, A3S[16 * (sl + 1) - 1]["n2_1"])
            tensor.wait_ge(sm.a_d1, A3S[16 * (sl + 1) - 1]["n1_1"])
            for r in range(4):
                if r >= 1:
                    tensor.wait_ge(sm.o_oa, sl * 4 + r)
                for jj in range(16):
                    f = r * 16 + jj
                    ks, qt = f // 4, f % 4
                    op = tensor.transpose(PSAb[:, jj * 128:(jj + 1) * 128],
                                          OS[:, qt * 2048 + ks * 128:
                                             qt * 2048 + (ks + 1) * 128],
                                          IDB[:, :])
                op.then_inc(sm.o_pe, 1)
            tensor.wait_ge(sm.o_oa, sl * 4 + 4)
            for pr in range(2):
                if pr == 1:
                    tensor.wait_ge(sm.o_dv, 3 * (sl * 4 + 2))
                    tensor.wait_ge(sm.o_ac, 3 * (sl * 4 + 1) + 3)
                for ks in range(16):
                    it = sl * 32 + pr * 16 + ks
                    tensor.wait_ge(OL[it % 2], 16 * (it // 2 + 1))
                    wob = BB1[:, (it % 2) * 2048:(it % 2) * 2048 + 2048]
                    for qt2 in range(2):
                        qt = pr * 2 + qt2
                        aop = PSA if qt2 == 0 else PSB
                        for hc in range(4):
                            op = tensor.matmul(aop[:, hc * 512:(hc + 1) * 512],
                                               OTS[:, (ks * 4 + qt) * 128:
                                                   (ks * 4 + qt + 1) * 128],
                                               wob[:, hc * 512:(hc + 1) * 512],
                                               start=(ks == 0), stop=(ks == 15))
                    op.then_inc(sm.o_pe, 1)
                for qt2 in range(2):
                    qt = pr * 2 + qt2
                    it2 = sl * 4 + qt
                    tensor.wait_ge(sm.o_dv, 3 * it2 + 3)
                    if it2 >= 2:
                        tensor.wait_ge(sm.o_ac, 3 * (it2 - 2) + 3)
                    y1b = BB3[:, (qt % 2) * 2048:(qt % 2) * 2048 + 2048]
                    dstb = PSAb if qt2 == 0 else PSBb
                    for hh in range(16):
                        op = tensor.transpose(dstb[:, hh * 128:(hh + 1) * 128],
                                              y1b[:, hh * 128:(hh + 1) * 128],
                                              IDB[:, :])
                    op.then_inc(sm.o_pe, 1)

        def emit_a4_scalar(scalar, sl):
            for r in range(4):
                scalar.wait_ge(sm.o_pe, sl * A4_BASE + r + 1)
                op = scalar.activation(OTS[:, r * 2048:(r + 1) * 2048],
                                       PSAb[:, 0:2048], AF.Identity)
                if r == 3:
                    op = scalar.activation(DUMW[:, 128:256], IDB[:, 0:128], AF.Identity)
                op.then_inc(sm.o_oa, 1)
            def y1t_store(j):
                scalar.dma_start(
                    out=bass.AP(y1t_dram.tensor, j * 128,
                                [[TL, 128], [128 * TL, 16], [1, 128]]),
                    in_=BB2[:, (j % 2) * 2048:(j % 2) * 2048 + 2048
                            ].rearrange("p (h j) -> p h j", h=16)
                ).then_inc(sm.o_sz, 16)

            for qt in range(4):
                it2 = sl * 4 + qt
                pr, qt2 = qt // 2, qt % 2
                scalar.wait_ge(sm.o_pe, _a4_blk_cnt(sl, pr, 15))
                if it2 >= 2:
                    scalar.wait_ge(sm.o_dv, 3 * (it2 - 2) + 1)
                aop = PSA if qt2 == 0 else PSB
                scalar.activation(AOF[:, (qt % 2) * 2048:(qt % 2) * 2048 + 2048],
                                  aop[:, :], AF.Identity)
                scalar.activation(DUMW[:, 128:256], IDB[:, 0:128], AF.Identity).then_inc(sm.o_ac, 1)
                if it2 >= 2:
                    scalar.wait_ge(sm.o_sz, 16 * (it2 - 1))
                scalar.wait_ge(sm.o_dv, 3 * it2 + 2)
                if DEBUG and it2 == 0:
                    scalar.dma_start(out=a4c_dbg[:, :], in_=COLA[:, 0:8]
                                     ).then_inc(sm.c_ld, 16)
                    scalar.dma_start(out=fb2_dbg[:, :], in_=FB2[:, :]
                                     ).then_inc(sm.c_ld, 16)
                    scalar.wait_ge(sm.c_ld, 64 + 16 * 16)
                scalar.activation(COLC[:, 0:1], COLA[:, 5:6], AF.Sqrt,
                                  bias=EPSC[:, 1:2], scale=1.0)
                scalar.activation(DUM[:, 0:1], COLC[:, 0:1], AF.Sqrt
                                  ).then_inc(sm.o_ac, 1)
                scalar.wait_ge(sm.o_dv, 3 * it2 + 3)
                scalar.dma_start(out=y1_dram[it2 * 128:(it2 + 1) * 128, :],
                                 in_=FB4[:, :]).then_inc(sm.o_sy, 16)
                scalar.wait_ge(sm.o_pe, _a4_y1t_cnt(sl, qt))
                pb = PSAb if qt % 2 == 0 else PSBb
                scalar.activation(BB2[:, (qt % 2) * 2048:(qt % 2) * 2048 + 2048],
                                  pb[:, 0:2048], AF.Identity).then_inc(sm.o_ac, 1)
                if it2 >= 1:
                    y1t_store(it2 - 1)
            if sl == 3:
                scalar.wait_ge(sm.o_sy, 16 * 16)
                scalar.wait_ge(sm.o_sz, 16 * 15)
                y1t_store(15)

        def emit_a4_vector(vector, sl):
            for pr in range(2):
                for qt2 in range(2):
                    qt = pr * 2 + qt2
                    it2 = sl * 4 + qt
                    vector.wait_ge(sm.o_ac, 3 * it2 + 1)
                    vector.wait_ge(sm.o_lx, 16 * (it2 + 1))
                    vector.scalar_tensor_tensor(FB2[:, :],
                                                AOF[:, (qt % 2) * 2048:
                                                    (qt % 2) * 2048 + 2048],
                                                0.0, FB1[:, :],
                                                ALU.add, ALU.add,
                                                accum_out=COLA[:, 0:1]
                                                ).then_inc(sm.o_dv, 1)
                    vector.scalar_tensor_tensor(FB3[:, :], FB2[:, :], 0.0,
                                                FB2[:, :], ALU.add, ALU.mult,
                                                accum_out=COLA[:, 1:2])
                    vector.tensor_scalar(COLA[:, 2:3], COLA[:, 0:1], 1.0 / HB,
                                         None, ALU.mult)
                    vector.tensor_copy(out=DUM[:, 32:48], in_=IDB[:, 0:16])
                    vector.tensor_copy(out=DUM[:, 48:64], in_=IDB[:, 16:32])
                    vector.tensor_tensor(COLA[:, 3:4], COLA[:, 2:3],
                                         COLA[:, 2:3], ALU.mult)
                    vector.tensor_scalar(COLA[:, 4:5], COLA[:, 1:2], 1.0 / HB,
                                         None, ALU.mult)
                    vector.tensor_copy(out=DUM[:, 32:48], in_=IDB[:, 0:16])
                    vector.tensor_copy(out=DUM[:, 48:64], in_=IDB[:, 16:32])
                    vector.tensor_tensor(COLA[:, 5:6], COLA[:, 4:5],
                                         COLA[:, 3:4], ALU.subtract)
                    vector.tensor_copy(out=DUM[:, 16:17], in_=COLA[:, 5:6]
                                       ).then_inc(sm.o_dv, 1)
                    vector.wait_ge(sm.o_ac, 3 * it2 + 2)
                    vector.reciprocal(COLB[:, 0:1], COLC[:, 0:1])
                    vector.tensor_copy(out=DUM[:, 32:33], in_=COLB[:, 0:1])
                    vector.tensor_copy(out=DUM[:, 48:49], in_=DUM[:, 32:33])
                    if it2 >= 1:
                        vector.wait_ge(sm.o_sy, 16 * it2)
                    vector.tensor_scalar(FB4[:, :], FB2[:, :], COLA[:, 2:3],
                                         COLB[:, 0:1], ALU.subtract, ALU.mult)
                    if it2 >= 2:
                        vector.wait_ge(sm.o_pe, _a4_y1t_cnt(sl, qt - 2)
                                       if qt >= 2 else
                                       _a4_y1t_cnt(sl - 1, qt + 2))
                    vector.tensor_copy(out=BB3[:, (qt % 2) * 2048:
                                               (qt % 2) * 2048 + 2048],
                                       in_=FB4[:, :])
                    vector.tensor_copy(out=DUMW[:, 0:128], in_=IDB[:, 0:128]).then_inc(sm.o_dv, 1)

        @block.sync
        def _(sync):
            sync.dma_start(out=E8S[:, :], in_=e8[:, :]).then_inc(sm.c_ld, 16)
            sync.dma_start(out=GQC[:, :], in_=gqv[:, :]).then_inc(sm.c_ld, 16)
            sync.dma_start(out=GKC[:, :], in_=gkv[:, :]).then_inc(sm.c_ld, 16)
            sync.dma_start(out=IDB[:, :], in_=identin[:, :]).then_inc(sm.c_ld, 16)
            for j, (m, wt, ch) in enumerate(DCH):
                if j >= 2:
                    sync.wait_ge(sm.d_dv, j - 1)
                    sync.wait_ge(sm.d_pe, j - 1)
                buf = CSD[:, (j % 2) * 2048:(j % 2) * 2048 + 2048]
                sc = (SCL0 if j % 2 == 0 else SCL1)[:, :]
                dl = DL[j % 2]
                if m == "g":
                    sync.dma_start(out=buf, in_=gsh[:, ch * 2048:(ch + 1) * 2048]
                                   ).then_inc(dl, 16)
                    sync.dma_start(out=sc, in_=gsc[:, ch * 2048:(ch + 1) * 2048]
                                   ).then_inc(dl, 16)
                elif m == "u":
                    sync.dma_start(out=buf, in_=ush[:, ch * 2048:(ch + 1) * 2048]
                                   ).then_inc(dl, 16)
                    sync.dma_start(out=sc, in_=usc[:, ch * 2048:(ch + 1) * 2048]
                                   ).then_inc(dl, 16)
                else:
                    sync.dma_start(out=buf, in_=dsh[wt * 128:(wt + 1) * 128, :]
                                   ).then_inc(dl, 16)
                    sync.dma_start(out=sc, in_=dsc[wt * 8:(wt + 1) * 8, :]
                                   ).then_inc(dl, 16)
            def emit_xq(sl, qt):
                it2 = sl * 4 + qt
                if it2 >= 1:
                    sync.wait_ge(sm.o_dv, 3 * it2 - 2)
                sync.dma_start(
                    out=FB1[:, :],
                    in_=xq[(sl * 4 + qt) * 128:(sl * 4 + qt + 1) * 128, :]
                ).then_inc(sm.o_lx, 16)

            def emit_a4_sync(sl):
                emit_xq(sl, 0)
                for pr in range(2):
                    for ks in range(16):
                        it = sl * 32 + pr * 16 + ks
                        if it == 0:
                            sync.wait_ge(WS[0], 16 * 8)
                            sync.wait_ge(WS[1], 16 * 8)
                        if it >= 2:
                            it_prev = it - 2
                            slp, rem = divmod(it_prev, 32)
                            prp, ksp = divmod(rem, 16)
                            sync.wait_ge(sm.o_pe, _a4_blk_cnt(slp, prp, ksp))
                        sync.wait_ge(sm.a_df, 16 * (sl + 1))
                        sync.dma_start(
                            out=BB1[:, (it % 2) * 2048:(it % 2) * 2048 + 2048],
                            in_=wo_bf[ks * 128:(ks + 1) * 128, :]
                        ).then_inc(OL[it % 2], 16)
                    emit_xq(sl, 1 + pr)
                emit_xq(sl, 3)

            for u in A3S:
                i, sl, h = u["i"], u["sl"], u["h"]
                if h == 0:
                    if sl > 0:
                        emit_a4_sync(sl - 1)
                        sync.wait_ge(sm.a_df, i)
                    sync.dma_start(out=M1S[:, :],
                                   in_=m1in[sl * 128:(sl + 1) * 128, :]
                                   ).then_inc(sm.a_lm, 16)
                    sync.dma_start(out=M2S[:, :],
                                   in_=m2in[sl * 128:(sl + 1) * 128, :]
                                   ).then_inc(sm.a_lm, 16)
                if i == 0:
                    sync.wait_ge(sm.q_st, 16 * 16)
                if i >= 2:
                    sync.wait_ge(sm.a_df, i - 1)
                sync.dma_start(
                    out=QTS[:, (i % 2) * 512:(i % 2) * 512 + 512],
                    in_=qt_dram[h * 128:(h + 1) * 128, sl * 512:(sl + 1) * 512]
                ).then_inc(AL[i % 2], 16)
            emit_a4_sync(3)
            sync.wait_ge(sm.o_sy, 16 * 16)
            sync.wait_ge(sm.o_sz, 16 * 16)
            for hc in range(16):
                sync.dma_start(out=Y1TS[:, hc * 2048:(hc + 1) * 2048],
                               in_=y1t_dram[hc * 128:(hc + 1) * 128, :]
                               ).then_inc(sm.f_ly, 16)
            for s in range(16):
                for sub in range(8):
                    if sub == 0:
                        if s == 0:
                            sync.wait_ge(sm.d_cc, 3)
                        else:
                            sync.wait_ge(sm.f_pe2, 16 * s)
                    sync.dma_start(
                        out=WD8[:, sub * 2048:(sub + 1) * 2048],
                        in_=gd[(s * 8 + sub) * 128:(s * 8 + sub + 1) * 128, :]
                    ).then_inc(sm.f_lw, 16)
                for sub in range(8):
                    for mi, gap in enumerate((gg, gu)):
                        gidx = s * 8 + sub
                        if gidx >= 2:
                            sync.wait_ge(sm.f_pe, 4 * (gidx - 1))
                        slot = (sub % 2) * 2 + mi
                        sync.dma_start(
                            out=GUBF[:, slot * 2048:(slot + 1) * 2048],
                            in_=bass.AP(gap.tensor, gidx * 128,
                                        [[MLP, 128], [128 * MLP, 16], [1, 128]])
                        ).then_inc(FLG[sub % 2], 16)
            sync.wait_ge(FG[0], 16 * 128)
            sync.wait_ge(FG[1], 16 * 128)
            for tg in range(16):
                if tg >= 1:
                    sync.wait_ge(sm.l_dv, 2 * tg)
                sync.dma_start(out=FB1[:, :],
                               in_=part[tg * 128:(tg + 1) * 128, :]
                               ).then_inc(sm.l_ld, 16)
                sync.dma_start(out=FB2[:, :],
                               in_=y1_dram[tg * 128:(tg + 1) * 128, :]
                               ).then_inc(sm.l_ld, 16)
            if DEBUG:
                sync.dma_start(out=y1d_out[:, :], in_=y1_dram[:, :]
                               ).then_inc(sm.c_ld, 16)
                sync.dma_start(out=qt_out[:, :], in_=qt_dram[:, :]
                               ).then_inc(sm.c_ld, 16)
                sync.dma_start(out=gg_out[:, :], in_=gg[:, :]
                               ).then_inc(sm.c_ld, 16)
                sync.dma_start(out=gd_out[:, :], in_=gd[:, :]
                               ).then_inc(sm.c_ld, 16)
                sync.dma_start(out=pt_out[:, :], in_=part[:, :]
                               ).then_inc(sm.c_ld, 16)
                sync.wait_ge(sm.c_ld, 64 + 10 * 16)

        @block.gpsimd
        def _(gpsimd):
            gpsimd.wait_ge(sm.d_dv, NDC)
            for i in range(16):
                gpsimd.dma_start(out=WQS[:, i * 2048:(i + 1) * 2048],
                                 in_=wq[i * 128:(i + 1) * 128, :]
                                 ).then_inc(sm.q_gw, 16)
            for qt in range(16):
                if qt >= 2:
                    gpsimd.wait_ge(sm.q_pe2, qt - 1)
                gpsimd.dma_start(out=BB1[:, (qt % 2) * 2048:(qt % 2) * 2048 + 2048],
                                 in_=xq[qt * 128:(qt + 1) * 128, :]
                                 ).then_inc(QG[qt % 2], 16)
            for i in range(16):
                if i >= 2:
                    gpsimd.wait_ge(WS[i % 2], 16 * ((i - 2) // 2 + 1))
                gpsimd.dma_start(out=P32B[:, (i % 2) * 2048:(i % 2) * 2048 + 2048],
                                 in_=wo[i * 128:(i + 1) * 128, :]
                                 ).then_inc(WG[i % 2], 16)
            gpsimd.wait_ge(sm.q_pe, 32)
            gpsimd.dma_start(
                out=WKS[:, :].rearrange("p (t d) -> p t d", t=16),
                in_=bass.AP(wkv.tensor, 0, [[512, 128], [128 * 512, 16], [1, 512]])
            ).then_inc(sm.k_gw, 16)
            gpsimd.dma_start(
                out=WVS[:, :].rearrange("p (t d) -> p t d", t=16),
                in_=bass.AP(wkv.tensor, HB * 512,
                            [[512, 128], [128 * 512, 16], [1, 512]])
            ).then_inc(sm.k_gw, 16)
            for i in range(32):
                if i >= 2:
                    gpsimd.wait_ge(sm.k_pe2, i - 1)
                gpsimd.dma_start(out=BB1[:, (i % 2) * 2048:(i % 2) * 2048 + 2048],
                                 in_=xb[i * 128:(i + 1) * 128, :]
                                 ).then_inc(KG[i % 2], 16)
            for gi, (piece, gath) in enumerate(((pg, gg), (pu, gu), (pd, gd))):
                gpsimd.wait_ge(sm.d_sc, 16 * 16 * (gi + 1))
                gpsimd.collective_compute(
                    "AllGather", ALU.bypass,
                    replica_groups=[[0, 1, 2, 3, 4, 5, 6, 7]],
                    ins=[piece.opt()], outs=[gath.opt()],
                ).then_inc(sm.d_cc)
            for s in range(16):
                for tg in range(16):
                    it = s * 16 + tg
                    gpsimd.wait_ge(sm.f_ac2, it + 1)
                    if s >= 1:
                        prev = (s - 1) * 16 + tg
                        gpsimd.wait_ge(FG[tg % 2], 16 * (prev // 2 + 1))
                    stg = FB4 if tg % 2 == 0 else FB1
                    gpsimd.dma_start(
                        out=part[tg * 128:(tg + 1) * 128, :],
                        in_=stg[:, :],
                        accum_op=(ALU.bypass if s == 0 else ALU.add),
                    ).then_inc(FG[tg % 2], 16)
            gpsimd.wait_ge(FG[0], 16 * 128)
            gpsimd.wait_ge(FG[1], 16 * 128)

        @block.tensor
        def _(tensor):
            tensor.wait_ge(sm.c_ld, 64)
            for j in range(NDC):
                tensor.wait_ge(DL[j % 2], 32 * (j // 2 + 1))
                if j >= 1:
                    tensor.wait_ge(sm.d_ac, j)
                scl = SCL0 if j % 2 == 0 else SCL1
                for q4 in range(4):
                    op = tensor.matmul(PSA[:, q4 * 512:(q4 + 1) * 512],
                                       E8S[:, :],
                                       scl[0:8, q4 * 512:(q4 + 1) * 512],
                                       start=True, stop=True)
                op.then_inc(sm.d_pe, 1)
            tensor.wait_ge(sm.c_ld, 64)
            tensor.wait_ge(sm.q_gw, 16 * 16)
            for qt in range(16):
                tensor.wait_ge(QG[qt % 2], 16 * (qt // 2 + 1))
                if qt >= 1:
                    tensor.wait_ge(sm.q_ac, 3 * (qt - 1) + 2)
                xin = BB1[:, (qt % 2) * 2048:(qt % 2) * 2048 + 2048]
                for kc in range(16):
                    op = tensor.transpose(PSBb[:, kc * 128:(kc + 1) * 128],
                                          xin[:, kc * 128:(kc + 1) * 128],
                                          IDB[:, :])
                op.then_inc(sm.q_pe2, 1)
                tensor.wait_ge(sm.q_ac, 3 * qt + 1)
                xt = BB2[:, (qt % 2) * 2048:(qt % 2) * 2048 + 2048]
                for dqc in range(4):
                    for kc in range(16):
                        op = tensor.matmul(PSA[:, dqc * 512:(dqc + 1) * 512],
                                           xt[:, kc * 128:(kc + 1) * 128],
                                           WQS[:, kc * 2048 + dqc * 512:
                                               kc * 2048 + (dqc + 1) * 512],
                                           start=(kc == 0), stop=(kc == 15))
                op.then_inc(sm.q_pe, 1)
                tensor.wait_ge(sm.q_dv, 4 * qt + 3)
                for hh in range(16):
                    op = tensor.transpose(
                        PSBb[:, 2048 + hh * 128:2048 + (hh + 1) * 128],
                        xt[:, hh * 128:(hh + 1) * 128], IDB[:, :])
                op.then_inc(sm.q_pe, 1)
            tensor.wait_ge(sm.k_gw, 32)
            for tt in range(32):
                tensor.wait_ge(KG[tt % 2], 16 * (tt // 2 + 1))
                if tt == 0:
                    tensor.wait_ge(sm.q_ac, 48)
                    tensor.wait_ge(sm.q_dv, 64)
                if tt >= 1:
                    tensor.wait_ge(sm.k_ac, 4 * (tt - 1) + 1)
                xin = BB1[:, (tt % 2) * 2048:(tt % 2) * 2048 + 2048]
                for kc in range(16):
                    op = tensor.transpose(PSBb[:, kc * 128:(kc + 1) * 128],
                                          xin[:, kc * 128:(kc + 1) * 128],
                                          IDB[:, :])
                op.then_inc(sm.k_pe2, 1)
                tensor.wait_ge(sm.k_ac, 4 * tt + 1)
                xt = BB2[:, (tt % 2) * 2048:(tt % 2) * 2048 + 2048]
                for kc in range(16):
                    tensor.matmul(PSA[:, 0:512], xt[:, kc * 128:(kc + 1) * 128],
                                  WKS[:, kc * 512:(kc + 1) * 512],
                                  start=(kc == 0), stop=(kc == 15))
                for kc in range(16):
                    op = tensor.matmul(PSA[:, 512:1024],
                                       xt[:, kc * 128:(kc + 1) * 128],
                                       WVS[:, kc * 512:(kc + 1) * 512],
                                       start=(kc == 0), stop=(kc == 15))
                op.then_inc(sm.k_pe, 1)
                tensor.wait_ge(sm.k_dv, 4 * tt + 3)
                for sgi in range(4):
                    op = tensor.transpose(
                        PSBb[:, 2048 + sgi * 128:2048 + (sgi + 1) * 128],
                        xt[:, sgi * 128:(sgi + 1) * 128], IDB[:, :])
                op.then_inc(sm.k_pe, 1)
            tensor.wait_ge(sm.k_dv, 4 * 32)
            tensor.wait_ge(sm.k_ac, 4 * 32)
            for u in A3S:
                i, sl, h = u["i"], u["sl"], u["h"]
                kvh = h // 4
                qtl = QTS[:, (i % 2) * 512:(i % 2) * 512 + 512]
                if h == 0 and sl > 0:
                    emit_a4_tensor(tensor, sl - 1)
                    tensor.wait_ge(sm.o_ac, 12 * sl)
                tensor.wait_ge(sm.a_lm, 32 * (sl + 1))
                tensor.wait_ge(AL[i % 2], 16 * (i // 2 + 1))
                n1 = u["n1_0"]
                for qt in range(4):
                    for k5 in range(KT512[sl]):
                        if n1 >= 2:
                            tensor.wait_ge(sm.a_d1, n1 - 1)
                        tensor.matmul(PSA[:, (n1 % 2) * 512:(n1 % 2) * 512 + 512],
                                      qtl[:, qt * 128:(qt + 1) * 128],
                                      KTS[:, kvh * 4096 + k5 * 512:
                                          kvh * 4096 + (k5 + 1) * 512],
                                      start=True, stop=True
                                      ).then_inc(sm.a_p1, 1)
                        n1 += 1
                tensor.wait_ge(sm.a_dm, i + 1)
                if i >= 1:
                    tensor.wait_ge(sm.a_df, i)
                for qt in range(4):
                    op = tensor.matmul(PSB[0:1, 1024 + qt * 128:
                                           1024 + (qt + 1) * 128],
                                       MCB[:, qt:qt + 1], IDB[:, :],
                                       start=(qt == 0), stop=(qt == 3))
                op.then_inc(sm.a_pr, 1)
                tensor.wait_ge(sm.a_dn, i + 1)
                n2 = u["n2_0"]
                nk = KT128[sl]

                def emit_av(kt, n2v):
                    tensor.wait_ge(sm.a_ae, n2v + 1)
                    pt = PTB[:, (n2v % 2) * 512:(n2v % 2) * 512 + 512]
                    last = (kt == nk - 1)
                    for qt in range(4):
                        oacc = PSB[:, qt * 512:qt * 512 + 512]
                        tensor.matmul(oacc[:, 0:128],
                                      pt[:, qt * 128:(qt + 1) * 128],
                                      VS[:, kt * 512 + kvh * 128:
                                         kt * 512 + (kvh + 1) * 128],
                                      start=(kt == 0), stop=False)
                        op = tensor.matmul(oacc[:, 128:129],
                                           pt[:, qt * 128:(qt + 1) * 128],
                                           ONESC[:, :], start=False, stop=last)
                    op.then_inc(sm.a_av, 1)

                for kt in range(nk):
                    if n2 >= 2:
                        tensor.wait_ge(sm.a_ae, n2 - 1)
                    bank = PSA[:, 1024 + (n2 % 2) * 512:
                               1024 + (n2 % 2) * 512 + 512]
                    tensor.matmul(bank,
                                  KTS[:, kvh * 4096 + kt * 128:
                                      kvh * 4096 + (kt + 1) * 128],
                                  qtl[:, :], start=True, stop=False)
                    tensor.matmul(bank, ONES1[:, :], NEGM[0:1, :],
                                  start=False, stop=True
                                  ).then_inc(sm.a_p2, 1)
                    if kt >= 1:
                        emit_av(kt - 1, n2 - 1)
                    n2 += 1
                emit_av(nk - 1, n2 - 1)
            emit_a4_tensor(tensor, 3)
            tensor.wait_ge(sm.o_ac, 48)
            tensor.wait_ge(sm.f_ly, 16 * 16)
            for s in range(16):
                if s >= 1:
                    tensor.wait_ge(sm.f_ac2, s * 16 - 1)
                for sub in range(8):
                    tensor.wait_ge(FLG[sub % 2], 32 * (s * 4 + sub // 2 + 1))
                    gslot = GUBF[:, ((sub % 2) * 2) * 2048:
                                 ((sub % 2) * 2) * 2048 + 2048]
                    uslot = GUBF[:, ((sub % 2) * 2 + 1) * 2048:
                                 ((sub % 2) * 2 + 1) * 2048 + 2048]
                    for tq in range(4):
                        k = (s * 8 + sub) * 4 + tq
                        if k >= 2:
                            tensor.wait_ge(sm.f_dv, k - 1)
                        gb = PSA[:, (tq % 2) * 512:(tq % 2) * 512 + 512]
                        ub = PSA[:, 1024 + (tq % 2) * 512:
                                 1024 + (tq % 2) * 512 + 512]
                        for hs in range(16):
                            tensor.matmul(gb, gslot[:, hs * 128:(hs + 1) * 128],
                                          Y1TS[:, hs * 2048 + tq * 512:
                                               hs * 2048 + (tq + 1) * 512],
                                          start=(hs == 0), stop=(hs == 15))
                        for hs in range(16):
                            op = tensor.matmul(ub,
                                               uslot[:, hs * 128:(hs + 1) * 128],
                                               Y1TS[:, hs * 2048 + tq * 512:
                                                    hs * 2048 + (tq + 1) * 512],
                                               start=(hs == 0), stop=(hs == 15))
                        op.then_inc(sm.f_pe, 1)
                tensor.wait_ge(sm.f_lw, 16 * 8 * (s + 1))
                for tg in range(16):
                    it = s * 16 + tg
                    accb = PSA if tg % 2 == 0 else PSB
                    tensor.wait_ge(sm.f_dv, 32 * s + 29 + tg // 4)
                    if tg == 0:
                        tensor.wait_ge(sm.f_ac, 64 * (s + 1))
                    if it >= 2:
                        tensor.wait_ge(sm.f_ac2, it - 1)
                    for sub in range(8):
                        for hc in range(4):
                            op = tensor.matmul(accb[:, hc * 512:(hc + 1) * 512],
                                               INTERT[:, sub * 2048 + tg * 128:
                                                      sub * 2048 + (tg + 1) * 128],
                                               WD8[:, sub * 2048 + hc * 512:
                                                   sub * 2048 + (hc + 1) * 512],
                                               start=(sub == 0), stop=(sub == 7))
                    op.then_inc(sm.f_pe2, 1)

        @block.scalar
        def _(scalar):
            def d_piece(jj):
                m, wt, ch = DCH[jj]
                scalar.wait_ge(sm.d_dv, jj + 1)
                if m in ("g", "u"):
                    dst = pg if m == "g" else pu
                    o_lo = bass.AP(dst.tensor, ch * 2048,
                                   [[2 * MLP, 128], [1, 2048]])
                    o_hi = bass.AP(dst.tensor, MLP + ch * 2048,
                                   [[2 * MLP, 128], [1, 2048]])
                else:
                    o_lo = bass.AP(pd.tensor, (2 * wt * 128) * HB,
                                   [[2 * HB, 128], [1, 2048]])
                    o_hi = bass.AP(pd.tensor, (2 * wt * 128 + 1) * HB,
                                   [[2 * HB, 128], [1, 2048]])
                scalar.dma_start(out=o_lo, in_=DLO[:, :]).then_inc(sm.d_sc, 16)
                scalar.dma_start(out=o_hi, in_=DHI[:, :]).then_inc(sm.d_sc, 16)

            for j in range(NDC):
                scalar.wait_ge(sm.d_pe, j + 1)
                scalar.activation(DNSX[:, (j % 2) * 2048:(j % 2) * 2048 + 2048],
                                  PSA[:, :], AF.Identity).then_inc(sm.d_ac, 1)
                if j >= 1:
                    d_piece(j - 1)
            d_piece(NDC - 1)
            for qt in range(16):
                scalar.wait_ge(sm.q_pe2, qt + 1)
                if qt >= 2:
                    scalar.wait_ge(sm.q_pe, 2 * (qt - 2) + 2)
                scalar.activation(BB2[:, (qt % 2) * 2048:(qt % 2) * 2048 + 2048],
                                  PSBb[:, 0:2048], AF.Identity)
                scalar.activation(DUMW[:, 128:256], IDB[:, 0:128], AF.Identity).then_inc(sm.q_ac, 1)
                scalar.wait_ge(sm.q_pe, 2 * qt + 1)
                if qt >= 1:
                    scalar.wait_ge(sm.q_dv, 4 * (qt - 1) + 4)
                scalar.activation(FB3[:, :], PSA[:, :], AF.Identity)
                scalar.activation(DUMW[:, 128:256], IDB[:, 0:128], AF.Identity).then_inc(sm.q_ac, 1)
                scalar.wait_ge(sm.q_dv, 4 * qt + 1)
                scalar.activation(COLC[:, 0:16], COLA[:, 0:16], AF.Sqrt,
                                  bias=EPSC[:, 0:1], scale=1.0 / D)
                scalar.activation(DUM[:, 0:16], COLC[:, 0:16], AF.Sqrt
                                  ).then_inc(sm.q_ac, 1)
                scalar.wait_ge(sm.q_dv, 4 * qt + 4)
                if DEBUG and qt == 0:
                    scalar.dma_start(out=qf_dbg[:, :], in_=FB3[:, :]
                                     ).then_inc(sm.c_ld, 16)
                    scalar.dma_start(out=col_dbg[:, 0:16], in_=COLA[:, :]
                                     ).then_inc(sm.c_ld, 16)
                    scalar.dma_start(out=col_dbg[:, 16:32], in_=COLB[:, :]
                                     ).then_inc(sm.c_ld, 16)
                    scalar.dma_start(out=col_dbg[:, 32:48], in_=COLC[:, :]
                                     ).then_inc(sm.c_ld, 16)
                    scalar.dma_start(out=qn_dbg[:, :], in_=BB3[:, 0:2048]
                                     ).then_inc(sm.c_ld, 16)
                    scalar.dma_start(out=qn2_dbg[:, :], in_=BB2[:, 0:2048]
                                     ).then_inc(sm.c_ld, 16)
                    scalar.dma_start(out=qe_dbg[:, :],
                                     in_=FB3[:, :].bitcast(BF16)[:, 0:2048]
                                     ).then_inc(sm.c_ld, 16)
                    scalar.wait_ge(sm.c_ld, 64 + 7 * 16)
                scalar.dma_start(
                    out=bass.AP(qt_dram.tensor, qt * 128,
                                [[TL, 128], [128 * TL, 16], [1, 128]]),
                    in_=BB3[:, 0:2048].rearrange("p (h j) -> p h j", h=16)
                ).then_inc(sm.q_st, 16)
            for i in range(16):
                scalar.wait_ge(WG[i % 2], 16 * (i // 2 + 1))
                scalar.dma_start(out=wo_bf[i * 128:(i + 1) * 128, :],
                                 in_=P32B[:, (i % 2) * 2048:(i % 2) * 2048 + 2048]
                                 ).then_inc(WS[i % 2], 16)
            for tt in range(32):
                scalar.wait_ge(sm.k_pe2, tt + 1)
                if tt >= 2:
                    scalar.wait_ge(sm.k_pe, 2 * (tt - 2) + 2)
                scalar.activation(BB2[:, (tt % 2) * 2048:(tt % 2) * 2048 + 2048],
                                  PSBb[:, 0:2048], AF.Identity)
                scalar.activation(DUMW[:, 128:256], IDB[:, 0:128], AF.Identity).then_inc(sm.k_ac, 1)
                scalar.wait_ge(sm.k_pe, 2 * tt + 1)
                if tt >= 1:
                    scalar.wait_ge(sm.k_dv, 4 * (tt - 1) + 1)
                scalar.activation(FB3[:, 0:512], PSA[:, 0:512], AF.Identity)
                scalar.activation(DUMW[:, 128:256], IDB[:, 0:128], AF.Identity).then_inc(sm.k_ac, 1)
                scalar.activation(VS[:, tt * 512:(tt + 1) * 512], PSA[:, 512:1024],
                                  AF.Identity)
                scalar.activation(DUMW[:, 128:256], IDB[:, 0:128], AF.Identity).then_inc(sm.k_ac, 1)
                scalar.wait_ge(sm.k_dv, 4 * tt + 1)
                scalar.activation(COLC[:, 0:4], COLA[:, 0:4], AF.Sqrt,
                                  bias=EPSC[:, 0:1], scale=1.0 / D)
                scalar.activation(DUM[:, 0:4], COLC[:, 0:4], AF.Sqrt
                                  ).then_inc(sm.k_ac, 1)
            for u in A3S:
                i, sl, h = u["i"], u["sl"], u["h"]
                if h == 0 and sl > 0:
                    emit_a4_scalar(scalar, sl - 1)
                scalar.wait_ge(sm.a_pr, i + 1)
                scalar.activation(NEGF[0:1, 0:512], PSB[0:1, 1024:1536],
                                  AF.Identity)
                scalar.activation(DUMW[:, 128:256], IDB[:, 0:128], AF.Identity).then_inc(sm.a_an, 1)
                n2 = u["n2_0"]
                for kt in range(KT128[sl]):
                    scalar.wait_ge(sm.a_p2, n2 + 1)
                    if n2 >= 2:
                        scalar.wait_ge(sm.a_av, n2 - 1)
                    if kt >= KT128[sl] - 8:
                        jj = kt - (KT128[sl] - 8)
                        cnt = 8 * i + jj + 1
                        scalar.wait_ge(sm.a_d2, cnt)
                        scalar.activation(
                            PTB[:, (n2 % 2) * 512:(n2 % 2) * 512 + 512],
                            PSA[:, 1024 + (n2 % 2) * 512:
                                1024 + (n2 % 2) * 512 + 512],
                            AF.Exp).then_inc(sm.a_ae, 1)
                    else:
                        scalar.activation(
                            PTB[:, (n2 % 2) * 512:(n2 % 2) * 512 + 512],
                            PSA[:, 1024 + (n2 % 2) * 512:
                                1024 + (n2 % 2) * 512 + 512],
                            AF.Exp).then_inc(sm.a_ae, 1)
                    n2 += 1
                if DEBUG and i == 0:
                    scalar.wait_ge(sm.a_df, i + 1)
                    scalar.wait_ge(sm.a_dm, 1)
                    scalar.wait_ge(sm.a_dn, 1)
                    scalar.wait_ge(sm.a_df, 1)
                    scalar.dma_start(out=ofn_dbg[:, :], in_=OFN[:, 0:640]
                                     ).then_inc(sm.c_ld, 16)
                    scalar.dma_start(out=mcb_dbg[:, 0:4], in_=MCOL[:, :]
                                     ).then_inc(sm.c_ld, 16)
                    scalar.dma_start(out=negm_dbg[:, :], in_=NEGM[0:1, :]
                                     ).then_inc(sm.c_ld, 16)
                    for dq in range(4):
                        scalar.dma_start(
                            out=os_dbg[:, dq * 128:(dq + 1) * 128],
                            in_=OS[:, dq * 2048:dq * 2048 + 128]
                        ).then_inc(sm.c_ld, 16)
                    scalar.wait_ge(sm.c_ld, 64 + 14 * 16)
            emit_a4_scalar(scalar, 3)
            for s in range(16):
                for sub in range(8):
                    for tq in range(4):
                        k = (s * 8 + sub) * 4 + tq
                        scalar.wait_ge(sm.f_pe, k + 1)
                        if k >= 1:
                            scalar.wait_ge(sm.f_dv, k)
                        gb = PSA[:, (tq % 2) * 512:(tq % 2) * 512 + 512]
                        ub = PSA[:, 1024 + (tq % 2) * 512:
                                 1024 + (tq % 2) * 512 + 512]
                        scalar.activation(FB3[:, 0:512], gb,
                                          AF.Identity).then_inc(sm.f_ac, 1)
                        scalar.activation(FB3[:, 512:1024], ub,
                                          AF.Identity).then_inc(sm.f_ac, 1)
                        scalar.wait_ge(sm.f_tg, k + 1)
                        scalar.activation(FB2[:, 512:1024], FB2[:, 0:512],
                                          AF.Tanh).then_inc(sm.f_th, 1)
                for tg in range(16):
                    it = s * 16 + tg
                    scalar.wait_ge(sm.f_pe2, it + 1)
                    accb = PSA if tg % 2 == 0 else PSB
                    stg = FB4 if tg % 2 == 0 else FB1
                    if it >= 2:
                        scalar.wait_ge(FG[it % 2], 16 * (it // 2))
                    scalar.activation(stg[:, :], accb[:, :],
                                      AF.Identity).then_inc(sm.f_ac2, 1)
            for tg in range(16):
                scalar.wait_ge(sm.l_dv, 2 * tg + 1)
                scalar.activation(COLC[:, 0:1], COLA[:, 5:6], AF.Sqrt,
                                  bias=EPSC[:, 1:2], scale=1.0)
                scalar.activation(DUM[:, 0:1], COLC[:, 0:1], AF.Sqrt
                                  ).then_inc(sm.l_ac, 1)
                scalar.wait_ge(sm.l_dv, 2 * tg + 2)
                scalar.dma_start(out=y2out[tg * 128:(tg + 1) * 128, :],
                                 in_=FB3[:, :]).then_inc(sm.l_st, 16)
            scalar.wait_ge(sm.l_st, 16 * 16)
            if DEBUG:
                scalar.dma_start(out=it_dbg[:, :], in_=INTERT[:, :]
                                 ).then_inc(sm.c_ld, 16)
                scalar.dma_start(out=y1t_dbg[:, :], in_=P64A[:, :]
                                 ).then_inc(sm.c_ld, 16)
                scalar.dma_start(out=y1td_dbg[:, :], in_=y1t_dram[:, :]
                                 ).then_inc(sm.c_ld, 16)
                scalar.wait_ge(sm.c_ld, 64 + 19 * 16)

        @block.vector
        def _(vector):
            vector.memset(ONES1[:, :], 1.0)
            vector.memset(ONESC[:, :], 1.0)
            vector.memset(EPSC[:, 0:1], EPS_RMS)
            vector.memset(EPSC[:, 1:2], EPS_LN)
            vector.wait_ge(sm.c_ld, 64)
            vector.tensor_scalar(GQS[:, :], GQC[:, :], QK_SCALE, None, ALU.mult)
            for j in range(NDC):
                vector.wait_ge(DL[j % 2], 32 * (j // 2 + 1))
                vector.wait_ge(sm.d_ac, j + 1)
                if j >= 1:
                    vector.wait_ge(sm.d_sc, 32 * j)
                cb = CSD[:, (j % 2) * 2048:(j % 2) * 2048 + 2048]
                nx = DNSX[:, (j % 2) * 2048:(j % 2) * 2048 + 2048]
                vector.tensor_scalar(DI1[:, :], cb, 15, None, ALU.bitwise_and)
                vector.tensor_scalar(DI2[:, :], cb, 4, None,
                                     ALU.logical_shift_right)
                vector.tensor_scalar(DI3[:, :], cb, 3, 1,
                                     ALU.logical_shift_right, ALU.bitwise_and)
                vector.tensor_scalar(DI4[:, :], cb, 7, 1,
                                     ALU.logical_shift_right, ALU.bitwise_and)
                for half, (xx, ss, dst) in enumerate(
                        ((DI1, DI3, DLO), (DI2, DI4, DHI))):
                    sl2 = slice(half * 2048, half * 2048 + 2048)
                    vector.tensor_scalar(DT1[:, :], xx[:, :], 64.0, 16128.0,
                                         ALU.mult, ALU.add)
                    vector.scalar_tensor_tensor(DBITS[:, sl2], ss[:, :],
                                                -33280.0, DT1[:, :],
                                                ALU.mult, ALU.add)
                    uu = DBITS[:, sl2].bitcast(BF16)
                    vector.tensor_scalar(DC1[:, :], uu, 1.0, -1.0,
                                         ALU.min, ALU.max)
                    vector.tensor_scalar(DC2[:, :], uu, 2.0, 1.0,
                                         ALU.mult, ALU.min)
                    vector.tensor_tensor(DT2[:, :], uu, DC1[:, :], ALU.add)
                    vector.scalar_tensor_tensor(DT3[:, :], DC2[:, :], -1.0,
                                                DT2[:, :], ALU.max,
                                                ALU.subtract)
                    if half == 0:
                        vector.tensor_tensor(dst[:, :], DT3[:, :], nx, ALU.mult)
                    else:
                        vector.tensor_tensor(dst[:, :], DT3[:, :], nx, ALU.mult
                                             ).then_inc(sm.d_dv, 1)
            for qt in range(16):
                vector.wait_ge(sm.q_ac, 3 * qt + 2)
                for hh in range(16):
                    op = vector.scalar_tensor_tensor(
                        FB2[:, hh * 128:(hh + 1) * 128],
                        FB3[:, hh * 128:(hh + 1) * 128], 0.0,
                        FB3[:, hh * 128:(hh + 1) * 128],
                        ALU.add, ALU.mult, accum_out=COLA[:, hh:hh + 1])
                vector.tensor_copy(out=DUM[:, 16:32], in_=COLA[:, 0:16]
                                   ).then_inc(sm.q_dv, 1)
                vector.wait_ge(sm.q_ac, 3 * qt + 3)
                vector.reciprocal(COLB[:, 0:16], COLC[:, 0:16]).then_inc(sm.q_dv, 1)
                vector.tensor_copy(out=DUM[:, 32:48], in_=COLB[:, 0:16])
                vector.tensor_copy(out=DUM[:, 48:64], in_=DUM[:, 32:48])
                xt = BB2[:, (qt % 2) * 2048:(qt % 2) * 2048 + 2048]
                for hh in range(16):
                    vector.tensor_scalar(xt[:, hh * 128:(hh + 1) * 128],
                                         FB3[:, hh * 128:(hh + 1) * 128],
                                         COLB[:, hh:hh + 1], None, ALU.mult)
                vector.tensor_copy(out=DUMW[:, 0:128], in_=IDB[:, 0:128]).then_inc(sm.q_dv, 1)
                vector.wait_ge(sm.q_pe, 2 * qt + 2)
                if qt >= 1:
                    vector.wait_ge(sm.q_st, 16 * qt)
                vector.tensor_scalar(BB3[:, 0:2048],
                                     PSBb[:, 2048:4096],
                                     GQS[:, 0:1], None, ALU.mult)
                vector.tensor_copy(out=DUMW[:, 0:128], in_=IDB[:, 0:128]).then_inc(sm.q_dv, 1)
            vector.wait_ge(sm.d_sc, 16 * 48)
            for tt in range(32):
                vector.wait_ge(sm.k_ac, 4 * tt + 2)
                for sgi in range(4):
                    op = vector.scalar_tensor_tensor(
                        FB2[:, sgi * 128:(sgi + 1) * 128],
                        FB3[:, sgi * 128:(sgi + 1) * 128], 0.0,
                        FB3[:, sgi * 128:(sgi + 1) * 128],
                        ALU.add, ALU.mult, accum_out=COLA[:, sgi:sgi + 1])
                vector.tensor_copy(out=DUM[:, 16:20], in_=COLA[:, 0:4]
                                   ).then_inc(sm.k_dv, 1)
                vector.wait_ge(sm.k_ac, 4 * tt + 4)
                vector.reciprocal(COLB[:, 0:4], COLC[:, 0:4]).then_inc(sm.k_dv, 1)
                vector.tensor_copy(out=DUM[:, 32:36], in_=COLB[:, 0:4])
                vector.tensor_copy(out=DUM[:, 48:52], in_=DUM[:, 32:36])
                xt = BB2[:, (tt % 2) * 2048:(tt % 2) * 2048 + 2048]
                for sgi in range(4):
                    vector.tensor_scalar(xt[:, sgi * 128:(sgi + 1) * 128],
                                         FB3[:, sgi * 128:(sgi + 1) * 128],
                                         COLB[:, sgi:sgi + 1], None,
                                         ALU.mult)
                vector.tensor_copy(out=DUMW[:, 0:128], in_=IDB[:, 0:128]).then_inc(sm.k_dv, 1)
                vector.wait_ge(sm.k_pe, 2 * tt + 2)
                for sgi in range(4):
                    vector.tensor_scalar(
                        KTS[:, sgi * 4096 + tt * 128:sgi * 4096 + (tt + 1) * 128],
                        PSBb[:, 2048 + sgi * 128:2048 + (sgi + 1) * 128],
                        GKC[:, 0:1], None, ALU.mult)
                vector.tensor_copy(out=DUMW[:, 0:128], in_=IDB[:, 0:128]).then_inc(sm.k_dv, 1)
            for u in A3S:
                i, sl, h = u["i"], u["sl"], u["h"]
                if h == 0 and sl > 0:
                    emit_a4_vector(vector, sl - 1)
                n1 = u["n1_0"]
                for qt in range(4):
                    for k5 in range(KT512[sl]):
                        vector.wait_ge(sm.a_p1, n1 + 1)
                        bank = PSA[:, (n1 % 2) * 512:(n1 % 2) * 512 + 512]
                        if k5 >= KT512[sl] - 2:
                            jj = k5 - (KT512[sl] - 2)
                            vector.tensor_tensor(
                                bank, bank,
                                M1S[:, (jj * 4 + qt) * 512:
                                    (jj * 4 + qt + 1) * 512], ALU.add)
                        vector.reduce_max(COLB[:, k5:k5 + 1], bank,
                                          axis=AX.X).then_inc(sm.a_d1, 1)
                        n1 += 1
                    vector.tensor_copy(out=DUM[:, 32:48], in_=IDB[:, 0:16])
                    vector.tensor_copy(out=DUM[:, 48:64], in_=IDB[:, 16:32])
                    vector.reduce_max(MCOL[:, qt:qt + 1],
                                      COLB[:, 0:KT512[sl]], axis=AX.X)
                vector.tensor_copy(out=DUM[:, 32:48], in_=IDB[:, 0:16])
                vector.tensor_copy(out=DUM[:, 48:64], in_=IDB[:, 16:32])
                vector.tensor_copy(out=MCB[:, :], in_=MCOL[:, :])
                vector.tensor_copy(out=DUM[:, 16:20], in_=MCB[:, 0:4])
                vector.tensor_copy(out=DUMW[:, 0:128], in_=IDB[:, 0:128]).then_inc(sm.a_dm, 1)
                vector.wait_ge(sm.a_an, i + 1)
                vector.tensor_scalar(NEGM[0:1, :], NEGF[0:1, 0:512], -1.0,
                                     None, ALU.mult)
                vector.tensor_copy(out=DUMW[:, 0:128], in_=IDB[:, 0:128]).then_inc(sm.a_dn, 1)
                n2 = u["n2_0"]
                for kt in range(KT128[sl]):
                    if kt >= KT128[sl] - 8:
                        jj = kt - (KT128[sl] - 8)
                        cnt = 8 * i + jj + 1
                        vector.wait_ge(sm.a_p2, n2 + 1)
                        bank = PSA[:, 1024 + (n2 % 2) * 512:
                                   1024 + (n2 % 2) * 512 + 512]
                        vector.tensor_tensor(bank, bank,
                                             M2S[:, jj * 512:(jj + 1) * 512],
                                             ALU.add)
                        vector.tensor_copy(out=DUMW[:, 0:128], in_=IDB[:, 0:128]).then_inc(sm.a_d2, 1)
                    n2 += 1
                vector.wait_ge(sm.a_av, u["n2_1"])
                for qt in range(4):
                    vector.reciprocal(RC[:, qt:qt + 1],
                                      PSB[:, qt * 512 + 128:qt * 512 + 129])
                vector.tensor_copy(out=DUM[:, 32:36], in_=RC[:, 0:4])
                vector.tensor_copy(out=DUM[:, 48:52], in_=DUM[:, 32:36])
                for qt in range(4):
                    vector.tensor_scalar(
                        OS[:, qt * 2048 + h * 128:qt * 2048 + (h + 1) * 128],
                        PSB[:, qt * 512:qt * 512 + 128],
                        RC[:, qt:qt + 1], None, ALU.mult)
                vector.tensor_copy(out=DUMW[:, 0:128], in_=IDB[:, 0:128]).then_inc(sm.a_df, 1)
            emit_a4_vector(vector, 3)
            for s in range(16):
                for sub in range(8):
                    for tq in range(4):
                        k = (s * 8 + sub) * 4 + tq
                        vector.wait_ge(sm.f_ac, 2 * k + 2)
                        if s > 0 and sub == 0 and tq == 0:
                            vector.wait_ge(sm.f_pe2, 16 * s)
                        gf = FB3[:, 0:512]
                        upf = FB3[:, 512:1024]
                        g2 = FB3[:, 1024:1536]
                        u1 = FB3[:, 1536:2048]
                        targ = FB2[:, 0:512]
                        th = FB2[:, 512:1024]
                        vv = FB2[:, 1024:1536]
                        vector.tensor_tensor(g2, gf, gf, ALU.mult)
                        vector.tensor_scalar(u1, g2, G_C1, G_C2,
                                             ALU.mult, ALU.add)
                        vector.tensor_tensor(targ, u1, gf, ALU.mult
                                             ).then_inc(sm.f_tg, 1)
                        vector.wait_ge(sm.f_th, k + 1)
                        vector.scalar_tensor_tensor(vv, th, 1.0, gf,
                                                    ALU.add, ALU.mult)
                        op = vector.scalar_tensor_tensor(
                            INTERT[:, sub * 2048 + tq * 512:
                                   sub * 2048 + (tq + 1) * 512],
                            vv, 0.5, upf, ALU.mult, ALU.mult)
                        if sub == 7 and tq == 3:
                            op = vector.tensor_copy(out=DUMW[:, 0:128], in_=IDB[:, 0:128])
                        op.then_inc(sm.f_dv, 1)
            for tg in range(16):
                vector.wait_ge(sm.l_ld, 32 * (tg + 1))
                if tg >= 1:
                    vector.wait_ge(sm.l_st, 16 * tg)
                vector.scalar_tensor_tensor(FB4[:, :], FB1[:, :], 0.0,
                                            FB2[:, :], ALU.add, ALU.add,
                                            accum_out=COLA[:, 0:1])
                vector.scalar_tensor_tensor(FB3[:, :], FB4[:, :], 0.0,
                                            FB4[:, :], ALU.add, ALU.mult,
                                            accum_out=COLA[:, 1:2])
                vector.tensor_scalar(COLA[:, 2:3], COLA[:, 0:1], 1.0 / HB,
                                     None, ALU.mult)
                vector.tensor_copy(out=DUM[:, 32:48], in_=IDB[:, 0:16])
                vector.tensor_copy(out=DUM[:, 48:64], in_=IDB[:, 16:32])
                vector.tensor_tensor(COLA[:, 3:4], COLA[:, 2:3], COLA[:, 2:3],
                                     ALU.mult)
                vector.tensor_scalar(COLA[:, 4:5], COLA[:, 1:2], 1.0 / HB,
                                     None, ALU.mult)
                vector.tensor_copy(out=DUM[:, 32:48], in_=IDB[:, 0:16])
                vector.tensor_copy(out=DUM[:, 48:64], in_=IDB[:, 16:32])
                vector.tensor_tensor(COLA[:, 5:6], COLA[:, 4:5], COLA[:, 3:4],
                                     ALU.subtract)
                vector.tensor_copy(out=DUM[:, 16:17], in_=COLA[:, 5:6]
                                   ).then_inc(sm.l_dv, 1)
                vector.wait_ge(sm.l_ac, tg + 1)
                vector.reciprocal(COLB[:, 0:1], COLC[:, 0:1])
                vector.tensor_copy(out=DUM[:, 32:33], in_=COLB[:, 0:1])
                vector.tensor_copy(out=DUM[:, 48:49], in_=DUM[:, 32:33])
                vector.tensor_scalar(FB3[:, :], FB4[:, :], COLA[:, 2:3],
                                     COLB[:, 0:1], ALU.subtract, ALU.mult)
                vector.tensor_copy(out=DUMW[:, 0:128], in_=IDB[:, 0:128]).then_inc(sm.l_dv, 1)

    stack.close()
    return nc


def _build_masks(sub):
    m1 = np.zeros((4, 128, 8 * 512), np.float32)
    m2 = np.zeros((4, 128, 8 * 512), np.float32)
    for sl in range(4):
        P = SLOT_CHUNKS[sub][sl]
        for j in range(2):
            k5 = KT512[sl] - 2 + j
            for qt in range(4):
                q = np.arange(128)[:, None] + qt * 128 + P * 512
                k = np.arange(512)[None, :] + k5 * 512
                m1[sl][:, (j * 4 + qt) * 512:(j * 4 + qt + 1) * 512] = \
                    np.where(k <= q, 0.0, NEGV)
        for j in range(8):
            kt = KT128[sl] - 8 + j
            k = np.arange(128)[:, None] + kt * 128
            q = np.arange(512)[None, :] + P * 512
            m2[sl][:, j * 512:(j + 1) * 512] = np.where(k <= q, 0.0, NEGV)
    m1 = m1.reshape(4 * 128, 8 * 512).astype(ml_dtypes.bfloat16)
    m2 = m2.reshape(4 * 128, 8 * 512).astype(ml_dtypes.bfloat16)
    return m1, m2


_NC_CACHE = [None]


def kernel(L0_attn_x, L0_attn_Wq, L0_attn_Wk, L0_attn_Wv, L0_attn_Wo,
           L0_attn_gq, L0_attn_gk, L0_ffn_gate_W_packed, L0_ffn_gate_scales,
           L0_ffn_up_W_packed, L0_ffn_up_scales, L0_ffn_down_W_packed,
           L0_ffn_down_scales):
    x = np.ascontiguousarray(np.asarray(L0_attn_x, dtype=np.float32))
    Wq = np.ascontiguousarray(np.asarray(L0_attn_Wq, dtype=np.float32))
    Wk = np.asarray(L0_attn_Wk, dtype=np.float32)
    Wv = np.asarray(L0_attn_Wv, dtype=np.float32)
    Wkv = np.ascontiguousarray(np.concatenate([Wk, Wv], axis=0))
    Wo = np.ascontiguousarray(np.asarray(L0_attn_Wo, dtype=np.float32))
    gq = np.ascontiguousarray(np.asarray(L0_attn_gq, dtype=np.float32
                                         ).reshape(D, 1))
    gk = np.ascontiguousarray(np.asarray(L0_attn_gk, dtype=np.float32
                                         ).reshape(D, 1))
    gW = np.ascontiguousarray(np.asarray(L0_ffn_gate_W_packed, np.int32).T)
    uW = np.ascontiguousarray(np.asarray(L0_ffn_up_W_packed, np.int32).T)
    dW = np.ascontiguousarray(np.asarray(L0_ffn_down_W_packed, np.int32).T)
    gS = np.ascontiguousarray(np.asarray(L0_ffn_gate_scales, np.float32).T)
    uS = np.ascontiguousarray(np.asarray(L0_ffn_up_scales, np.float32).T)
    dS = np.ascontiguousarray(np.asarray(L0_ffn_down_scales, np.float32).T)
    e8v = np.zeros((8, 128), np.float32)
    for jj in range(8):
        e8v[jj, jj * 16:(jj + 1) * 16] = -1.0
    identv = np.eye(128, dtype=np.float32).astype(ml_dtypes.bfloat16)

    if _NC_CACHE[0] is None:
        _NC_CACHE[0] = build_nc()
    nc = _NC_CACHE[0]
    in_maps = []
    for c in range(8):
        b, sub = c // 2, c % 2
        chunks = SLOT_CHUNKS[sub]
        xqv = np.ascontiguousarray(
            np.concatenate([x[b, ch * 512:(ch + 1) * 512] for ch in chunks], 0))
        m1, m2 = _build_masks(sub)
        in_maps.append({
            "xb": np.ascontiguousarray(x[b]),
            "xq": xqv,
            "wq": Wq, "wkv": Wkv, "wo": Wo, "gqv": gq, "gkv": gk,
            "gsh": np.ascontiguousarray(gW[c * 128:(c + 1) * 128]),
            "ush": np.ascontiguousarray(uW[c * 128:(c + 1) * 128]),
            "dsh": np.ascontiguousarray(dW[c * 1024:(c + 1) * 1024]),
            "gsc": np.ascontiguousarray(gS[c * 8:(c + 1) * 8]),
            "usc": np.ascontiguousarray(uS[c * 8:(c + 1) * 8]),
            "dsc": np.ascontiguousarray(dS[c * 64:(c + 1) * 64]),
            "e8": e8v, "identin": identv, "m1in": m1, "m2in": m2,
        })
    res = run_bass_kernel_spmd(nc, in_maps, core_ids=list(range(8)))
    out = np.empty((4, S, HB), np.float32)
    for c in range(8):
        b, sub = c // 2, c % 2
        y2 = res.results[c]["y2out"]
        for sl, ch in enumerate(SLOT_CHUNKS[sub]):
            out[b, ch * 512:(ch + 1) * 512] = y2[sl * 512:(sl + 1) * 512]
    return out


# version 12 (final)
# speedup vs baseline: 1.1786x; 1.0068x over previous
import numpy as np
import ml_dtypes
import concourse.bass as bass
import concourse.mybir as mybir
from concourse.bass_utils import run_bass_kernel_spmd

F32 = mybir.dt.float32
BF16 = mybir.dt.bfloat16
I32 = mybir.dt.int32
I16 = mybir.dt.int16
ALU = mybir.AluOpType
AF = mybir.ActivationFunctionType
AX = mybir.AxisListType

HB = 2048
S = 4096
TL = 2048
NH = 16
KVH = 4
D = 128
MLP = 16384
EPS_RMS = 1e-6
EPS_LN = 1e-5
QK_SCALE = float(D) ** -0.5
G_C1 = 0.044715 * 0.7978845608028654
G_C2 = 0.7978845608028654
KT512 = [8, 6, 4, 2]
KT128 = [32, 24, 16, 8]
SLOT_CHUNKS = {0: [7, 4, 3, 0], 1: [6, 5, 2, 1]}
NEGV = -60000.0
DEBUG = False

DCH = ([("g", 0, j) for j in range(8)] + [("u", 0, j) for j in range(8)]
       + [("d", t, 0) for t in range(8)])
NDC = len(DCH)
A3L = [(sl, h) for sl in range(4) for h in range(16)]


def _a3_sched():
    n1 = n2 = 0
    out = []
    for i, (sl, h) in enumerate(A3L):
        u = {"i": i, "sl": sl, "h": h, "n1_0": n1, "n2_0": n2}
        n1 += 4 * KT512[sl]
        n2 += KT128[sl]
        u["n1_1"], u["n2_1"] = n1, n2
        out.append(u)
    return out


A3S = _a3_sched()

A4_BASE = 40


def _a4_blk_cnt(sl, pr, ks):
    return sl * A4_BASE + 4 + pr * 18 + ks + 1


def _a4_y1t_cnt(sl, qt):
    return sl * A4_BASE + 4 + (qt // 2) * 18 + 16 + (qt % 2) + 1


def build_nc():
    nc = bass.Bass("TRN2", target_bir_lowering=False,
                   detect_race_conditions=False)
    xb = nc.dram_tensor("xb", [S, HB], F32, kind="ExternalInput").ap()
    xq = nc.dram_tensor("xq", [TL, HB], F32, kind="ExternalInput").ap()
    wq = nc.dram_tensor("wq", [HB, NH * D], F32, kind="ExternalInput").ap()
    wkv = nc.dram_tensor("wkv", [2 * HB, KVH * D], F32, kind="ExternalInput").ap()
    wo = nc.dram_tensor("wo", [NH * D, HB], F32, kind="ExternalInput").ap()
    gqv = nc.dram_tensor("gqv", [D, 1], F32, kind="ExternalInput").ap()
    gkv = nc.dram_tensor("gkv", [D, 1], F32, kind="ExternalInput").ap()
    gsh = nc.dram_tensor("gsh", [128, MLP], I32, kind="ExternalInput").ap()
    ush = nc.dram_tensor("ush", [128, MLP], I32, kind="ExternalInput").ap()
    dsh = nc.dram_tensor("dsh", [1024, HB], I32, kind="ExternalInput").ap()
    gsc = nc.dram_tensor("gsc", [8, MLP], F32, kind="ExternalInput").ap()
    usc = nc.dram_tensor("usc", [8, MLP], F32, kind="ExternalInput").ap()
    dsc = nc.dram_tensor("dsc", [64, HB], F32, kind="ExternalInput").ap()
    e8 = nc.dram_tensor("e8", [8, 128], F32, kind="ExternalInput").ap()
    identin = nc.dram_tensor("identin", [128, 128], BF16, kind="ExternalInput").ap()
    m1in = nc.dram_tensor("m1in", [4 * 128, 8 * 512], BF16, kind="ExternalInput").ap()
    m2in = nc.dram_tensor("m2in", [4 * 128, 8 * 512], BF16, kind="ExternalInput").ap()
    y2out = nc.dram_tensor("y2out", [TL, HB], F32, kind="ExternalOutput").ap()
    if DEBUG:
        y1d_out = nc.dram_tensor("y1d_out", [TL, HB], F32, kind="ExternalOutput").ap()
        qt_out = nc.dram_tensor("qt_out", [NH * D, TL], BF16, kind="ExternalOutput").ap()
        gg_out = nc.dram_tensor("gg_out", [HB, MLP], BF16, kind="ExternalOutput").ap()
        gd_out = nc.dram_tensor("gd_out", [MLP, HB], BF16, kind="ExternalOutput").ap()
        pt_out = nc.dram_tensor("pt_out", [TL, HB], F32, kind="ExternalOutput").ap()
        qf_dbg = nc.dram_tensor("qf_dbg", [128, HB], F32, kind="ExternalOutput").ap()
        qn2_dbg = nc.dram_tensor("qn2_dbg", [128, HB], BF16, kind="ExternalOutput").ap()
        ofn_dbg = nc.dram_tensor("ofn_dbg", [128, 640], F32, kind="ExternalOutput").ap()
        os_dbg = nc.dram_tensor("os_dbg", [128, 512], BF16, kind="ExternalOutput").ap()
        mcb_dbg = nc.dram_tensor("mcb_dbg", [128, 8], F32, kind="ExternalOutput").ap()
        negm_dbg = nc.dram_tensor("negm_dbg", [1, 512], BF16, kind="ExternalOutput").ap()
        a4c_dbg = nc.dram_tensor("a4c_dbg", [128, 8], F32, kind="ExternalOutput").ap()
        fb2_dbg = nc.dram_tensor("fb2_dbg", [128, HB], F32, kind="ExternalOutput").ap()
        it_dbg = nc.dram_tensor("it_dbg", [128, 16384], BF16, kind="ExternalOutput").ap()
        y1t_dbg = nc.dram_tensor("y1t_dbg", [128, 32768], BF16, kind="ExternalOutput").ap()
        y1td_dbg = nc.dram_tensor("y1td_dbg", [2048, 2048], BF16, kind="ExternalOutput").ap()
        qe_dbg = nc.dram_tensor("qe_dbg", [128, HB], BF16, kind="ExternalOutput").ap()
        col_dbg = nc.dram_tensor("col_dbg", [128, 48], F32, kind="ExternalOutput").ap()
        qn_dbg = nc.dram_tensor("qn_dbg", [128, HB], BF16, kind="ExternalOutput").ap()

    wo_bf = nc.dram_tensor("wo_bf", [NH * D, HB], BF16, kind="Internal").ap()
    qt_dram = nc.dram_tensor("qt_dram", [NH * D, TL], BF16, kind="Internal").ap()
    y1_dram = nc.dram_tensor("y1_dram", [TL, HB], F32, kind="Internal").ap()
    y1t_dram = nc.dram_tensor("y1t_dram", [HB, TL], BF16, kind="Internal").ap()
    part = nc.dram_tensor("part", [TL, HB], F32, kind="Internal").ap()
    pg = nc.dram_tensor("pg", [256, MLP], BF16, kind="Internal").ap()
    pu = nc.dram_tensor("pu", [256, MLP], BF16, kind="Internal").ap()
    pd = nc.dram_tensor("pd", [2048, HB], BF16, kind="Internal").ap()
    gg = nc.dram_tensor("gg", [HB, MLP], BF16, kind="Internal").ap()
    gu = nc.dram_tensor("gu", [HB, MLP], BF16, kind="Internal").ap()
    gd = nc.dram_tensor("gd", [MLP, HB], BF16, kind="Internal").ap()

    import contextlib
    stack = contextlib.ExitStack()

    def sbt(name, shape, dt):
        return stack.enter_context(nc.sbuf_tensor(name, shape, dt))

    P64A = sbt("P64A", [128, 32768], BF16)
    P32A = sbt("P32A", [128, 16384], BF16)
    P32B = sbt("P32B", [128, 16384], BF16)
    MISC = sbt("MISC", [128, 8192], BF16)
    FB1 = sbt("FB1", [128, 2048], F32)
    FB2 = sbt("FB2", [128, 2048], F32)
    FB3 = sbt("FB3", [128, 2048], F32)
    FB4 = sbt("FB4", [128, 2048], F32)
    BB1 = sbt("BB1", [128, 4096], BF16)
    BB2 = sbt("BB2", [128, 4096], BF16)
    BB3 = sbt("BB3", [128, 4096], BF16)
    E8S = sbt("E8S", [8, 128], F32)
    IDB = sbt("IDB", [128, 128], BF16)
    ONES1 = sbt("ONES1", [1, 128], BF16)
    ONESC = sbt("ONESC", [128, 1], BF16)
    GQC = sbt("GQC", [128, 1], F32)
    GKC = sbt("GKC", [128, 1], F32)
    GQS = sbt("GQS", [128, 1], F32)
    COLA = sbt("COLA", [128, 16], F32)
    COLB = sbt("COLB", [128, 16], F32)
    COLC = sbt("COLC", [128, 16], F32)
    MCOL = sbt("MCOL", [128, 4], F32)
    MCB = sbt("MCB", [128, 4], BF16)
    DUM = sbt("DUM", [128, 64], F32)
    DUMW = sbt("DUMW", [128, 256], F32)
    NEGM = sbt("NEGM", [1, 512], BF16)
    PTB = sbt("PTB", [128, 1024], BF16)
    QTS = sbt("QTS", [128, 1024], BF16)
    RC = sbt("RC", [128, 4], F32)
    EPSC = sbt("EPSC", [128, 2], F32)

    CSD = P64A[:, 0:8192].bitcast(I32)
    DI1 = P64A[:, 8192:12288].bitcast(I32)
    DI2 = P64A[:, 12288:16384].bitcast(I32)
    DI3 = P64A[:, 16384:20480].bitcast(I32)
    DI4 = P64A[:, 20480:24576].bitcast(I32)
    DBITS = P64A[:, 24576:28672].bitcast(I16)
    DT1 = P64A[:, 28672:30720].bitcast(I16)
    DNSX = P32A[:, 0:4096]
    DLO = P32A[:, 4096:6144]
    DHI = P32A[:, 6144:8192]
    DC1 = P32A[:, 8192:10240]
    DC2 = P32A[:, 10240:12288]
    DT2 = P32A[:, 12288:14336]
    DT3 = P32A[:, 14336:16384]
    WQS = P64A
    WKS = P64A[:, 0:8192]
    WVS = P64A[:, 8192:16384]
    KTS = P32A
    VS = P32B
    OS = P64A[:, 16384:24576]
    OTS = P64A[:, 24576:32768]
    M1S = MISC[:, 0:4096]
    M2S = MISC[:, 4096:8192]
    SCL0 = FB1[0:8, 0:2048]
    SCL1 = FB2[0:8, 0:2048]
    PS1F = P64A[:, 0:2048].bitcast(F32)
    NEGF = P64A[0:1, 2048:3072].bitcast(F32)
    OFN = P64A[:, 3072:4352].bitcast(F32)
    PS2F = P64A[:, 4352:6400].bitcast(F32)
    AOF = P64A[:, 8192:16384].bitcast(F32)
    Y1TS = P64A
    INTERT = P32A
    GUBF = MISC
    WD8 = P32B

    PSA = stack.enter_context(nc.psum_tensor("PSA", [128, 2048], F32))
    PSB = stack.enter_context(nc.psum_tensor("PSB", [128, 2048], F32))
    PSAb = PSA[:, :].bitcast(BF16)
    PSBb = PSB[:, :].bitcast(BF16)

    sems = {}
    for n in ("c_ld", "d_l0", "d_l1", "d_sc", "d_pe", "d_ac", "d_dv", "d_cc",
              "w_g0", "w_g1", "w_s0", "w_s1",
              "q_gw", "q_g0", "q_g1", "q_pe", "q_pe2", "q_ac", "q_dv", "q_st",
              "k_gw", "k_g0", "k_g1", "k_pe", "k_pe2", "k_ac", "k_dv",
              "a_lm", "a_l0", "a_l1", "a_p1", "a_a1", "a_d1", "a_dm",
              "a_pr", "a_an", "a_dn", "a_ao",
              "a_p2", "a_e2", "a_d2", "a_ae", "a_pe", "a_av", "a_df",
              "o_l0", "o_l1", "o_lx", "o_pe", "o_oa", "o_ac", "o_dv",
              "o_sy", "o_sz",
              "f_ly", "f_lw", "f_lg0", "f_lg1", "f_pe", "f_pe2",
              "f_ac", "f_ac2", "f_tg", "f_th", "f_dv", "f_g0", "f_g1",
              "l_ld", "l_ac", "l_dv", "l_st"):
        sems[n] = stack.enter_context(nc.semaphore(n))
    sm = type("SM", (), sems)
    DL = [sems["d_l0"], sems["d_l1"]]
    WG = [sems["w_g0"], sems["w_g1"]]
    WS = [sems["w_s0"], sems["w_s1"]]
    QG = [sems["q_g0"], sems["q_g1"]]
    KG = [sems["k_g0"], sems["k_g1"]]
    AL = [sems["a_l0"], sems["a_l1"]]
    OL = [sems["o_l0"], sems["o_l1"]]
    FLG = [sems["f_lg0"], sems["f_lg1"]]
    FG = [sems["f_g0"], sems["f_g1"]]

    with nc.Block() as block:

        def emit_a4_tensor(tensor, sl):
            tensor.wait_ge(sm.a_df, 16 * (sl + 1))
            tensor.wait_ge(sm.a_ae, A3S[16 * (sl + 1) - 1]["n2_1"])
            tensor.wait_ge(sm.a_d1, A3S[16 * (sl + 1) - 1]["n1_1"])
            for r in range(4):
                if r >= 1:
                    tensor.wait_ge(sm.o_oa, sl * 4 + r)
                for jj in range(16):
                    f = r * 16 + jj
                    ks, qt = f // 4, f % 4
                    op = tensor.transpose(PSAb[:, jj * 128:(jj + 1) * 128],
                                          OS[:, qt * 2048 + ks * 128:
                                             qt * 2048 + (ks + 1) * 128],
                                          IDB[:, :])
                op.then_inc(sm.o_pe, 1)
            tensor.wait_ge(sm.o_oa, sl * 4 + 4)
            for pr in range(2):
                if pr == 1:
                    tensor.wait_ge(sm.o_dv, 3 * (sl * 4 + 2))
                    tensor.wait_ge(sm.o_ac, 3 * (sl * 4 + 1) + 3)
                for ks in range(16):
                    it = sl * 32 + pr * 16 + ks
                    tensor.wait_ge(OL[it % 2], 16 * (it // 2 + 1))
                    wob = BB1[:, (it % 2) * 2048:(it % 2) * 2048 + 2048]
                    for qt2 in range(2):
                        qt = pr * 2 + qt2
                        aop = PSA if qt2 == 0 else PSB
                        for hc in range(4):
                            op = tensor.matmul(aop[:, hc * 512:(hc + 1) * 512],
                                               OTS[:, (ks * 4 + qt) * 128:
                                                   (ks * 4 + qt + 1) * 128],
                                               wob[:, hc * 512:(hc + 1) * 512],
                                               start=(ks == 0), stop=(ks == 15))
                    op.then_inc(sm.o_pe, 1)
                for qt2 in range(2):
                    qt = pr * 2 + qt2
                    it2 = sl * 4 + qt
                    tensor.wait_ge(sm.o_dv, 3 * it2 + 3)
                    if it2 >= 2:
                        tensor.wait_ge(sm.o_ac, 3 * (it2 - 2) + 3)
                    y1b = BB3[:, (qt % 2) * 2048:(qt % 2) * 2048 + 2048]
                    dstb = PSAb if qt2 == 0 else PSBb
                    for hh in range(16):
                        op = tensor.transpose(dstb[:, hh * 128:(hh + 1) * 128],
                                              y1b[:, hh * 128:(hh + 1) * 128],
                                              IDB[:, :])
                    op.then_inc(sm.o_pe, 1)

        def emit_a4_scalar(scalar, sl):
            for r in range(4):
                scalar.wait_ge(sm.o_pe, sl * A4_BASE + r + 1)
                op = scalar.activation(OTS[:, r * 2048:(r + 1) * 2048],
                                       PSAb[:, 0:2048], AF.Identity)
                if r == 3:
                    op = scalar.activation(DUMW[:, 128:256], IDB[:, 0:128], AF.Identity)
                op.then_inc(sm.o_oa, 1)
            def y1t_store(j):
                scalar.dma_start(
                    out=bass.AP(y1t_dram.tensor, j * 128,
                                [[TL, 128], [128 * TL, 16], [1, 128]]),
                    in_=BB2[:, (j % 2) * 2048:(j % 2) * 2048 + 2048
                            ].rearrange("p (h j) -> p h j", h=16)
                ).then_inc(sm.o_sz, 16)

            for qt in range(4):
                it2 = sl * 4 + qt
                pr, qt2 = qt // 2, qt % 2
                scalar.wait_ge(sm.o_pe, _a4_blk_cnt(sl, pr, 15))
                if it2 >= 2:
                    scalar.wait_ge(sm.o_dv, 3 * (it2 - 2) + 1)
                aop = PSA if qt2 == 0 else PSB
                scalar.activation(AOF[:, (qt % 2) * 2048:(qt % 2) * 2048 + 2048],
                                  aop[:, :], AF.Identity)
                scalar.activation(DUMW[:, 128:256], IDB[:, 0:128], AF.Identity).then_inc(sm.o_ac, 1)
                if it2 >= 2:
                    scalar.wait_ge(sm.o_sz, 16 * (it2 - 1))
                scalar.wait_ge(sm.o_dv, 3 * it2 + 2)
                if DEBUG and it2 == 0:
                    scalar.dma_start(out=a4c_dbg[:, :], in_=COLA[:, 0:8]
                                     ).then_inc(sm.c_ld, 16)
                    scalar.dma_start(out=fb2_dbg[:, :], in_=FB2[:, :]
                                     ).then_inc(sm.c_ld, 16)
                    scalar.wait_ge(sm.c_ld, 64 + 16 * 16)
                scalar.activation(COLC[:, 0:1], COLA[:, 5:6], AF.Sqrt,
                                  bias=EPSC[:, 1:2], scale=1.0)
                scalar.activation(DUM[:, 0:1], COLC[:, 0:1], AF.Sqrt
                                  ).then_inc(sm.o_ac, 1)
                scalar.wait_ge(sm.o_dv, 3 * it2 + 3)
                scalar.dma_start(out=y1_dram[it2 * 128:(it2 + 1) * 128, :],
                                 in_=FB4[:, :]).then_inc(sm.o_sy, 16)
                scalar.wait_ge(sm.o_pe, _a4_y1t_cnt(sl, qt))
                pb = PSAb if qt % 2 == 0 else PSBb
                scalar.activation(BB2[:, (qt % 2) * 2048:(qt % 2) * 2048 + 2048],
                                  pb[:, 0:2048], AF.Identity).then_inc(sm.o_ac, 1)
                if it2 >= 1:
                    y1t_store(it2 - 1)
            if sl == 3:
                scalar.wait_ge(sm.o_sy, 16 * 16)
                scalar.wait_ge(sm.o_sz, 16 * 15)
                y1t_store(15)

        def emit_a4_vector(vector, sl):
            for pr in range(2):
                for qt2 in range(2):
                    qt = pr * 2 + qt2
                    it2 = sl * 4 + qt
                    vector.wait_ge(sm.o_ac, 3 * it2 + 1)
                    vector.wait_ge(sm.o_lx, 16 * (it2 + 1))
                    vector.scalar_tensor_tensor(FB2[:, :],
                                                AOF[:, (qt % 2) * 2048:
                                                    (qt % 2) * 2048 + 2048],
                                                0.0, FB1[:, :],
                                                ALU.add, ALU.add,
                                                accum_out=COLA[:, 0:1]
                                                ).then_inc(sm.o_dv, 1)
                    vector.scalar_tensor_tensor(FB3[:, :], FB2[:, :], 0.0,
                                                FB2[:, :], ALU.add, ALU.mult,
                                                accum_out=COLA[:, 1:2])
                    vector.tensor_scalar(COLA[:, 2:3], COLA[:, 0:1], 1.0 / HB,
                                         None, ALU.mult)
                    vector.tensor_copy(out=DUM[:, 32:48], in_=IDB[:, 0:16])
                    vector.tensor_copy(out=DUM[:, 48:64], in_=IDB[:, 16:32])
                    vector.tensor_tensor(COLA[:, 3:4], COLA[:, 2:3],
                                         COLA[:, 2:3], ALU.mult)
                    vector.tensor_scalar(COLA[:, 4:5], COLA[:, 1:2], 1.0 / HB,
                                         None, ALU.mult)
                    vector.tensor_copy(out=DUM[:, 32:48], in_=IDB[:, 0:16])
                    vector.tensor_copy(out=DUM[:, 48:64], in_=IDB[:, 16:32])
                    vector.tensor_tensor(COLA[:, 5:6], COLA[:, 4:5],
                                         COLA[:, 3:4], ALU.subtract)
                    vector.tensor_copy(out=DUM[:, 16:17], in_=COLA[:, 5:6]
                                       ).then_inc(sm.o_dv, 1)
                    vector.wait_ge(sm.o_ac, 3 * it2 + 2)
                    vector.reciprocal(COLB[:, 0:1], COLC[:, 0:1])
                    vector.tensor_copy(out=DUM[:, 32:33], in_=COLB[:, 0:1])
                    vector.tensor_copy(out=DUM[:, 48:49], in_=DUM[:, 32:33])
                    if it2 >= 1:
                        vector.wait_ge(sm.o_sy, 16 * it2)
                    vector.tensor_scalar(FB4[:, :], FB2[:, :], COLA[:, 2:3],
                                         COLB[:, 0:1], ALU.subtract, ALU.mult)
                    if it2 >= 2:
                        vector.wait_ge(sm.o_pe, _a4_y1t_cnt(sl, qt - 2)
                                       if qt >= 2 else
                                       _a4_y1t_cnt(sl - 1, qt + 2))
                    vector.tensor_copy(out=BB3[:, (qt % 2) * 2048:
                                               (qt % 2) * 2048 + 2048],
                                       in_=FB4[:, :])
                    vector.tensor_copy(out=DUMW[:, 0:128], in_=IDB[:, 0:128]).then_inc(sm.o_dv, 1)

        @block.sync
        def _(sync):
            sync.dma_start(out=E8S[:, :], in_=e8[:, :]).then_inc(sm.c_ld, 16)
            sync.dma_start(out=GQC[:, :], in_=gqv[:, :]).then_inc(sm.c_ld, 16)
            sync.dma_start(out=GKC[:, :], in_=gkv[:, :]).then_inc(sm.c_ld, 16)
            sync.dma_start(out=IDB[:, :], in_=identin[:, :]).then_inc(sm.c_ld, 16)
            for j, (m, wt, ch) in enumerate(DCH):
                if j >= 2:
                    sync.wait_ge(sm.d_dv, j - 1)
                    sync.wait_ge(sm.d_pe, j - 1)
                buf = CSD[:, (j % 2) * 2048:(j % 2) * 2048 + 2048]
                sc = (SCL0 if j % 2 == 0 else SCL1)[:, :]
                dl = DL[j % 2]
                if m == "g":
                    sync.dma_start(out=buf, in_=gsh[:, ch * 2048:(ch + 1) * 2048]
                                   ).then_inc(dl, 16)
                    sync.dma_start(out=sc, in_=gsc[:, ch * 2048:(ch + 1) * 2048]
                                   ).then_inc(dl, 16)
                elif m == "u":
                    sync.dma_start(out=buf, in_=ush[:, ch * 2048:(ch + 1) * 2048]
                                   ).then_inc(dl, 16)
                    sync.dma_start(out=sc, in_=usc[:, ch * 2048:(ch + 1) * 2048]
                                   ).then_inc(dl, 16)
                else:
                    sync.dma_start(out=buf, in_=dsh[wt * 128:(wt + 1) * 128, :]
                                   ).then_inc(dl, 16)
                    sync.dma_start(out=sc, in_=dsc[wt * 8:(wt + 1) * 8, :]
                                   ).then_inc(dl, 16)
            def emit_xq(sl, qt):
                it2 = sl * 4 + qt
                if it2 >= 1:
                    sync.wait_ge(sm.o_dv, 3 * it2 - 2)
                sync.dma_start(
                    out=FB1[:, :],
                    in_=xq[(sl * 4 + qt) * 128:(sl * 4 + qt + 1) * 128, :]
                ).then_inc(sm.o_lx, 16)

            def emit_a4_sync(sl):
                emit_xq(sl, 0)
                for pr in range(2):
                    for ks in range(16):
                        it = sl * 32 + pr * 16 + ks
                        if it == 0:
                            sync.wait_ge(WS[0], 16 * 8)
                            sync.wait_ge(WS[1], 16 * 8)
                        if it >= 2:
                            it_prev = it - 2
                            slp, rem = divmod(it_prev, 32)
                            prp, ksp = divmod(rem, 16)
                            sync.wait_ge(sm.o_pe, _a4_blk_cnt(slp, prp, ksp))
                        sync.wait_ge(sm.a_df, 16 * (sl + 1))
                        sync.dma_start(
                            out=BB1[:, (it % 2) * 2048:(it % 2) * 2048 + 2048],
                            in_=wo_bf[ks * 128:(ks + 1) * 128, :]
                        ).then_inc(OL[it % 2], 16)
                    emit_xq(sl, 1 + pr)
                emit_xq(sl, 3)

            for u in A3S:
                i, sl, h = u["i"], u["sl"], u["h"]
                if h == 0:
                    if sl > 0:
                        emit_a4_sync(sl - 1)
                        sync.wait_ge(sm.a_df, i)
                    sync.dma_start(out=M1S[:, :],
                                   in_=m1in[sl * 128:(sl + 1) * 128, :]
                                   ).then_inc(sm.a_lm, 16)
                    sync.dma_start(out=M2S[:, :],
                                   in_=m2in[sl * 128:(sl + 1) * 128, :]
                                   ).then_inc(sm.a_lm, 16)
                if i == 0:
                    sync.wait_ge(sm.q_st, 16 * 16)
                if i >= 2:
                    sync.wait_ge(sm.a_df, i - 1)
                sync.dma_start(
                    out=QTS[:, (i % 2) * 512:(i % 2) * 512 + 512],
                    in_=qt_dram[h * 128:(h + 1) * 128, sl * 512:(sl + 1) * 512]
                ).then_inc(AL[i % 2], 16)
            emit_a4_sync(3)
            sync.wait_ge(sm.o_sy, 16 * 16)
            sync.wait_ge(sm.o_sz, 16 * 16)
            for hc in range(16):
                sync.dma_start(out=Y1TS[:, hc * 2048:(hc + 1) * 2048],
                               in_=y1t_dram[hc * 128:(hc + 1) * 128, :]
                               ).then_inc(sm.f_ly, 16)
            for s in range(16):
                for sub in range(8):
                    if sub == 0:
                        if s == 0:
                            sync.wait_ge(sm.d_cc, 3)
                        else:
                            sync.wait_ge(sm.f_pe2, 16 * s)
                    sync.dma_start(
                        out=WD8[:, sub * 2048:(sub + 1) * 2048],
                        in_=gd[(s * 8 + sub) * 128:(s * 8 + sub + 1) * 128, :]
                    ).then_inc(sm.f_lw, 16)
                for sub in range(8):
                    for mi, gap in enumerate((gg, gu)):
                        gidx = s * 8 + sub
                        if gidx >= 2:
                            sync.wait_ge(sm.f_pe, 4 * (gidx - 1))
                        slot = (sub % 2) * 2 + mi
                        sync.dma_start(
                            out=GUBF[:, slot * 2048:(slot + 1) * 2048],
                            in_=bass.AP(gap.tensor, gidx * 128,
                                        [[MLP, 128], [128 * MLP, 16], [1, 128]])
                        ).then_inc(FLG[sub % 2], 16)
            sync.wait_ge(FG[0], 16 * 128)
            sync.wait_ge(FG[1], 16 * 128)
            for tg in range(16):
                if tg >= 1:
                    sync.wait_ge(sm.l_dv, 2 * tg)
                sync.dma_start(out=FB1[:, :],
                               in_=part[tg * 128:(tg + 1) * 128, :]
                               ).then_inc(sm.l_ld, 16)
                sync.dma_start(out=FB2[:, :],
                               in_=y1_dram[tg * 128:(tg + 1) * 128, :]
                               ).then_inc(sm.l_ld, 16)
            if DEBUG:
                sync.dma_start(out=y1d_out[:, :], in_=y1_dram[:, :]
                               ).then_inc(sm.c_ld, 16)
                sync.dma_start(out=qt_out[:, :], in_=qt_dram[:, :]
                               ).then_inc(sm.c_ld, 16)
                sync.dma_start(out=gg_out[:, :], in_=gg[:, :]
                               ).then_inc(sm.c_ld, 16)
                sync.dma_start(out=gd_out[:, :], in_=gd[:, :]
                               ).then_inc(sm.c_ld, 16)
                sync.dma_start(out=pt_out[:, :], in_=part[:, :]
                               ).then_inc(sm.c_ld, 16)
                sync.wait_ge(sm.c_ld, 64 + 10 * 16)

        @block.gpsimd
        def _(gpsimd):
            gpsimd.wait_ge(sm.d_dv, NDC)
            for i in range(16):
                gpsimd.dma_start(out=WQS[:, i * 2048:(i + 1) * 2048],
                                 in_=wq[i * 128:(i + 1) * 128, :]
                                 ).then_inc(sm.q_gw, 16)
            for qt in range(16):
                if qt >= 2:
                    gpsimd.wait_ge(sm.q_pe2, qt - 1)
                gpsimd.dma_start(out=BB1[:, (qt % 2) * 2048:(qt % 2) * 2048 + 2048],
                                 in_=xq[qt * 128:(qt + 1) * 128, :]
                                 ).then_inc(QG[qt % 2], 16)
            for i in range(16):
                if i >= 2:
                    gpsimd.wait_ge(WS[i % 2], 16 * ((i - 2) // 2 + 1))
                gpsimd.dma_start(out=P32B[:, (i % 2) * 2048:(i % 2) * 2048 + 2048],
                                 in_=wo[i * 128:(i + 1) * 128, :]
                                 ).then_inc(WG[i % 2], 16)
            gpsimd.wait_ge(sm.q_pe, 32)
            gpsimd.dma_start(
                out=WKS[:, :].rearrange("p (t d) -> p t d", t=16),
                in_=bass.AP(wkv.tensor, 0, [[512, 128], [128 * 512, 16], [1, 512]])
            ).then_inc(sm.k_gw, 16)
            gpsimd.dma_start(
                out=WVS[:, :].rearrange("p (t d) -> p t d", t=16),
                in_=bass.AP(wkv.tensor, HB * 512,
                            [[512, 128], [128 * 512, 16], [1, 512]])
            ).then_inc(sm.k_gw, 16)
            for i in range(32):
                if i >= 2:
                    gpsimd.wait_ge(sm.k_pe2, i - 1)
                gpsimd.dma_start(out=BB1[:, (i % 2) * 2048:(i % 2) * 2048 + 2048],
                                 in_=xb[i * 128:(i + 1) * 128, :]
                                 ).then_inc(KG[i % 2], 16)
            for gi, (piece, gath) in enumerate(((pg, gg), (pu, gu), (pd, gd))):
                gpsimd.wait_ge(sm.d_sc, 16 * 16 * (gi + 1))
                gpsimd.collective_compute(
                    "AllGather", ALU.bypass,
                    replica_groups=[[0, 1, 2, 3, 4, 5, 6, 7]],
                    ins=[piece.opt()], outs=[gath.opt()],
                ).then_inc(sm.d_cc)
            for s in range(16):
                for tg in range(16):
                    it = s * 16 + tg
                    gpsimd.wait_ge(sm.f_ac2, it + 1)
                    if s >= 1:
                        prev = (s - 1) * 16 + tg
                        gpsimd.wait_ge(FG[tg % 2], 16 * (prev // 2 + 1))
                    stg = FB4 if tg % 2 == 0 else FB1
                    gpsimd.dma_start(
                        out=part[tg * 128:(tg + 1) * 128, :],
                        in_=stg[:, :],
                        accum_op=(ALU.bypass if s == 0 else ALU.add),
                    ).then_inc(FG[tg % 2], 16)
            gpsimd.wait_ge(FG[0], 16 * 128)
            gpsimd.wait_ge(FG[1], 16 * 128)

        @block.tensor
        def _(tensor):
            tensor.wait_ge(sm.c_ld, 64)
            for j in range(NDC):
                tensor.wait_ge(DL[j % 2], 32 * (j // 2 + 1))
                if j >= 1:
                    tensor.wait_ge(sm.d_ac, j)
                scl = SCL0 if j % 2 == 0 else SCL1
                for q4 in range(4):
                    op = tensor.matmul(PSA[:, q4 * 512:(q4 + 1) * 512],
                                       E8S[:, :],
                                       scl[0:8, q4 * 512:(q4 + 1) * 512],
                                       start=True, stop=True)
                op.then_inc(sm.d_pe, 1)
            tensor.wait_ge(sm.c_ld, 64)
            tensor.wait_ge(sm.q_gw, 16 * 16)
            for qt in range(16):
                tensor.wait_ge(QG[qt % 2], 16 * (qt // 2 + 1))
                if qt >= 1:
                    tensor.wait_ge(sm.q_ac, 3 * (qt - 1) + 2)
                xin = BB1[:, (qt % 2) * 2048:(qt % 2) * 2048 + 2048]
                for kc in range(16):
                    op = tensor.transpose(PSBb[:, kc * 128:(kc + 1) * 128],
                                          xin[:, kc * 128:(kc + 1) * 128],
                                          IDB[:, :])
                op.then_inc(sm.q_pe2, 1)
                tensor.wait_ge(sm.q_ac, 3 * qt + 1)
                xt = BB2[:, (qt % 2) * 2048:(qt % 2) * 2048 + 2048]
                for dqc in range(4):
                    for kc in range(16):
                        op = tensor.matmul(PSA[:, dqc * 512:(dqc + 1) * 512],
                                           xt[:, kc * 128:(kc + 1) * 128],
                                           WQS[:, kc * 2048 + dqc * 512:
                                               kc * 2048 + (dqc + 1) * 512],
                                           start=(kc == 0), stop=(kc == 15))
                op.then_inc(sm.q_pe, 1)
                tensor.wait_ge(sm.q_dv, 4 * qt + 3)
                for hh in range(16):
                    op = tensor.transpose(
                        PSBb[:, 2048 + hh * 128:2048 + (hh + 1) * 128],
                        xt[:, hh * 128:(hh + 1) * 128], IDB[:, :])
                op.then_inc(sm.q_pe, 1)
            tensor.wait_ge(sm.k_gw, 32)
            for tt in range(32):
                tensor.wait_ge(KG[tt % 2], 16 * (tt // 2 + 1))
                if tt == 0:
                    tensor.wait_ge(sm.q_ac, 48)
                    tensor.wait_ge(sm.q_dv, 64)
                if tt >= 1:
                    tensor.wait_ge(sm.k_ac, 4 * (tt - 1) + 1)
                xin = BB1[:, (tt % 2) * 2048:(tt % 2) * 2048 + 2048]
                for kc in range(16):
                    op = tensor.transpose(PSBb[:, kc * 128:(kc + 1) * 128],
                                          xin[:, kc * 128:(kc + 1) * 128],
                                          IDB[:, :])
                op.then_inc(sm.k_pe2, 1)
                tensor.wait_ge(sm.k_ac, 4 * tt + 1)
                xt = BB2[:, (tt % 2) * 2048:(tt % 2) * 2048 + 2048]
                for kc in range(16):
                    tensor.matmul(PSA[:, 0:512], xt[:, kc * 128:(kc + 1) * 128],
                                  WKS[:, kc * 512:(kc + 1) * 512],
                                  start=(kc == 0), stop=(kc == 15))
                for kc in range(16):
                    op = tensor.matmul(PSA[:, 512:1024],
                                       xt[:, kc * 128:(kc + 1) * 128],
                                       WVS[:, kc * 512:(kc + 1) * 512],
                                       start=(kc == 0), stop=(kc == 15))
                op.then_inc(sm.k_pe, 1)
                tensor.wait_ge(sm.k_dv, 4 * tt + 3)
                for sgi in range(4):
                    op = tensor.transpose(
                        PSBb[:, 2048 + sgi * 128:2048 + (sgi + 1) * 128],
                        xt[:, sgi * 128:(sgi + 1) * 128], IDB[:, :])
                op.then_inc(sm.k_pe, 1)
            tensor.wait_ge(sm.k_dv, 4 * 32)
            tensor.wait_ge(sm.k_ac, 4 * 32)
            for u in A3S:
                i, sl, h = u["i"], u["sl"], u["h"]
                kvh = h // 4
                qtl = QTS[:, (i % 2) * 512:(i % 2) * 512 + 512]
                if h == 0 and sl > 0:
                    emit_a4_tensor(tensor, sl - 1)
                    tensor.wait_ge(sm.o_ac, 12 * sl)
                tensor.wait_ge(sm.a_lm, 32 * (sl + 1))
                tensor.wait_ge(AL[i % 2], 16 * (i // 2 + 1))
                n1 = u["n1_0"]
                for qt in range(4):
                    for k5 in range(KT512[sl]):
                        if n1 >= 2:
                            tensor.wait_ge(sm.a_d1, n1 - 1)
                        tensor.matmul(PSA[:, (n1 % 2) * 512:(n1 % 2) * 512 + 512],
                                      qtl[:, qt * 128:(qt + 1) * 128],
                                      KTS[:, kvh * 4096 + k5 * 512:
                                          kvh * 4096 + (k5 + 1) * 512],
                                      start=True, stop=True
                                      ).then_inc(sm.a_p1, 1)
                        n1 += 1
                tensor.wait_ge(sm.a_dm, i + 1)
                if i >= 1:
                    tensor.wait_ge(sm.a_df, i)
                for qt in range(4):
                    op = tensor.matmul(PSB[0:1, 1024 + qt * 128:
                                           1024 + (qt + 1) * 128],
                                       MCB[:, qt:qt + 1], IDB[:, :],
                                       start=(qt == 0), stop=(qt == 3))
                op.then_inc(sm.a_pr, 1)
                tensor.wait_ge(sm.a_dn, i + 1)
                n2 = u["n2_0"]
                nk = KT128[sl]

                def emit_av(kt, n2v):
                    tensor.wait_ge(sm.a_ae, n2v + 1)
                    pt = PTB[:, (n2v % 2) * 512:(n2v % 2) * 512 + 512]
                    last = (kt == nk - 1)
                    for qt in range(4):
                        oacc = PSB[:, qt * 512:qt * 512 + 512]
                        tensor.matmul(oacc[:, 0:128],
                                      pt[:, qt * 128:(qt + 1) * 128],
                                      VS[:, kt * 512 + kvh * 128:
                                         kt * 512 + (kvh + 1) * 128],
                                      start=(kt == 0), stop=False)
                        op = tensor.matmul(oacc[:, 128:129],
                                           pt[:, qt * 128:(qt + 1) * 128],
                                           ONESC[:, :], start=False, stop=last)
                    op.then_inc(sm.a_av, 1)

                for kt in range(nk):
                    if n2 >= 2:
                        tensor.wait_ge(sm.a_ae, n2 - 1)
                    bank = PSA[:, 1024 + (n2 % 2) * 512:
                               1024 + (n2 % 2) * 512 + 512]
                    tensor.matmul(bank,
                                  KTS[:, kvh * 4096 + kt * 128:
                                      kvh * 4096 + (kt + 1) * 128],
                                  qtl[:, :], start=True, stop=False)
                    tensor.matmul(bank, ONES1[:, :], NEGM[0:1, :],
                                  start=False, stop=True
                                  ).then_inc(sm.a_p2, 1)
                    if kt >= 1:
                        emit_av(kt - 1, n2 - 1)
                    n2 += 1
                emit_av(nk - 1, n2 - 1)
            emit_a4_tensor(tensor, 3)
            tensor.wait_ge(sm.o_ac, 48)
            tensor.wait_ge(sm.f_ly, 16 * 16)
            for s in range(16):
                if s >= 1:
                    tensor.wait_ge(sm.f_ac2, s * 16 - 1)
                for sub in range(8):
                    tensor.wait_ge(FLG[sub % 2], 32 * (s * 4 + sub // 2 + 1))
                    gslot = GUBF[:, ((sub % 2) * 2) * 2048:
                                 ((sub % 2) * 2) * 2048 + 2048]
                    uslot = GUBF[:, ((sub % 2) * 2 + 1) * 2048:
                                 ((sub % 2) * 2 + 1) * 2048 + 2048]
                    for tq in range(4):
                        k = (s * 8 + sub) * 4 + tq
                        if k >= 2:
                            tensor.wait_ge(sm.f_dv, k - 1)
                        gb = PSA[:, (tq % 2) * 512:(tq % 2) * 512 + 512]
                        ub = PSA[:, 1024 + (tq % 2) * 512:
                                 1024 + (tq % 2) * 512 + 512]
                        for hs in range(16):
                            tensor.matmul(gb, gslot[:, hs * 128:(hs + 1) * 128],
                                          Y1TS[:, hs * 2048 + tq * 512:
                                               hs * 2048 + (tq + 1) * 512],
                                          start=(hs == 0), stop=(hs == 15))
                        for hs in range(16):
                            op = tensor.matmul(ub,
                                               uslot[:, hs * 128:(hs + 1) * 128],
                                               Y1TS[:, hs * 2048 + tq * 512:
                                                    hs * 2048 + (tq + 1) * 512],
                                               start=(hs == 0), stop=(hs == 15))
                        op.then_inc(sm.f_pe, 1)
                tensor.wait_ge(sm.f_lw, 16 * 8 * (s + 1))
                for tg in range(16):
                    it = s * 16 + tg
                    accb = PSA if tg % 2 == 0 else PSB
                    tensor.wait_ge(sm.f_dv, 32 * s + 29 + tg // 4)
                    if tg == 0:
                        tensor.wait_ge(sm.f_ac, 64 * (s + 1))
                    if it >= 2:
                        tensor.wait_ge(sm.f_ac2, it - 1)
                    for sub in range(8):
                        for hc in range(4):
                            op = tensor.matmul(accb[:, hc * 512:(hc + 1) * 512],
                                               INTERT[:, sub * 2048 + tg * 128:
                                                      sub * 2048 + (tg + 1) * 128],
                                               WD8[:, sub * 2048 + hc * 512:
                                                   sub * 2048 + (hc + 1) * 512],
                                               start=(sub == 0), stop=(sub == 7))
                    op.then_inc(sm.f_pe2, 1)

        @block.scalar
        def _(scalar):
            def d_piece(jj):
                m, wt, ch = DCH[jj]
                scalar.wait_ge(sm.d_dv, jj + 1)
                if m in ("g", "u"):
                    dst = pg if m == "g" else pu
                    o_lo = bass.AP(dst.tensor, ch * 2048,
                                   [[2 * MLP, 128], [1, 2048]])
                    o_hi = bass.AP(dst.tensor, MLP + ch * 2048,
                                   [[2 * MLP, 128], [1, 2048]])
                else:
                    o_lo = bass.AP(pd.tensor, (2 * wt * 128) * HB,
                                   [[2 * HB, 128], [1, 2048]])
                    o_hi = bass.AP(pd.tensor, (2 * wt * 128 + 1) * HB,
                                   [[2 * HB, 128], [1, 2048]])
                scalar.dma_start(out=o_lo, in_=DLO[:, :]).then_inc(sm.d_sc, 16)
                scalar.dma_start(out=o_hi, in_=DHI[:, :]).then_inc(sm.d_sc, 16)

            for j in range(NDC):
                scalar.wait_ge(sm.d_pe, j + 1)
                scalar.activation(DNSX[:, (j % 2) * 2048:(j % 2) * 2048 + 2048],
                                  PSA[:, :], AF.Identity).then_inc(sm.d_ac, 1)
                if j >= 1:
                    d_piece(j - 1)
            d_piece(NDC - 1)
            for qt in range(16):
                scalar.wait_ge(sm.q_pe2, qt + 1)
                if qt >= 2:
                    scalar.wait_ge(sm.q_pe, 2 * (qt - 2) + 2)
                scalar.activation(BB2[:, (qt % 2) * 2048:(qt % 2) * 2048 + 2048],
                                  PSBb[:, 0:2048], AF.Identity)
                scalar.activation(DUMW[:, 128:256], IDB[:, 0:128], AF.Identity).then_inc(sm.q_ac, 1)
                scalar.wait_ge(sm.q_pe, 2 * qt + 1)
                if qt >= 1:
                    scalar.wait_ge(sm.q_dv, 4 * (qt - 1) + 4)
                scalar.activation(FB3[:, :], PSA[:, :], AF.Identity)
                scalar.activation(DUMW[:, 128:256], IDB[:, 0:128], AF.Identity).then_inc(sm.q_ac, 1)
                scalar.wait_ge(sm.q_dv, 4 * qt + 1)
                scalar.activation(COLC[:, 0:16], COLA[:, 0:16], AF.Sqrt,
                                  bias=EPSC[:, 0:1], scale=1.0 / D)
                scalar.activation(DUM[:, 0:16], COLC[:, 0:16], AF.Sqrt
                                  ).then_inc(sm.q_ac, 1)
                scalar.wait_ge(sm.q_dv, 4 * qt + 4)
                if DEBUG and qt == 0:
                    scalar.dma_start(out=qf_dbg[:, :], in_=FB3[:, :]
                                     ).then_inc(sm.c_ld, 16)
                    scalar.dma_start(out=col_dbg[:, 0:16], in_=COLA[:, :]
                                     ).then_inc(sm.c_ld, 16)
                    scalar.dma_start(out=col_dbg[:, 16:32], in_=COLB[:, :]
                                     ).then_inc(sm.c_ld, 16)
                    scalar.dma_start(out=col_dbg[:, 32:48], in_=COLC[:, :]
                                     ).then_inc(sm.c_ld, 16)
                    scalar.dma_start(out=qn_dbg[:, :], in_=BB3[:, 0:2048]
                                     ).then_inc(sm.c_ld, 16)
                    scalar.dma_start(out=qn2_dbg[:, :], in_=BB2[:, 0:2048]
                                     ).then_inc(sm.c_ld, 16)
                    scalar.dma_start(out=qe_dbg[:, :],
                                     in_=FB3[:, :].bitcast(BF16)[:, 0:2048]
                                     ).then_inc(sm.c_ld, 16)
                    scalar.wait_ge(sm.c_ld, 64 + 7 * 16)
                scalar.dma_start(
                    out=bass.AP(qt_dram.tensor, qt * 128,
                                [[TL, 128], [128 * TL, 16], [1, 128]]),
                    in_=BB3[:, 0:2048].rearrange("p (h j) -> p h j", h=16)
                ).then_inc(sm.q_st, 16)
            for i in range(16):
                scalar.wait_ge(WG[i % 2], 16 * (i // 2 + 1))
                scalar.dma_start(out=wo_bf[i * 128:(i + 1) * 128, :],
                                 in_=P32B[:, (i % 2) * 2048:(i % 2) * 2048 + 2048]
                                 ).then_inc(WS[i % 2], 16)
            for tt in range(32):
                scalar.wait_ge(sm.k_pe2, tt + 1)
                if tt >= 2:
                    scalar.wait_ge(sm.k_pe, 2 * (tt - 2) + 2)
                scalar.activation(BB2[:, (tt % 2) * 2048:(tt % 2) * 2048 + 2048],
                                  PSBb[:, 0:2048], AF.Identity)
                scalar.activation(DUMW[:, 128:256], IDB[:, 0:128], AF.Identity).then_inc(sm.k_ac, 1)
                scalar.wait_ge(sm.k_pe, 2 * tt + 1)
                if tt >= 1:
                    scalar.wait_ge(sm.k_dv, 4 * (tt - 1) + 1)
                scalar.activation(FB3[:, 0:512], PSA[:, 0:512], AF.Identity)
                scalar.activation(DUMW[:, 128:256], IDB[:, 0:128], AF.Identity).then_inc(sm.k_ac, 1)
                scalar.activation(VS[:, tt * 512:(tt + 1) * 512], PSA[:, 512:1024],
                                  AF.Identity)
                scalar.activation(DUMW[:, 128:256], IDB[:, 0:128], AF.Identity).then_inc(sm.k_ac, 1)
                scalar.wait_ge(sm.k_dv, 4 * tt + 1)
                scalar.activation(COLC[:, 0:4], COLA[:, 0:4], AF.Sqrt,
                                  bias=EPSC[:, 0:1], scale=1.0 / D)
                scalar.activation(DUM[:, 0:4], COLC[:, 0:4], AF.Sqrt
                                  ).then_inc(sm.k_ac, 1)
            for u in A3S:
                i, sl, h = u["i"], u["sl"], u["h"]
                if h == 0 and sl > 0:
                    emit_a4_scalar(scalar, sl - 1)
                scalar.wait_ge(sm.a_pr, i + 1)
                scalar.activation(NEGM[0:1, :], PSB[0:1, 1024:1536],
                                  AF.Identity)
                scalar.activation(DUMW[:, 128:256], IDB[:, 0:128],
                                  AF.Identity).then_inc(sm.a_dn, 1)
                n2 = u["n2_0"]
                for kt in range(KT128[sl]):
                    scalar.wait_ge(sm.a_p2, n2 + 1)
                    if n2 >= 2:
                        scalar.wait_ge(sm.a_av, n2 - 1)
                    if kt >= KT128[sl] - 8:
                        jj = kt - (KT128[sl] - 8)
                        cnt = 8 * i + jj + 1
                        scalar.wait_ge(sm.a_d2, cnt)
                        scalar.activation(
                            PTB[:, (n2 % 2) * 512:(n2 % 2) * 512 + 512],
                            PSA[:, 1024 + (n2 % 2) * 512:
                                1024 + (n2 % 2) * 512 + 512],
                            AF.Exp).then_inc(sm.a_ae, 1)
                    else:
                        scalar.activation(
                            PTB[:, (n2 % 2) * 512:(n2 % 2) * 512 + 512],
                            PSA[:, 1024 + (n2 % 2) * 512:
                                1024 + (n2 % 2) * 512 + 512],
                            AF.Exp).then_inc(sm.a_ae, 1)
                    n2 += 1
                if DEBUG and i == 0:
                    scalar.wait_ge(sm.a_df, i + 1)
                    scalar.wait_ge(sm.a_dm, 1)
                    scalar.wait_ge(sm.a_dn, 1)
                    scalar.wait_ge(sm.a_df, 1)
                    scalar.dma_start(out=ofn_dbg[:, :], in_=OFN[:, 0:640]
                                     ).then_inc(sm.c_ld, 16)
                    scalar.dma_start(out=mcb_dbg[:, 0:4], in_=MCOL[:, :]
                                     ).then_inc(sm.c_ld, 16)
                    scalar.dma_start(out=negm_dbg[:, :], in_=NEGM[0:1, :]
                                     ).then_inc(sm.c_ld, 16)
                    for dq in range(4):
                        scalar.dma_start(
                            out=os_dbg[:, dq * 128:(dq + 1) * 128],
                            in_=OS[:, dq * 2048:dq * 2048 + 128]
                        ).then_inc(sm.c_ld, 16)
                    scalar.wait_ge(sm.c_ld, 64 + 14 * 16)
            emit_a4_scalar(scalar, 3)
            for s in range(16):
                for sub in range(8):
                    for tq in range(4):
                        k = (s * 8 + sub) * 4 + tq
                        scalar.wait_ge(sm.f_pe, k + 1)
                        if k >= 1:
                            scalar.wait_ge(sm.f_dv, k)
                        gb = PSA[:, (tq % 2) * 512:(tq % 2) * 512 + 512]
                        ub = PSA[:, 1024 + (tq % 2) * 512:
                                 1024 + (tq % 2) * 512 + 512]
                        scalar.activation(FB3[:, 0:512], gb,
                                          AF.Identity).then_inc(sm.f_ac, 1)
                        scalar.activation(FB3[:, 512:1024], ub,
                                          AF.Identity).then_inc(sm.f_ac, 1)
                        scalar.wait_ge(sm.f_tg, k + 1)
                        scalar.activation(FB2[:, 512:1024], FB2[:, 0:512],
                                          AF.Tanh).then_inc(sm.f_th, 1)
                for tg in range(16):
                    it = s * 16 + tg
                    scalar.wait_ge(sm.f_pe2, it + 1)
                    accb = PSA if tg % 2 == 0 else PSB
                    stg = FB4 if tg % 2 == 0 else FB1
                    if it >= 2:
                        scalar.wait_ge(FG[it % 2], 16 * (it // 2))
                    scalar.activation(stg[:, :], accb[:, :],
                                      AF.Identity).then_inc(sm.f_ac2, 1)
            for tg in range(16):
                scalar.wait_ge(sm.l_dv, 2 * tg + 1)
                scalar.activation(COLC[:, 0:1], COLA[:, 5:6], AF.Sqrt,
                                  bias=EPSC[:, 1:2], scale=1.0)
                scalar.activation(DUM[:, 0:1], COLC[:, 0:1], AF.Sqrt
                                  ).then_inc(sm.l_ac, 1)
                scalar.wait_ge(sm.l_dv, 2 * tg + 2)
                scalar.dma_start(out=y2out[tg * 128:(tg + 1) * 128, :],
                                 in_=FB3[:, :]).then_inc(sm.l_st, 16)
            scalar.wait_ge(sm.l_st, 16 * 16)
            if DEBUG:
                scalar.dma_start(out=it_dbg[:, :], in_=INTERT[:, :]
                                 ).then_inc(sm.c_ld, 16)
                scalar.dma_start(out=y1t_dbg[:, :], in_=P64A[:, :]
                                 ).then_inc(sm.c_ld, 16)
                scalar.dma_start(out=y1td_dbg[:, :], in_=y1t_dram[:, :]
                                 ).then_inc(sm.c_ld, 16)
                scalar.wait_ge(sm.c_ld, 64 + 19 * 16)

        @block.vector
        def _(vector):
            vector.memset(ONES1[:, :], 1.0)
            vector.memset(ONESC[:, :], 1.0)
            vector.memset(EPSC[:, 0:1], EPS_RMS)
            vector.memset(EPSC[:, 1:2], EPS_LN)
            vector.wait_ge(sm.c_ld, 64)
            vector.tensor_scalar(GQS[:, :], GQC[:, :], QK_SCALE, None, ALU.mult)
            for j in range(NDC):
                vector.wait_ge(DL[j % 2], 32 * (j // 2 + 1))
                vector.wait_ge(sm.d_ac, j + 1)
                if j >= 1:
                    vector.wait_ge(sm.d_sc, 32 * j)
                cb = CSD[:, (j % 2) * 2048:(j % 2) * 2048 + 2048]
                nx = DNSX[:, (j % 2) * 2048:(j % 2) * 2048 + 2048]
                vector.tensor_scalar(DI1[:, :], cb, 15, None, ALU.bitwise_and)
                vector.tensor_scalar(DI2[:, :], cb, 4, None,
                                     ALU.logical_shift_right)
                vector.tensor_scalar(DI3[:, :], cb, 3, 1,
                                     ALU.logical_shift_right, ALU.bitwise_and)
                vector.tensor_scalar(DI4[:, :], cb, 7, 1,
                                     ALU.logical_shift_right, ALU.bitwise_and)
                for half, (xx, ss, dst) in enumerate(
                        ((DI1, DI3, DLO), (DI2, DI4, DHI))):
                    sl2 = slice(half * 2048, half * 2048 + 2048)
                    vector.tensor_scalar(DT1[:, :], xx[:, :], 64.0, 16128.0,
                                         ALU.mult, ALU.add)
                    vector.scalar_tensor_tensor(DBITS[:, sl2], ss[:, :],
                                                -33280.0, DT1[:, :],
                                                ALU.mult, ALU.add)
                    uu = DBITS[:, sl2].bitcast(BF16)
                    vector.tensor_scalar(DC1[:, :], uu, 1.0, -1.0,
                                         ALU.min, ALU.max)
                    vector.tensor_scalar(DC2[:, :], uu, 2.0, 1.0,
                                         ALU.mult, ALU.min)
                    vector.tensor_tensor(DT2[:, :], uu, DC1[:, :], ALU.add)
                    vector.scalar_tensor_tensor(DT3[:, :], DC2[:, :], -1.0,
                                                DT2[:, :], ALU.max,
                                                ALU.subtract)
                    if half == 0:
                        vector.tensor_tensor(dst[:, :], DT3[:, :], nx, ALU.mult)
                    else:
                        vector.tensor_tensor(dst[:, :], DT3[:, :], nx, ALU.mult
                                             ).then_inc(sm.d_dv, 1)
            for qt in range(16):
                vector.wait_ge(sm.q_ac, 3 * qt + 2)
                for hh in range(16):
                    op = vector.scalar_tensor_tensor(
                        FB2[:, hh * 128:(hh + 1) * 128],
                        FB3[:, hh * 128:(hh + 1) * 128], 0.0,
                        FB3[:, hh * 128:(hh + 1) * 128],
                        ALU.add, ALU.mult, accum_out=COLA[:, hh:hh + 1])
                vector.tensor_copy(out=DUM[:, 16:32], in_=COLA[:, 0:16]
                                   ).then_inc(sm.q_dv, 1)
                vector.wait_ge(sm.q_ac, 3 * qt + 3)
                vector.reciprocal(COLB[:, 0:16], COLC[:, 0:16]).then_inc(sm.q_dv, 1)
                vector.tensor_copy(out=DUM[:, 32:48], in_=COLB[:, 0:16])
                vector.tensor_copy(out=DUM[:, 48:64], in_=DUM[:, 32:48])
                xt = BB2[:, (qt % 2) * 2048:(qt % 2) * 2048 + 2048]
                for hh in range(16):
                    vector.tensor_scalar(xt[:, hh * 128:(hh + 1) * 128],
                                         FB3[:, hh * 128:(hh + 1) * 128],
                                         COLB[:, hh:hh + 1], None, ALU.mult)
                vector.tensor_copy(out=DUMW[:, 0:128], in_=IDB[:, 0:128]).then_inc(sm.q_dv, 1)
                vector.wait_ge(sm.q_pe, 2 * qt + 2)
                if qt >= 1:
                    vector.wait_ge(sm.q_st, 16 * qt)
                vector.tensor_scalar(BB3[:, 0:2048],
                                     PSBb[:, 2048:4096],
                                     GQS[:, 0:1], None, ALU.mult)
                vector.tensor_copy(out=DUMW[:, 0:128], in_=IDB[:, 0:128]).then_inc(sm.q_dv, 1)
            vector.wait_ge(sm.d_sc, 16 * 48)
            for tt in range(32):
                vector.wait_ge(sm.k_ac, 4 * tt + 2)
                for sgi in range(4):
                    op = vector.scalar_tensor_tensor(
                        FB2[:, sgi * 128:(sgi + 1) * 128],
                        FB3[:, sgi * 128:(sgi + 1) * 128], 0.0,
                        FB3[:, sgi * 128:(sgi + 1) * 128],
                        ALU.add, ALU.mult, accum_out=COLA[:, sgi:sgi + 1])
                vector.tensor_copy(out=DUM[:, 16:20], in_=COLA[:, 0:4]
                                   ).then_inc(sm.k_dv, 1)
                vector.wait_ge(sm.k_ac, 4 * tt + 4)
                vector.reciprocal(COLB[:, 0:4], COLC[:, 0:4]).then_inc(sm.k_dv, 1)
                vector.tensor_copy(out=DUM[:, 32:36], in_=COLB[:, 0:4])
                vector.tensor_copy(out=DUM[:, 48:52], in_=DUM[:, 32:36])
                xt = BB2[:, (tt % 2) * 2048:(tt % 2) * 2048 + 2048]
                for sgi in range(4):
                    vector.tensor_scalar(xt[:, sgi * 128:(sgi + 1) * 128],
                                         FB3[:, sgi * 128:(sgi + 1) * 128],
                                         COLB[:, sgi:sgi + 1], None,
                                         ALU.mult)
                vector.tensor_copy(out=DUMW[:, 0:128], in_=IDB[:, 0:128]).then_inc(sm.k_dv, 1)
                vector.wait_ge(sm.k_pe, 2 * tt + 2)
                for sgi in range(4):
                    vector.tensor_scalar(
                        KTS[:, sgi * 4096 + tt * 128:sgi * 4096 + (tt + 1) * 128],
                        PSBb[:, 2048 + sgi * 128:2048 + (sgi + 1) * 128],
                        GKC[:, 0:1], None, ALU.mult)
                vector.tensor_copy(out=DUMW[:, 0:128], in_=IDB[:, 0:128]).then_inc(sm.k_dv, 1)
            for u in A3S:
                i, sl, h = u["i"], u["sl"], u["h"]
                if h == 0 and sl > 0:
                    emit_a4_vector(vector, sl - 1)
                n1 = u["n1_0"]
                for qt in range(4):
                    for k5 in range(KT512[sl]):
                        vector.wait_ge(sm.a_p1, n1 + 1)
                        bank = PSA[:, (n1 % 2) * 512:(n1 % 2) * 512 + 512]
                        if k5 >= KT512[sl] - 2:
                            jj = k5 - (KT512[sl] - 2)
                            vector.tensor_tensor(
                                bank, bank,
                                M1S[:, (jj * 4 + qt) * 512:
                                    (jj * 4 + qt + 1) * 512], ALU.add)
                        vector.reduce_max(COLB[:, k5:k5 + 1], bank,
                                          axis=AX.X).then_inc(sm.a_d1, 1)
                        n1 += 1
                    vector.tensor_copy(out=DUM[:, 32:48], in_=IDB[:, 0:16])
                    vector.tensor_copy(out=DUM[:, 48:64], in_=IDB[:, 16:32])
                    vector.reduce_max(MCOL[:, qt:qt + 1],
                                      COLB[:, 0:KT512[sl]], axis=AX.X)
                vector.tensor_copy(out=DUM[:, 32:48], in_=IDB[:, 0:16])
                vector.tensor_copy(out=DUM[:, 48:64], in_=IDB[:, 16:32])
                vector.tensor_scalar(MCB[:, :], MCOL[:, :], -1.0, None,
                                     ALU.mult)
                vector.tensor_copy(out=DUM[:, 16:20], in_=MCB[:, 0:4])
                vector.tensor_copy(out=DUMW[:, 0:128], in_=IDB[:, 0:128]
                                   ).then_inc(sm.a_dm, 1)

                n2 = u["n2_0"]
                for kt in range(KT128[sl]):
                    if kt >= KT128[sl] - 8:
                        jj = kt - (KT128[sl] - 8)
                        cnt = 8 * i + jj + 1
                        vector.wait_ge(sm.a_p2, n2 + 1)
                        bank = PSA[:, 1024 + (n2 % 2) * 512:
                                   1024 + (n2 % 2) * 512 + 512]
                        vector.tensor_tensor(bank, bank,
                                             M2S[:, jj * 512:(jj + 1) * 512],
                                             ALU.add)
                        vector.tensor_copy(out=DUMW[:, 0:128], in_=IDB[:, 0:128]).then_inc(sm.a_d2, 1)
                    n2 += 1
                vector.wait_ge(sm.a_av, u["n2_1"])
                for qt in range(4):
                    vector.reciprocal(RC[:, qt:qt + 1],
                                      PSB[:, qt * 512 + 128:qt * 512 + 129])
                vector.tensor_copy(out=DUM[:, 32:36], in_=RC[:, 0:4])
                vector.tensor_copy(out=DUM[:, 48:52], in_=DUM[:, 32:36])
                for qt in range(4):
                    vector.tensor_scalar(
                        OS[:, qt * 2048 + h * 128:qt * 2048 + (h + 1) * 128],
                        PSB[:, qt * 512:qt * 512 + 128],
                        RC[:, qt:qt + 1], None, ALU.mult)
                vector.tensor_copy(out=DUMW[:, 0:128], in_=IDB[:, 0:128]).then_inc(sm.a_df, 1)
            emit_a4_vector(vector, 3)
            for s in range(16):
                for sub in range(8):
                    for tq in range(4):
                        k = (s * 8 + sub) * 4 + tq
                        vector.wait_ge(sm.f_ac, 2 * k + 2)
                        if s > 0 and sub == 0 and tq == 0:
                            vector.wait_ge(sm.f_pe2, 16 * s)
                        gf = FB3[:, 0:512]
                        upf = FB3[:, 512:1024]
                        g2 = FB3[:, 1024:1536]
                        u1 = FB3[:, 1536:2048]
                        targ = FB2[:, 0:512]
                        th = FB2[:, 512:1024]
                        vv = FB2[:, 1024:1536]
                        vector.tensor_tensor(g2, gf, gf, ALU.mult)
                        vector.tensor_scalar(u1, g2, G_C1, G_C2,
                                             ALU.mult, ALU.add)
                        vector.tensor_tensor(targ, u1, gf, ALU.mult
                                             ).then_inc(sm.f_tg, 1)
                        vector.wait_ge(sm.f_th, k + 1)
                        vector.scalar_tensor_tensor(vv, th, 1.0, gf,
                                                    ALU.add, ALU.mult)
                        op = vector.scalar_tensor_tensor(
                            INTERT[:, sub * 2048 + tq * 512:
                                   sub * 2048 + (tq + 1) * 512],
                            vv, 0.5, upf, ALU.mult, ALU.mult)
                        if sub == 7 and tq == 3:
                            op = vector.tensor_copy(out=DUMW[:, 0:128], in_=IDB[:, 0:128])
                        op.then_inc(sm.f_dv, 1)
            for tg in range(16):
                vector.wait_ge(sm.l_ld, 32 * (tg + 1))
                if tg >= 1:
                    vector.wait_ge(sm.l_st, 16 * tg)
                vector.scalar_tensor_tensor(FB4[:, :], FB1[:, :], 0.0,
                                            FB2[:, :], ALU.add, ALU.add,
                                            accum_out=COLA[:, 0:1])
                vector.scalar_tensor_tensor(FB3[:, :], FB4[:, :], 0.0,
                                            FB4[:, :], ALU.add, ALU.mult,
                                            accum_out=COLA[:, 1:2])
                vector.tensor_scalar(COLA[:, 2:3], COLA[:, 0:1], 1.0 / HB,
                                     None, ALU.mult)
                vector.tensor_copy(out=DUM[:, 32:48], in_=IDB[:, 0:16])
                vector.tensor_copy(out=DUM[:, 48:64], in_=IDB[:, 16:32])
                vector.tensor_tensor(COLA[:, 3:4], COLA[:, 2:3], COLA[:, 2:3],
                                     ALU.mult)
                vector.tensor_scalar(COLA[:, 4:5], COLA[:, 1:2], 1.0 / HB,
                                     None, ALU.mult)
                vector.tensor_copy(out=DUM[:, 32:48], in_=IDB[:, 0:16])
                vector.tensor_copy(out=DUM[:, 48:64], in_=IDB[:, 16:32])
                vector.tensor_tensor(COLA[:, 5:6], COLA[:, 4:5], COLA[:, 3:4],
                                     ALU.subtract)
                vector.tensor_copy(out=DUM[:, 16:17], in_=COLA[:, 5:6]
                                   ).then_inc(sm.l_dv, 1)
                vector.wait_ge(sm.l_ac, tg + 1)
                vector.reciprocal(COLB[:, 0:1], COLC[:, 0:1])
                vector.tensor_copy(out=DUM[:, 32:33], in_=COLB[:, 0:1])
                vector.tensor_copy(out=DUM[:, 48:49], in_=DUM[:, 32:33])
                vector.tensor_scalar(FB3[:, :], FB4[:, :], COLA[:, 2:3],
                                     COLB[:, 0:1], ALU.subtract, ALU.mult)
                vector.tensor_copy(out=DUMW[:, 0:128], in_=IDB[:, 0:128]).then_inc(sm.l_dv, 1)

    stack.close()
    return nc


def _build_masks(sub):
    m1 = np.zeros((4, 128, 8 * 512), np.float32)
    m2 = np.zeros((4, 128, 8 * 512), np.float32)
    for sl in range(4):
        P = SLOT_CHUNKS[sub][sl]
        for j in range(2):
            k5 = KT512[sl] - 2 + j
            for qt in range(4):
                q = np.arange(128)[:, None] + qt * 128 + P * 512
                k = np.arange(512)[None, :] + k5 * 512
                m1[sl][:, (j * 4 + qt) * 512:(j * 4 + qt + 1) * 512] = \
                    np.where(k <= q, 0.0, NEGV)
        for j in range(8):
            kt = KT128[sl] - 8 + j
            k = np.arange(128)[:, None] + kt * 128
            q = np.arange(512)[None, :] + P * 512
            m2[sl][:, j * 512:(j + 1) * 512] = np.where(k <= q, 0.0, NEGV)
    m1 = m1.reshape(4 * 128, 8 * 512).astype(ml_dtypes.bfloat16)
    m2 = m2.reshape(4 * 128, 8 * 512).astype(ml_dtypes.bfloat16)
    return m1, m2


_NC_CACHE = [None]


def kernel(L0_attn_x, L0_attn_Wq, L0_attn_Wk, L0_attn_Wv, L0_attn_Wo,
           L0_attn_gq, L0_attn_gk, L0_ffn_gate_W_packed, L0_ffn_gate_scales,
           L0_ffn_up_W_packed, L0_ffn_up_scales, L0_ffn_down_W_packed,
           L0_ffn_down_scales):
    x = np.ascontiguousarray(np.asarray(L0_attn_x, dtype=np.float32))
    Wq = np.ascontiguousarray(np.asarray(L0_attn_Wq, dtype=np.float32))
    Wk = np.asarray(L0_attn_Wk, dtype=np.float32)
    Wv = np.asarray(L0_attn_Wv, dtype=np.float32)
    Wkv = np.ascontiguousarray(np.concatenate([Wk, Wv], axis=0))
    Wo = np.ascontiguousarray(np.asarray(L0_attn_Wo, dtype=np.float32))
    gq = np.ascontiguousarray(np.asarray(L0_attn_gq, dtype=np.float32
                                         ).reshape(D, 1))
    gk = np.ascontiguousarray(np.asarray(L0_attn_gk, dtype=np.float32
                                         ).reshape(D, 1))
    gW = np.ascontiguousarray(np.asarray(L0_ffn_gate_W_packed, np.int32).T)
    uW = np.ascontiguousarray(np.asarray(L0_ffn_up_W_packed, np.int32).T)
    dW = np.ascontiguousarray(np.asarray(L0_ffn_down_W_packed, np.int32).T)
    gS = np.ascontiguousarray(np.asarray(L0_ffn_gate_scales, np.float32).T)
    uS = np.ascontiguousarray(np.asarray(L0_ffn_up_scales, np.float32).T)
    dS = np.ascontiguousarray(np.asarray(L0_ffn_down_scales, np.float32).T)
    e8v = np.zeros((8, 128), np.float32)
    for jj in range(8):
        e8v[jj, jj * 16:(jj + 1) * 16] = -1.0
    identv = np.eye(128, dtype=np.float32).astype(ml_dtypes.bfloat16)

    if _NC_CACHE[0] is None:
        _NC_CACHE[0] = build_nc()
    nc = _NC_CACHE[0]
    in_maps = []
    for c in range(8):
        b, sub = c // 2, c % 2
        chunks = SLOT_CHUNKS[sub]
        xqv = np.ascontiguousarray(
            np.concatenate([x[b, ch * 512:(ch + 1) * 512] for ch in chunks], 0))
        m1, m2 = _build_masks(sub)
        in_maps.append({
            "xb": np.ascontiguousarray(x[b]),
            "xq": xqv,
            "wq": Wq, "wkv": Wkv, "wo": Wo, "gqv": gq, "gkv": gk,
            "gsh": np.ascontiguousarray(gW[c * 128:(c + 1) * 128]),
            "ush": np.ascontiguousarray(uW[c * 128:(c + 1) * 128]),
            "dsh": np.ascontiguousarray(dW[c * 1024:(c + 1) * 1024]),
            "gsc": np.ascontiguousarray(gS[c * 8:(c + 1) * 8]),
            "usc": np.ascontiguousarray(uS[c * 8:(c + 1) * 8]),
            "dsc": np.ascontiguousarray(dS[c * 64:(c + 1) * 64]),
            "e8": e8v, "identin": identv, "m1in": m1, "m2in": m2,
        })
    res = run_bass_kernel_spmd(nc, in_maps, core_ids=list(range(8)))
    out = np.empty((4, S, HB), np.float32)
    for c in range(8):
        b, sub = c // 2, c % 2
        y2 = res.results[c]["y2out"]
        for sl, ch in enumerate(SLOT_CHUNKS[sub]):
            out[b, ch * 512:(ch + 1) * 512] = y2[sl * 512:(sl + 1) * 512]
    return out
